# version 23
# baseline (speedup 1.0000x reference)
import numpy as np
from contextlib import ExitStack

import concourse.bass as bass
import concourse.mybir as mybir
from concourse.bass_utils import run_bass_kernel_spmd

F32 = mybir.dt.float32
BF16 = mybir.dt.bfloat16
I8 = mybir.dt.int8
U8 = mybir.dt.uint8
AF = mybir.ActivationFunctionType
ALU = mybir.AluOpType

NCORES = 8
T = 1024
C = 2048
H = 2048
FF = 8192
CT = C // 128
HT = H // 128
ITT = FF // 128
EPS = 1e-5
KB = 1024


class Sem:
    def __init__(self, nc, ctx, name):
        self.h = ctx.enter_context(nc.semaphore(name))
        self.n = 0

    def tick(self, k):
        self.n += k
        return self.n

    def chain(self, eng):
        if self.n:
            eng.wait_ge(self.h, self.n)


def wait_all(eng, guards):
    for g in guards:
        if g is not None:
            s, v = g
            if v > 0:
                eng.wait_ge(s.h, v)


class Bank:
    def __init__(self, t):
        self.t = t
        self.guard = None

    def acquire(self, eng):
        wait_all(eng, [self.guard])

    def release(self, sem, val):
        self.guard = (sem, val)


def build_program():
    nc = bass.Bass("TRN2", target_bir_lowering=False, debug=False)

    x_seg = nc.dram_tensor("x_seg", [T + 1, C], F32, kind="ExternalInput").ap()
    Wmats = {n: nc.dram_tensor(n2, [2048, 2048], F32, kind="ExternalInput").ap()
             for n, n2 in (("r", "Wr"), ("k", "Wk"), ("v", "Wv"), ("o", "Wo"))}
    wg8 = nc.dram_tensor("wg8", [C, FF], I8, kind="ExternalInput").ap()
    wu8 = nc.dram_tensor("wu8", [C, FF], I8, kind="ExternalInput").ap()
    wd8 = nc.dram_tensor("wd8", [FF, C], I8, kind="ExternalInput").ap()
    mu_r_pt = nc.dram_tensor("mu_r_pt", [128, CT], F32, kind="ExternalInput").ap()
    mu_k_pt = nc.dram_tensor("mu_k_pt", [128, CT], F32, kind="ExternalInput").ap()
    mu_v_pt = nc.dram_tensor("mu_v_pt", [128, CT], F32, kind="ExternalInput").ap()
    w_pt = nc.dram_tensor("w_pt", [128, HT], F32, kind="ExternalInput").ap()
    u_pt = nc.dram_tensor("u_pt", [128, HT], F32, kind="ExternalInput").ap()
    sgate_pt = nc.dram_tensor("sgate_pt", [128, ITT], F32, kind="ExternalInput").ap()
    sup_pt = nc.dram_tensor("sup_pt", [128, ITT], F32, kind="ExternalInput").ap()
    sdown_b = nc.dram_tensor("sdown_b", [128, C], F32, kind="ExternalInput").ap()
    ident_h = nc.dram_tensor("ident_h", [128, 128], F32, kind="ExternalInput").ap()
    iota_rev_h = nc.dram_tensor("iota_rev_h", [128, T], F32, kind="ExternalInput").ap()
    ones_h = nc.dram_tensor("ones_h", [128, 512], F32, kind="ExternalInput").ap()
    segc_h = nc.dram_tensor("segc_h", [128, NCORES], F32, kind="ExternalInput").ap()
    out = nc.dram_tensor("out", [T, C], F32, kind="ExternalOutput").ap()

    wbf = {n: nc.dram_tensor(f"wbf_{n}", [2048, 2048], BF16, kind="Internal").ap()
           for n in ("r", "k", "v", "o")}
    spill_q = nc.dram_tensor("spill_q", [H, T], F32, kind="Internal").ap()
    spill_p = nc.dram_tensor("spill_p", [H, T], F32, kind="Internal").ap()
    spill_s = nc.dram_tensor("spill_s", [H, T], F32, kind="Internal").ap()
    fuse_hbm = nc.dram_tensor("fuse_hbm", [FF, T], BF16, kind="Internal").ap()
    x1_hbm = nc.dram_tensor("x1_hbm", [T, C], F32, kind="Internal").ap()
    cc_in = nc.dram_tensor("cc_in", [128, 2 * HT], F32, kind="Internal")
    cc_out = nc.dram_tensor("cc_out", [NCORES * 128, 2 * HT], F32, kind="Internal")

    ctx = ExitStack()
    with ctx:
        ARENA_BYTES = 196224
        backing = ctx.enter_context(nc.sbuf_tensor("arena", [128, ARENA_BYTES], U8))
        base = nc.lookup_mloc(backing).addr

        def alloc(name, shape, dtype, off):
            assert off % 64 == 0, name
            nbytes = int(np.prod(shape[1:])) * mybir.dt.size(dtype)
            assert off + nbytes <= ARENA_BYTES, (name, off, nbytes)
            h = nc.virtual_sbuf_tensor(name, shape, dtype)
            nc._tensor(h.name, shape, dtype, type="SB")
            mloc = nc.lookup_mloc(h)
            mloc.allocated = True
            mloc.addr = base + off
            return h

        ident = alloc("ident", [128, 128], BF16, 0)
        ones = alloc("ones", [128, 512], F32, 320)
        iota_rev = alloc("iota_rev", [128, T], F32, 2368)
        vb = [6464]
        def vec(name):
            h = alloc(name, [128, 16], F32, vb[0])
            vb[0] += 64
            return h
        mu = {"r": vec("mu_r"), "k": vec("mu_k"), "v": vec("mu_v")}
        wraw = vec("wraw"); uraw = vec("uraw")
        w_eff = vec("w_eff"); d1 = vec("d1"); eu = vec("eu")
        AinV = vec("AinV"); BinV = vec("BinV")
        Acar = vec("Acar"); Bcar = vec("Bcar")
        tmp16a = vec("tmp16a"); tmp16b = vec("tmp16b"); eps_t = vec("eps_t"); e_t = vec("e_t")
        summ = alloc("summ", [128, 2 * HT], F32, vb[0]); vb[0] += 128
        segc = alloc("segc", [128, 8], F32, vb[0]); vb[0] += 64
        stat = alloc("stat", [128, 16], F32, vb[0]); vb[0] += 64
        sgate_v = alloc("sgate_v", [128, ITT], F32, 8000)
        sup_v = alloc("sup_v", [128, ITT], F32, 8256)
        sdown_v = alloc("sdown_v", [128, C], F32, 8512)
        gath = alloc("gath", [128, NCORES * 2 * HT], F32, 16704)
        coef_all = alloc("coef_all", [128, 128], F32, 17792)
        tmp3 = alloc("tmp3", [128, 128], F32, 18368)
        e128 = alloc("e128", [128, 128], F32, 18944)

        XB = 20 * KB
        mixT = alloc("mixT", [128, CT * T], BF16, XB)
        trash2 = alloc("trash2", [128, C], BF16, XB)
        h2T = alloc("h2T", [128, CT * T], BF16, XB)
        fuse_tt = [alloc("fuse_tt0", [128, ITT * 128], BF16, XB),
                   alloc("fuse_tt1", [128, ITT * 128], BF16, XB + ITT * 128 * 2)]

        BB = 53 * KB
        hT = alloc("hT", [128, CT * (T + 1)], BF16, BB)
        WT = alloc("WT", [128, CT * 2048], BF16, BB + 33 * KB)
        SCR = BB + 99 * KB
        x_t = [alloc(f"x_t{i}", [128, C], F32, SCR + i * 8 * KB) for i in range(2)] + [
            alloc("x_t2", [128, C], F32, SCR + 20 * KB),
            alloc("x_t3", [128, C], F32, SCR + 28 * KB)]
        h_bf = alloc("h_bf", [128, C], BF16, SCR + 16 * KB)
        delta = alloc("delta", [128, T], F32, SCR)
        q_t = [alloc(f"q_t{i}", [128, T], F32, SCR + 4 * KB + i * 4 * KB) for i in range(2)]
        qrr = [alloc(f"qrr{i}", [128, T], F32, SCR + 12 * KB + i * 4 * KB) for i in range(2)]
        p_t = [alloc(f"p_t{i}", [128, T], F32, SCR + 20 * KB + i * 4 * KB) for i in range(2)]
        dpow = [alloc(f"dpow{i}", [128, T], F32, SCR + 28 * KB + i * 4 * KB) for i in range(2)]
        trash_bf = alloc("trash_bf", [128, T], BF16, SCR + 36 * KB)
        s_buf = alloc("s_buf", [128, HT * T], BF16, BB)
        WoT = alloc("WoT", [128, HT * 2048], BF16, BB + 33 * KB)
        rrq = [alloc(f"rrq{i}", [128, 512], F32, SCR + 12 * KB + i * 2 * KB) for i in range(2)]
        rrp = [alloc(f"rrp{i}", [128, 512], F32, SCR + 16 * KB + i * 2 * KB) for i in range(2)]
        rrs = [alloc(f"rrs{i}", [128, 512], F32, SCR + 20 * KB + i * 2 * KB) for i in range(2)]
        Abuf = alloc("Abuf", [128, 513], F32, SCR + 24 * KB)
        Bbuf = alloc("Bbuf", [128, 513], F32, SCR + 26 * KB + 64)
        nbuf = alloc("nbuf", [128, 512], F32, SCR + 28 * KB + 128)
        dbuf = alloc("dbuf", [128, 512], F32, SCR + 30 * KB + 128)
        rec = alloc("rec", [128, 512], F32, SCR)
        d_bc = alloc("d_bc", [128, 512], F32, SCR + 32 * KB + 192)
        x1_t = [alloc("x1_t0", [128, C], F32, SCR + 2 * KB)]
        h2_bf = alloc("h2_bf", [128, C], BF16, SCR + 34 * KB + 256)
        sig_t = alloc("sig_t", [128, T], F32, SCR + 34 * KB + 256)
        x1rr = [alloc(f"x1rr{i}", [128, 512], F32, SCR + 30 * KB + i * 2 * KB) for i in range(2)]
        o_st = [alloc(f"o_st{i}", [128, 512], F32, SCR + 34 * KB + i * 2 * KB) for i in range(2)]
        wgb = [alloc(f"wgb{i}", [128, CT * 512], BF16, BB + i * 16 * KB) for i in range(2)]
        wub = [alloc(f"wub{i}", [128, CT * 512], BF16, BB + 32 * KB + i * 16 * KB) for i in range(2)]
        wgb_alt = alloc("wgba", [128, CT * 512], BF16, SCR)
        wub_alt = alloc("wuba", [128, CT * 512], BF16, SCR + 16 * KB)
        fch = [alloc(f"fch{i}", [128, 8 * 512], BF16, BB + 64 * KB + i * 8 * KB) for i in range(2)]
        g_scr = [alloc(f"g_scr{i}", [128, 512], F32, BB + 80 * KB + i * 2 * KB) for i in range(2)]
        wdb = [alloc(f"wdb{i}", [128, ITT * 512], BF16, BB + i * 64 * KB) for i in range(2)]

        pb = [Bank(ctx.enter_context(nc.psum_tensor(f"pb{i}", [128, 512], F32)))
              for i in range(7)]
        ptr = Bank(ctx.enter_context(nc.psum_tensor("ptr", [128, 1024], BF16)))

        sx = Sem(nc, ctx, "sx")
        swt = Sem(nc, ctx, "swt")
        srr = Sem(nc, ctx, "srr")
        saq = Sem(nc, ctx, "saq")
        sgq = Sem(nc, ctx, "sgq")
        scc = Sem(nc, ctx, "scc")
        sPE = Sem(nc, ctx, "sPE")
        sDV = Sem(nc, ctx, "sDV")
        sAC = Sem(nc, ctx, "sAC")
        sout = Sem(nc, ctx, "sout")
        sspq = Sem(nc, ctx, "sspq")
        sspp = Sem(nc, ctx, "sspp")
        ssps = Sem(nc, ctx, "ssps")
        sx1w = Sem(nc, ctx, "sx1w")
        sfw = Sem(nc, ctx, "sfw")
        srrp = [Sem(nc, ctx, "srrp0"), Sem(nc, ctx, "srrp1")]
        srrs = [Sem(nc, ctx, "srrs0"), Sem(nc, ctx, "srrs1")]
        sPL = Sem(nc, ctx, "sPL")
        sftp = [Sem(nc, ctx, "sftp0"), Sem(nc, ctx, "sftp1")]

        block = ctx.enter_context(nc.Block())

        @block.gpsimd
        def _(g):
            g.dma_start(out=ident[:], in_=ident_h[:, :]).then_inc(sgq.h, 16)
            g.dma_start(out=ones[:], in_=ones_h[:, :]).then_inc(sgq.h, 16)
            g.dma_start(out=iota_rev[:], in_=iota_rev_h[:, :]).then_inc(sgq.h, 16)
            g.dma_start(out=mu["r"][:], in_=mu_r_pt[:, :]).then_inc(sgq.h, 16)
            g.dma_start(out=mu["k"][:], in_=mu_k_pt[:, :]).then_inc(sgq.h, 16)
            g.dma_start(out=mu["v"][:], in_=mu_v_pt[:, :]).then_inc(sgq.h, 16)
            g.dma_start(out=wraw[:], in_=w_pt[:, :]).then_inc(sgq.h, 16)
            g.dma_start(out=uraw[:], in_=u_pt[:, :]).then_inc(sgq.h, 16)
            g.dma_start(out=segc[:], in_=segc_h[:, :]).then_inc(sgq.h, 16)
            g.dma_start(out=sgate_v[:], in_=sgate_pt[:, :]).then_inc(sgq.h, 16)
            g.dma_start(out=sup_v[:], in_=sup_pt[:, :]).then_inc(sgq.h, 16)
            g.dma_start(out=sdown_v[:], in_=sdown_b[:, :]).then_inc(sgq.h, 16)
        SMALL_LD = sgq.tick(12 * 16)

        cast_done = {}
        cast_ct = {}
        def emit_cast(n, pe_gate=0, chunks=1):
            base = sgq.n
            @block.gpsimd
            def _(g):
                if pe_gate:
                    g.wait_ge(sPE.h, pe_gate)
                sgq.chain(g)
                if chunks == 1:
                    g.dma_start(out=wbf[n][:, :], in_=Wmats[n][:, :]).then_inc(sgq.h, 16)
                else:
                    for ct in range(chunks):
                        if ct:
                            g.wait_ge(sgq.h, base + 16 * ct)
                        g.dma_start(out=wbf[n][:, ct * 128:(ct + 1) * 128],
                                    in_=Wmats[n][:, ct * 128:(ct + 1) * 128]).then_inc(sgq.h, 16)
            if chunks > 1:
                cast_ct[n] = [base + 16 * (i + 1) for i in range(chunks)]
            cast_done[n] = sgq.tick(16 * chunks)
        emit_cast("k", chunks=16)

        @block.scalar
        def _(a):
            a.wait_ge(sgq.h, SMALL_LD)
            a.activation(out=tmp16a[:], in_=wraw[:], func=AF.Exp).then_inc(sAC.h, 1)
            a.activation(out=eu[:], in_=uraw[:], func=AF.Exp).then_inc(sAC.h, 1)
        AC_EXPW = sAC.tick(2)
        @block.vector
        def _(v):
            v.memset(e128[:], 2.718281828459045).then_inc(sDV.h, 1)
            v.memset(eps_t[:, 0:1], EPS).then_inc(sDV.h, 1)
            v.memset(e_t[:, 0:1], -1.0).then_inc(sDV.h, 1)
            v.wait_ge(sAC.h, AC_EXPW)
            v.tensor_scalar(out=w_eff[:], in0=tmp16a[:], scalar1=-1.0, scalar2=None,
                            op0=ALU.mult).then_inc(sDV.h, 1)
        sDV.tick(3)
        DV_WEFF = sDV.tick(1)
        @block.scalar
        def _(a):
            a.wait_ge(sDV.h, DV_WEFF)
            a.activation(out=d1[:], in_=w_eff[:], func=AF.Exp).then_inc(sAC.h, 1)
        AC_D1 = sAC.tick(1)
        @block.vector
        def _(v):
            v.wait_ge(sDV.h, DV_WEFF)
            for r in range(NCORES):
                v.tensor_scalar(out=coef_all[:, r * 16:(r + 1) * 16], in0=w_eff[:],
                                scalar1=segc[:, r:r + 1], scalar2=None,
                                op0=ALU.mult).then_inc(sDV.h, 1)
        DV_COEF_T = sDV.tick(NCORES)
        @block.gpsimd
        def _(g):
            g.wait_ge(sDV.h, DV_COEF_T)
            g.tensor_tensor(out=coef_all[:], in0=e128[:], in1=coef_all[:],
                            op=ALU.pow).then_inc(sPL.h, 1)
        PL_COEF = sPL.tick(1)

        hT_r = hT[:].rearrange("p (ct w) -> p ct w", ct=CT)
        ln1_dve_done = [0] * 9
        ln1_pe_last = [0] * 9
        x_loaded = [0] * 9
        def ln1_partA(a):
            rows = 128 if a < 8 else 1
            buf = x_t[a % 4]
            @block.sync
            def _(s, a=a, buf=buf, rows=rows):
                if a >= 4:
                    s.wait_ge(sDV.h, ln1_dve_done[a - 4])
                sx.chain(s)
                s.dma_start(out=buf[0:rows, :],
                            in_=x_seg[a * 128:a * 128 + rows, :]).then_inc(sx.h, 16)
            x_loaded[a] = sx.tick(16)
            so = (a % 2) * 8
            xa = buf[0:rows, :]
            @block.vector
            def _(v, a=a, xa=xa, rows=rows, so=so):
                r = slice(0, rows)
                v.wait_ge(sx.h, x_loaded[a])
                if a >= 2:
                    v.wait_ge(sDV.h, ln1_dve_done[a - 2])
                if a >= 1:
                    v.wait_ge(sDV.h, ln1_stt[a - 1])
                v.tensor_reduce(out=stat[r, so:so + 1], in_=xa,
                                axis=mybir.AxisListType.X, op=ALU.add).then_inc(sDV.h, 1)
                v.scalar_tensor_tensor(out=trash2[0:rows, :], in0=xa, scalar=1.0, in1=xa,
                                       op0=ALU.mult, op1=ALU.mult,
                                       accum_out=stat[r, so + 1:so + 2]).then_inc(sDV.h, 1)
                both = sDV.tick(2)
                ln1_stt[a] = both
                v.wait_ge(sDV.h, both)
                v.tensor_scalar(out=stat[r, so + 2:so + 3], in0=stat[r, so:so + 1],
                                scalar1=1.0 / C, scalar2=None, op0=ALU.mult).then_inc(sDV.h, 1)
                v.tensor_scalar(out=stat[r, so + 3:so + 4], in0=stat[r, so + 1:so + 2],
                                scalar1=1.0 / C, scalar2=None, op0=ALU.mult).then_inc(sDV.h, 1)
                m2 = sDV.tick(2)
                v.wait_ge(sDV.h, m2)
                v.tensor_tensor(out=stat[r, so + 4:so + 5], in0=stat[r, so + 2:so + 3],
                                in1=stat[r, so + 2:so + 3], op=ALU.mult).then_inc(sDV.h, 1)
                me = sDV.tick(1)
                v.wait_ge(sDV.h, me)
                v.tensor_tensor(out=stat[r, so + 5:so + 6], in0=stat[r, so + 3:so + 4],
                                in1=stat[r, so + 4:so + 5], op=ALU.subtract).then_inc(sDV.h, 1)
                sDV.tick(1)
            ln1_var[a] = sDV.n
            @block.scalar
            def _(a_, rows=rows, so=so, a=a):
                a_.wait_ge(sDV.h, ln1_var[a])
                a_.activation(out=stat[0:rows, so + 6:so + 7], in_=stat[0:rows, so + 5:so + 6],
                              func=AF.Sqrt, bias=eps_t[0:rows, 0:1]).then_inc(sAC.h, 1)
            ln1_sqrt[a] = sAC.tick(1)

        def ln1_partB(a):
            rows = 128 if a < 8 else 1
            buf = x_t[a % 4]
            so = (a % 2) * 8
            xa = buf[0:rows, :]
            @block.vector
            def _(v, a=a, xa=xa, rows=rows, so=so):
                r = slice(0, rows)
                v.wait_ge(sAC.h, ln1_sqrt[a])
                v.reciprocal(out=stat[r, so + 7:so + 8], in_=stat[r, so + 6:so + 7]).then_inc(sDV.h, 1)
                rst = sDV.tick(1)
                v.wait_ge(sDV.h, rst)
                if a >= 1:
                    v.wait_ge(sPE.h, ln1_pe_last[a - 1])
                v.tensor_scalar(out=h_bf[0:rows, :], in0=xa, scalar1=stat[r, so + 2:so + 3],
                                scalar2=stat[r, so + 7:so + 8], op0=ALU.subtract,
                                op1=ALU.mult).then_inc(sDV.h, 1)
                sDV.tick(1)
            dv_h = sDV.n
            if a < 8:
                for rnd in range(2):
                    @block.tensor
                    def _(t, a=a, rnd=rnd, dv_h=dv_h):
                        t.wait_ge(sDV.h, dv_h)
                        if a == 0 and rnd == 0:
                            t.wait_ge(sgq.h, SMALL_LD)
                        ptr.acquire(t)
                        for b in range(8):
                            ct = rnd * 8 + b
                            mm = t.transpose(ptr.t[:, b * 128:(b + 1) * 128],
                                             h_bf[:, ct * 128:(ct + 1) * 128], ident[:])
                            if b == 7:
                                mm.then_inc(sPE.h, 1)
                    pe_tr = sPE.tick(1)
                    @block.scalar
                    def _(a_, a=a, rnd=rnd, pe_tr=pe_tr):
                        a_.wait_ge(sPE.h, pe_tr)
                        a_.copy(out=hT_r[:, rnd * 8:(rnd + 1) * 8, a * 128 + 0:a * 128 + 128],
                                in_=ptr.t[:].rearrange("p (b w) -> p b w", b=8)).then_inc(sAC.h, 1)
                    ptr.release(sAC, sAC.tick(1))
                    ln1_pe_last[a] = sPE.n
            else:
                @block.tensor
                def _(t, dv_h=dv_h):
                    t.wait_ge(sDV.h, dv_h)
                    ptr.acquire(t)
                    for ct2 in range(16):
                        mm = t.transpose(ptr.t[:, 2 * ct2:2 * ct2 + 1],
                                         h_bf[0:1, ct2 * 128:(ct2 + 1) * 128], ident[0:1, 0:1])
                        if ct2 == 15:
                            mm.then_inc(sPE.h, 1)
                pe_tr = sPE.tick(1)
                @block.scalar
                def _(a_, pe_tr=pe_tr):
                    a_.wait_ge(sPE.h, pe_tr)
                    a_.copy(out=hT_r[:, :, 1024:1025],
                            in_=ptr.t[:].rearrange("p (b w) -> p b w", w=2)[:, 0:16, 0:1]
                            ).then_inc(sAC.h, 1)
                ptr.release(sAC, sAC.tick(1))
                ln1_pe_last[a] = sPE.n
            ln1_dve_done[a] = sDV.n

        ln1_var = [0] * 9
        ln1_sqrt = [0] * 9
        ln1_stt = [0] * 9
        for a in range(9):
            ln1_partA(a)
            if a >= 1:
                ln1_partB(a - 1)
        ln1_partB(8)
        LN1_DVE_END = sDV.n
        LN1_AC_END = sAC.n

        def emit_mixes(which, prev_pe_end):
            muv = mu[which]
            for ct in range(CT):
                @block.vector
                def _(v, ct=ct):
                    if ct == 0:
                        v.wait_ge(sDV.h, LN1_DVE_END)
                        v.wait_ge(sAC.h, LN1_AC_END)
                        if prev_pe_end:
                            v.wait_ge(sPE.h, prev_pe_end)
                    b0 = ct * (T + 1)
                    v.wait_ge(sDV.h, sDV.n)
                    v.tensor_tensor(out=delta[:], in0=hT[:, b0 + 1:b0 + T + 1],
                                    in1=hT[:, b0:b0 + T], op=ALU.subtract).then_inc(sDV.h, 1)
                    dd = sDV.tick(1)
                    v.wait_ge(sDV.h, dd)
                    v.scalar_tensor_tensor(out=mixT[:, ct * T:(ct + 1) * T], in0=delta[:],
                                           scalar=muv[:, ct:ct + 1], in1=hT[:, b0:b0 + T],
                                           op0=ALU.mult, op1=ALU.add).then_inc(sDV.h, 1)
                    sDV.tick(1)
            return sDV.n

        def emit_wt_loads(name, ready_after_pe):
            gates = cast_ct.get(name)
            @block.sync
            def _(s):
                if gates is None:
                    s.wait_ge(sgq.h, cast_done[name])
                if ready_after_pe:
                    s.wait_ge(sPE.h, ready_after_pe)
                swt.chain(s)
                for ct in range(CT):
                    if gates is not None:
                        s.wait_ge(sgq.h, gates[ct])
                    s.dma_start(out=WT[:, ct * 2048:(ct + 1) * 2048],
                                in_=wbf[name][0:2048, ct * 128:(ct + 1) * 128],
                                transpose=True).then_inc(swt.h, 16)
            return swt.tick(16 * 16)

        mm_banks = [pb[0], pb[1], pb[2], pb[3]]

        def emit_proj_group(ht, mix_ready, wt_gates, first):
            res = []
            @block.tensor
            def _(t, ht=ht):
                if first:
                    t.wait_ge(sDV.h, mix_ready)
                if ht in wt_gates:
                    t.wait_ge(swt.h, wt_gates[ht])
                for tc in range(2):
                    bk = mm_banks[(ht * 2 + tc) % 4]
                    bk.acquire(t)
                    for ct in range(CT):
                        mm = t.matmul(bk.t[:],
                                      WT[:, ct * 2048 + ht * 128: ct * 2048 + (ht + 1) * 128],
                                      mixT[:, ct * T + tc * 512: ct * T + tc * 512 + 512],
                                      start=(ct == 0), stop=(ct == CT - 1))
                        if ct == CT - 1:
                            mm.then_inc(sPE.h, 1)
                    res.append((bk, sPE.tick(1)))
            return res

        mixk_done = emit_mixes("k", 0)
        wtk_done = emit_wt_loads("k", 0)
        KG_FIRST = [0]
        q_spill_done = [0] * HT
        qbuf_free = [[], []]
        dpow_free = [[], []]
        for ht in range(HT):
            grp = emit_proj_group(ht, mixk_done, {0: wtk_done}, ht == 0)
            if ht == 0:
                KG_FIRST[0] = sPE.n
                emit_cast("v", pe_gate=KG_FIRST[0])
                emit_cast("r", pe_gate=KG_FIRST[0])
            qb = q_t[ht % 2]
            dp = dpow[ht % 2]
            @block.scalar
            def _(a_, ht=ht, qb=qb, grp=grp):
                wait_all(a_, qbuf_free[ht % 2])
                a_.wait_ge(sPE.h, grp[0][1])
                a_.activation(out=qb[:, 0:512], in_=grp[0][0].t[:], func=AF.Exp).then_inc(sAC.h, 1)
                grp[0][0].release(sAC, sAC.tick(1))
                a_.wait_ge(sPE.h, grp[1][1])
                a_.activation(out=qb[:, 512:1024], in_=grp[1][0].t[:], func=AF.Exp).then_inc(sAC.h, 1)
                grp[1][0].release(sAC, sAC.tick(1))
            ac_q = sAC.n
            @block.sync
            def _(s, ht=ht, qb=qb, ac_q=ac_q):
                s.wait_ge(sAC.h, ac_q)
                sspq.chain(s)
                s.dma_start(out=spill_q[ht * 128:(ht + 1) * 128, :], in_=qb[:]).then_inc(sspq.h, 16)
            q_spill_done[ht] = sspq.tick(16)
            @block.scalar
            def _(a_, ht=ht, dp=dp):
                wait_all(a_, dpow_free[ht % 2])
                if ht < 2:
                    a_.wait_ge(sAC.h, AC_D1)
                    a_.wait_ge(sDV.h, LN1_DVE_END)
                a_.activation(out=dp[:], in_=iota_rev[:], func=AF.Exp,
                              scale=w_eff[:, ht:ht + 1]).then_inc(sAC.h, 1)
            ac_dp = sAC.tick(1)
            @block.vector
            def _(v, ht=ht, qb=qb, dp=dp, ac_q=ac_q, ac_dp=ac_dp):
                v.wait_ge(sAC.h, max(ac_q, ac_dp))
                v.wait_ge(sDV.h, sDV.n)
                v.scalar_tensor_tensor(out=trash_bf[:], in0=qb[:], scalar=1.0, in1=dp[:],
                                       op0=ALU.mult, op1=ALU.mult,
                                       accum_out=summ[:, HT + ht:HT + ht + 1]).then_inc(sDV.h, 1)
            dv_stt = sDV.tick(1)
            qbuf_free[ht % 2] = [(sDV, dv_stt), (sspq, q_spill_done[ht])]
            dpow_free[ht % 2] = [(sDV, dv_stt)]
        K_DVE_END = sDV.n
        K_PE_END = sPE.n
        K_SPILL_END = sspq.n

        mixv_done = emit_mixes("v", K_PE_END)
        wtv_done = emit_wt_loads("v", K_PE_END)
        VG_FIRST = [0]
        p_spill_done = [0] * HT
        pbuf_free = [[], []]
        qrr_free = [[(sspq, K_SPILL_END)], [(sspq, K_SPILL_END)]]
        qrr_loaded = [0] * HT
        for ht in range(HT):
            grp = emit_proj_group(ht, mixv_done, {0: wtv_done}, ht == 0)
            if ht == 0:
                VG_FIRST[0] = sPE.n
                emit_cast("o", pe_gate=VG_FIRST[0])
            @block.sync
            def _(s, ht=ht):
                wait_all(s, qrr_free[ht % 2])
                if ht == 0:
                    s.wait_ge(sspq.h, K_SPILL_END)
                srr.chain(s)
                s.dma_start(out=qrr[ht % 2][:], in_=spill_q[ht * 128:(ht + 1) * 128, :]
                            ).then_inc(srr.h, 16)
            qrr_loaded[ht] = srr.tick(16)
            pbx = p_t[ht % 2]
            dp = dpow[ht % 2]
            @block.vector
            def _(v, ht=ht, pbx=pbx, grp=grp):
                wait_all(v, pbuf_free[ht % 2])
                v.wait_ge(srr.h, qrr_loaded[ht])
                v.wait_ge(sPE.h, grp[0][1])
                v.tensor_tensor(out=pbx[:, 0:512], in0=qrr[ht % 2][:, 0:512], in1=grp[0][0].t[:],
                                op=ALU.mult).then_inc(sDV.h, 1)
                grp[0][0].release(sDV, sDV.tick(1))
                v.wait_ge(sPE.h, grp[1][1])
                v.tensor_tensor(out=pbx[:, 512:1024], in0=qrr[ht % 2][:, 512:1024],
                                in1=grp[1][0].t[:], op=ALU.mult).then_inc(sDV.h, 1)
                grp[1][0].release(sDV, sDV.tick(1))
            qrr_free[ht % 2] = [(sDV, sDV.n)]
            dv_p = sDV.n
            @block.scalar
            def _(a_, ht=ht, dp=dp, dv_p=dv_p, pbx=pbx):
                wait_all(a_, dpow_free[ht % 2])
                a_.activation(out=dp[:], in_=iota_rev[:], func=AF.Exp,
                              scale=w_eff[:, ht:ht + 1]).then_inc(sAC.h, 1)
                a_.wait_ge(sDV.h, dv_p)
                sspp.chain(a_)
                a_.dma_start(out=spill_p[ht * 128:(ht + 1) * 128, :], in_=pbx[:]).then_inc(sspp.h, 16)
            ac_dp = sAC.tick(1)
            p_spill_done[ht] = sspp.tick(16)
            @block.vector
            def _(v, ht=ht, pbx=pbx, dp=dp, ac_dp=ac_dp):
                v.wait_ge(sAC.h, ac_dp)
                v.wait_ge(sDV.h, sDV.n)
                v.scalar_tensor_tensor(out=trash_bf[:], in0=pbx[:], scalar=1.0, in1=dp[:],
                                       op0=ALU.mult, op1=ALU.mult,
                                       accum_out=summ[:, ht:ht + 1]).then_inc(sDV.h, 1)
            dv_stt = sDV.tick(1)
            pbuf_free[ht % 2] = [(sDV, dv_stt), (sspp, p_spill_done[ht])]
            dpow_free[ht % 2] = [(sDV, dv_stt)]
        V_DVE_END = sDV.n
        V_PE_END = sPE.n

        @block.gpsimd
        def _(g):
            g.wait_ge(sDV.h, V_DVE_END)
            sgq.chain(g)
            g.dma_start(out=cc_in[:, :], in_=summ[:]).then_inc(sgq.h, 16)
            ccin_ld = sgq.tick(16)
            g.wait_ge(sgq.h, ccin_ld)
            g.collective_compute(
                "AllGather", ALU.bypass,
                replica_groups=[list(range(NCORES))],
                ins=[cc_in.ap().opt()],
                outs=[cc_out.ap().opt()],
            ).then_inc(scc.h, 1)
            g.wait_ge(scc.h, scc.tick(1))
            g.dma_start(out=gath[:].rearrange("p (r f) -> p r f", r=NCORES),
                        in_=cc_out.ap().rearrange("(r p) f -> p r f", p=128)).then_inc(sgq.h, 16)
        GATH_LD = sgq.tick(16)

        mixr_done = emit_mixes("r", V_PE_END)
        wtr_done = emit_wt_loads("r", V_PE_END)
        RG_FIRST = [0]
        @block.vector
        def _(v):
            v.wait_ge(sgq.h, GATH_LD)
            v.wait_ge(sPL.h, PL_COEF)
            v.wait_ge(sDV.h, sDV.n)
            cview = coef_all[:].rearrange("p (r i) -> p i r", r=NCORES)
            gA = gath[:].rearrange("p (r f) -> p f r", r=NCORES)[:, 0:HT, :]
            gB = gath[:].rearrange("p (r f) -> p f r", r=NCORES)[:, HT:2 * HT, :]
            t3 = tmp3[:].rearrange("p (i r) -> p i r", r=NCORES)
            v.tensor_tensor(out=t3, in0=cview, in1=gA, op=ALU.mult).then_inc(sDV.h, 1)
            pA = sDV.tick(1)
            v.wait_ge(sDV.h, pA)
            v.tensor_reduce(out=AinV[:], in_=t3, axis=mybir.AxisListType.X,
                            op=ALU.add).then_inc(sDV.h, 1)
            pA2 = sDV.tick(1)
            v.wait_ge(sDV.h, pA2)
            v.tensor_tensor(out=t3, in0=cview, in1=gB, op=ALU.mult).then_inc(sDV.h, 1)
            pB = sDV.tick(1)
            v.wait_ge(sDV.h, pB)
            v.tensor_reduce(out=BinV[:], in_=t3, axis=mybir.AxisListType.X,
                            op=ALU.add).then_inc(sDV.h, 1)
            sDV.tick(1)
        CARRY_DONE = sDV.n
        s_spill_done = [0] * HT
        sgbuf_free = [[], []]
        for ht in range(HT):
            grp = emit_proj_group(ht, mixr_done, {0: wtr_done}, ht == 0)
            if ht == 0:
                RG_FIRST[0] = sPE.n
            sbx = sig_t
            @block.scalar
            def _(a_, ht=ht, sbx=sbx, grp=grp):
                wait_all(a_, sgbuf_free[0])
                if ht == 0:
                    wait_all(a_, pbuf_free[0])
                    wait_all(a_, pbuf_free[1])
                a_.wait_ge(sPE.h, grp[0][1])
                a_.activation(out=sbx[:, 0:512], in_=grp[0][0].t[:], func=AF.Sigmoid).then_inc(sAC.h, 1)
                grp[0][0].release(sAC, sAC.tick(1))
                a_.wait_ge(sPE.h, grp[1][1])
                a_.activation(out=sbx[:, 512:1024], in_=grp[1][0].t[:], func=AF.Sigmoid).then_inc(sAC.h, 1)
                grp[1][0].release(sAC, sAC.tick(1))
            ac_s = sAC.n
            @block.scalar
            def _(a_, ht=ht, sbx=sbx, ac_s=ac_s):
                a_.wait_ge(sAC.h, ac_s)
                ssps.chain(a_)
                a_.dma_start(out=spill_s[ht * 128:(ht + 1) * 128, :], in_=sbx[:]).then_inc(ssps.h, 16)
            s_spill_done[ht] = ssps.tick(16)
            sgbuf_free[0] = [(ssps, s_spill_done[ht])]
        R_PE_END = sPE.n
        SPILLQ_ALL = sspq.n
        SPILLP_ALL = sspp.n
        SPILLS_ALL = ssps.n


        rrqp_free = [[(sspq, SPILLQ_ALL), (sspp, SPILLP_ALL)],
                     [(sspq, SPILLQ_ALL), (sspp, SPILLP_ALL)]]
        rrs_free = [[], []]
        scan_first = [True]

        def scan_block(k, ch, ht):
            pp = ht % 2
            sem = srrp[pp]
            @block.sync
            def _(s, ch=ch, ht=ht, pp=pp, sem=sem):
                wait_all(s, rrqp_free[pp])
                cs_ = slice(ch * 512, ch * 512 + 512)
                rw = slice(ht * 128, (ht + 1) * 128)
                s.dma_start(out=rrq[pp][:], in_=spill_q[rw, cs_]).then_inc(sem.h, 16)
                s.dma_start(out=rrp[pp][:], in_=spill_p[rw, cs_]).then_inc(sem.h, 16)
                wait_all(s, rrs_free[pp])
                s.wait_ge(ssps.h, s_spill_done[ht])
                s.dma_start(out=rrs[pp][:], in_=spill_s[rw, cs_]).then_inc(srrs[pp].h, 16)
            rr_ld = sem.tick(32)
            rs_ld = srrs[pp].tick(16)
            @block.vector
            def _(v, k=k, ch=ch, ht=ht, pp=pp, rr_ld=rr_ld, rs_ld=rs_ld, sem=sem):
                if scan_first[0]:
                    v.wait_ge(sDV.h, CARRY_DONE)
                    v.wait_ge(sAC.h, AC_D1)
                    v.wait_ge(sgq.h, SMALL_LD)
                    scan_first[0] = False
                v.wait_ge(sem.h, rr_ld)
                v.wait_ge(sDV.h, sDV.n)
                v.tensor_scalar(out=d_bc[:], in0=ones[:], scalar1=d1[:, ht:ht + 1],
                                scalar2=None, op0=ALU.mult).then_inc(sDV.h, 1)
                srcA = AinV if ch == 0 else Acar
                srcB = BinV if ch == 0 else Bcar
                v.tensor_scalar(out=Abuf[:, 0:1], in0=srcA[:, ht:ht + 1], scalar1=0.0,
                                scalar2=None, op0=ALU.add).then_inc(sDV.h, 1)
                v.tensor_scalar(out=Bbuf[:, 0:1], in0=srcB[:, ht:ht + 1], scalar1=0.0,
                                scalar2=None, op0=ALU.add).then_inc(sDV.h, 1)
                pre = sDV.tick(3)
                v.wait_ge(sDV.h, pre)
                v.tensor_tensor_scan(out=Abuf[:, 1:513], data0=d_bc[:], data1=rrp[pp][:],
                                     initial=Abuf[:, 0:1], op0=ALU.mult,
                                     op1=ALU.add).then_inc(sDV.h, 1)
                v.tensor_tensor_scan(out=Bbuf[:, 1:513], data0=d_bc[:], data1=rrq[pp][:],
                                     initial=Bbuf[:, 0:1], op0=ALU.mult,
                                     op1=ALU.add).then_inc(sDV.h, 1)
                sc = sDV.tick(2)
                v.wait_ge(sDV.h, sc)
                if ch == 0:
                    v.tensor_scalar(out=Acar[:, ht:ht + 1], in0=Abuf[:, 512:513], scalar1=0.0,
                                    scalar2=None, op0=ALU.add).then_inc(sDV.h, 1)
                    v.tensor_scalar(out=Bcar[:, ht:ht + 1], in0=Bbuf[:, 512:513], scalar1=0.0,
                                    scalar2=None, op0=ALU.add).then_inc(sDV.h, 1)
                    sDV.tick(2)
                v.scalar_tensor_tensor(out=nbuf[:], in0=rrp[pp][:], scalar=eu[:, ht:ht + 1],
                                       in1=Abuf[:, 0:512], op0=ALU.mult,
                                       op1=ALU.add).then_inc(sDV.h, 1)
                v.scalar_tensor_tensor(out=dbuf[:], in0=rrq[pp][:], scalar=eu[:, ht:ht + 1],
                                       in1=Bbuf[:, 0:512], op0=ALU.mult,
                                       op1=ALU.add).then_inc(sDV.h, 1)
                nd = sDV.tick(2)
                rrqp_free[pp] = [(sDV, nd)]
                v.wait_ge(sDV.h, nd)
                v.reciprocal(out=rec[:], in_=dbuf[:]).then_inc(sDV.h, 1)
                rc = sDV.tick(1)
                v.wait_ge(sDV.h, rc)
                v.tensor_tensor(out=dbuf[:], in0=nbuf[:], in1=rec[:],
                                op=ALU.mult).then_inc(sDV.h, 1)
                yy = sDV.tick(1)
                v.wait_ge(sDV.h, yy)
                v.wait_ge(srrs[pp].h, rs_ld)
                v.tensor_tensor(out=s_buf[:, ht * T + ch * 512: ht * T + ch * 512 + 512],
                                in0=dbuf[:], in1=rrs[pp][:], op=ALU.mult).then_inc(sDV.h, 1)
                sDV.tick(1)
            rrs_free[pp] = [(sDV, sDV.n)]

        ch1_pend = [None]
        s_done_ch1 = [0] * HT

        def emit_pending(v):
            ht_p, pp_p, ac_p, rs_p = ch1_pend[0]
            v.wait_ge(sAC.h, ac_p)
            v.tensor_tensor(out=dbuf[:], in0=nbuf[:], in1=rec[:],
                            op=ALU.mult).then_inc(sDV.h, 1)
            yy = sDV.tick(1)
            v.wait_ge(sDV.h, yy)
            v.wait_ge(srrs[pp_p].h, rs_p)
            v.tensor_tensor(out=s_buf[:, ht_p * T + 512: ht_p * T + 1024],
                            in0=dbuf[:], in1=rrs[pp_p][:], op=ALU.mult).then_inc(sDV.h, 1)
            sDV.tick(1)
            s_done_ch1[ht_p] = sDV.n
            rrs_free[pp_p] = [(sDV, sDV.n)]
            ch1_pend[0] = None

        def scan_block_ch1(ht):
            pp = ht % 2
            sem = srrp[pp]
            @block.sync
            def _(s, ht=ht, pp=pp, sem=sem):
                wait_all(s, rrqp_free[pp])
                cs_ = slice(512, 1024)
                rw = slice(ht * 128, (ht + 1) * 128)
                s.dma_start(out=rrq[pp][:], in_=spill_q[rw, cs_]).then_inc(sem.h, 16)
                s.dma_start(out=rrp[pp][:], in_=spill_p[rw, cs_]).then_inc(sem.h, 16)
                wait_all(s, rrs_free[pp])
                s.dma_start(out=rrs[pp][:], in_=spill_s[rw, cs_]).then_inc(srrs[pp].h, 16)
            rr_ld = sem.tick(32)
            rs_ld = srrs[pp].tick(16)
            @block.vector
            def _(v, ht=ht, pp=pp, rr_ld=rr_ld, sem=sem):
                v.wait_ge(sem.h, rr_ld)
                v.wait_ge(sDV.h, sDV.n)
                v.tensor_scalar(out=d_bc[:], in0=ones[:], scalar1=d1[:, ht:ht + 1],
                                scalar2=None, op0=ALU.mult).then_inc(sDV.h, 1)
                v.tensor_scalar(out=Abuf[:, 0:1], in0=Acar[:, ht:ht + 1], scalar1=0.0,
                                scalar2=None, op0=ALU.add).then_inc(sDV.h, 1)
                v.tensor_scalar(out=Bbuf[:, 0:1], in0=Bcar[:, ht:ht + 1], scalar1=0.0,
                                scalar2=None, op0=ALU.add).then_inc(sDV.h, 1)
                pre = sDV.tick(3)
                v.wait_ge(sDV.h, pre)
                v.tensor_tensor_scan(out=Abuf[:, 1:513], data0=d_bc[:], data1=rrp[pp][:],
                                     initial=Abuf[:, 0:1], op0=ALU.mult,
                                     op1=ALU.add).then_inc(sDV.h, 1)
                v.tensor_tensor_scan(out=Bbuf[:, 1:513], data0=d_bc[:], data1=rrq[pp][:],
                                     initial=Bbuf[:, 0:1], op0=ALU.mult,
                                     op1=ALU.add).then_inc(sDV.h, 1)
                sc = sDV.tick(2)
                if ch1_pend[0] is not None:
                    emit_pending(v)
                v.wait_ge(sDV.h, sDV.n)
                v.scalar_tensor_tensor(out=nbuf[:], in0=rrp[pp][:], scalar=eu[:, ht:ht + 1],
                                       in1=Abuf[:, 0:512], op0=ALU.mult,
                                       op1=ALU.add).then_inc(sDV.h, 1)
                v.scalar_tensor_tensor(out=dbuf[:], in0=rrq[pp][:], scalar=eu[:, ht:ht + 1],
                                       in1=Bbuf[:, 0:512], op0=ALU.mult,
                                       op1=ALU.add).then_inc(sDV.h, 1)
                sDV.tick(2)
            nd = sDV.n
            lnt = rrq[pp]
            @block.scalar
            def _(a_, nd=nd, lnt=lnt):
                a_.wait_ge(sDV.h, nd)
                a_.activation(out=lnt[:], in_=dbuf[:], func=AF.Ln).then_inc(sAC.h, 1)
                a_.wait_ge(sAC.h, sAC.tick(1))
                a_.activation(out=rec[:], in_=lnt[:], func=AF.Exp,
                              scale=e_t[:, 0:1]).then_inc(sAC.h, 1)
            ac2 = sAC.tick(1)
            rrqp_free[pp] = [(sDV, nd), (sAC, ac2)]
            ch1_pend[0] = (ht, pp, ac2, rs_ld)

        h2T_r = h2T[:].rearrange("p (ct w) -> p ct w", ct=CT)
        wo_banks = [pb[0], pb[1], pb[2], pb[3]]
        ph4_pe_last = [0] * 8
        x1_free = [[(sDV, 0)]]
        wo_x1_done = [0] * 8
        h2t_done_tt = [0] * 8

        def wo_mm_part(tt, s_done):
            @block.tensor
            def _(t, tt=tt, s_done=s_done):
                for cs in range(4):
                    wo_banks[cs].acquire(t)
                for ht2 in range(HT):
                    t.wait_ge(sDV.h, s_done[ht2])
                    t.wait_ge(swt.h, WO_LD_HT[ht2])
                    for cs in range(4):
                        mm = t.matmul(wo_banks[cs].t[:],
                                      s_buf[:, ht2 * T + tt * 128: ht2 * T + (tt + 1) * 128],
                                      WoT[:, ht2 * 2048 + cs * 512: ht2 * 2048 + cs * 512 + 512],
                                      start=(ht2 == 0), stop=(ht2 == HT - 1))
                        if ht2 == HT - 1:
                            mm.then_inc(sPE.h, 1)
            return [(wo_banks[cs], sPE.tick(1)) for cs in range(4)]

        def wo_evac_part(tt, wo_g):
            x1b = x1_t[0]
            @block.sync
            def _(s, tt=tt, x1b=x1b):
                wait_all(s, x1_free[0])
                sx.chain(s)
                s.dma_start(out=x1b[:],
                            in_=x_seg[1 + tt * 128:1 + (tt + 1) * 128, :]).then_inc(sx.h, 16)
            x_ld = sx.tick(16)
            for cs in range(4):
                bk, pev = wo_g[cs]
                @block.vector
                def _(v, cs=cs, bk=bk, pev=pev, x1b=x1b, x_ld=x_ld):
                    if cs == 0:
                        v.wait_ge(sx.h, x_ld)
                        v.wait_ge(sDV.h, sDV.n)
                    v.wait_ge(sPE.h, pev)
                    v.tensor_tensor(out=x1b[:, cs * 512:cs * 512 + 512],
                                    in0=x1b[:, cs * 512:cs * 512 + 512],
                                    in1=bk.t[:], op=ALU.add).then_inc(sDV.h, 1)
                bk.release(sDV, sDV.tick(1))
            wo_x1_done[tt] = sDV.n

        ln2_dvh2 = [0] * 8

        def ln2_math(tt):
            x1b = x1_t[0]
            so = (tt % 2) * 8
            @block.scalar
            def _(a_, x1b=x1b, so=so, tt=tt):
                a_.wait_ge(sDV.h, wo_x1_done[tt])
                a_.wait_ge(sAC.h, sAC.n)
                if tt >= 1:
                    a_.wait_ge(sPE.h, ph4_pe_last[tt - 1])
                a_.activation(out=h2_bf[:], in_=x1b[:], func=AF.Square,
                              accum_out=stat[:, so + 1:so + 2]).then_inc(sAC.h, 1)
            ac_sq2 = sAC.tick(1)
            @block.vector
            def _(v, x1b=x1b, so=so, tt=tt, ac_sq2=ac_sq2):
                v.wait_ge(sDV.h, wo_x1_done[tt])
                v.tensor_reduce(out=stat[:, so:so + 1], in_=x1b[:], axis=mybir.AxisListType.X,
                                op=ALU.add).then_inc(sDV.h, 1)
                both = sDV.tick(1)
                v.wait_ge(sDV.h, both)
                v.wait_ge(sAC.h, ac_sq2)
                v.tensor_scalar(out=stat[:, so + 2:so + 3], in0=stat[:, so:so + 1],
                                scalar1=1.0 / C, scalar2=None, op0=ALU.mult).then_inc(sDV.h, 1)
                v.tensor_scalar(out=stat[:, so + 3:so + 4], in0=stat[:, so + 1:so + 2],
                                scalar1=1.0 / C, scalar2=None, op0=ALU.mult).then_inc(sDV.h, 1)
                m2 = sDV.tick(2)
                v.wait_ge(sDV.h, m2)
                v.tensor_tensor(out=stat[:, so + 4:so + 5], in0=stat[:, so + 2:so + 3],
                                in1=stat[:, so + 2:so + 3], op=ALU.mult).then_inc(sDV.h, 1)
                me = sDV.tick(1)
                v.wait_ge(sDV.h, me)
                v.tensor_tensor(out=stat[:, so + 5:so + 6], in0=stat[:, so + 3:so + 4],
                                in1=stat[:, so + 4:so + 5], op=ALU.subtract).then_inc(sDV.h, 1)
                sDV.tick(1)
            dv_var = sDV.n
            @block.scalar
            def _(a_, tt=tt, x1b=x1b, so=so, dv_var=dv_var):
                a_.wait_ge(sDV.h, dv_var)
                a_.activation(out=stat[:, so + 6:so + 7], in_=stat[:, so + 5:so + 6],
                              func=AF.Sqrt, bias=eps_t[:, 0:1]).then_inc(sAC.h, 1)
                sx1w.chain(a_)
                a_.dma_start(out=x1_hbm[tt * 128:(tt + 1) * 128, :], in_=x1b[:]).then_inc(sx1w.h, 16)
            ac_sqrt = sAC.tick(1)
            x1_sp = sx1w.tick(16)
            @block.vector
            def _(v, tt=tt, x1b=x1b, so=so, ac_sqrt=ac_sqrt):
                v.wait_ge(sAC.h, ac_sqrt)
                v.reciprocal(out=stat[:, so + 7:so + 8], in_=stat[:, so + 6:so + 7]).then_inc(sDV.h, 1)
                rst = sDV.tick(1)
                v.wait_ge(sDV.h, rst)
                if tt >= 1:
                    v.wait_ge(sPE.h, ph4_pe_last[tt - 1])
                v.tensor_scalar(out=h2_bf[:], in0=x1b[:], scalar1=stat[:, so + 2:so + 3],
                                scalar2=stat[:, so + 7:so + 8], op0=ALU.subtract,
                                op1=ALU.mult).then_inc(sDV.h, 1)
                sDV.tick(1)
            dv_h2 = sDV.n
            ln2_dvh2[tt] = dv_h2
            x1_free[0] = [(sDV, dv_h2), (sx1w, x1_sp)]

        def ln2_tr(tt):
            dv_h2 = ln2_dvh2[tt]
            for rnd in range(2):
                @block.tensor
                def _(t, rnd=rnd, dv_h2=dv_h2):
                    t.wait_ge(sDV.h, dv_h2)
                    ptr.acquire(t)
                    for b in range(8):
                        ct2 = rnd * 8 + b
                        mm = t.transpose(ptr.t[:, b * 128:(b + 1) * 128],
                                         h2_bf[:, ct2 * 128:(ct2 + 1) * 128], ident[:])
                        if b == 7:
                            mm.then_inc(sPE.h, 1)
                pe_tr = sPE.tick(1)
                @block.scalar
                def _(a_, tt=tt, rnd=rnd, pe_tr=pe_tr):
                    a_.wait_ge(sPE.h, pe_tr)
                    a_.copy(out=h2T_r[:, rnd * 8:(rnd + 1) * 8, tt * 128:(tt + 1) * 128],
                            in_=ptr.t[:].rearrange("p (b w) -> p b w", b=8)).then_inc(sAC.h, 1)
                ptr.release(sAC, sAC.tick(1))
                ph4_pe_last[tt] = sPE.n
            h2t_done_tt[tt] = sAC.n

        s_done_ch0 = [0] * HT
        for ht in range(HT):
            scan_block(ht, 0, ht)
            s_done_ch0[ht] = sDV.n
        WO_B = swt.n
        @block.scalar
        def _(a_):
            a_.wait_ge(sgq.h, cast_done["o"])
            a_.wait_ge(sPE.h, R_PE_END)
            swt.chain(a_)
            for ht2 in range(HT):
                if ht2:
                    a_.wait_ge(swt.h, WO_B + 16 * ht2)
                a_.dma_start(out=WoT[:, ht2 * 2048:(ht2 + 1) * 2048],
                             in_=wbf["o"][0:2048, ht2 * 128:(ht2 + 1) * 128],
                             transpose=True).then_inc(swt.h, 16)
        swt.tick(16 * 16)
        WO_LD_HT = [WO_B + 16 * (i + 1) for i in range(HT)]

        for tt in range(4):
            wo_g = wo_mm_part(tt, s_done_ch0)
            if tt >= 1:
                ln2_tr(tt - 1)
            wo_evac_part(tt, wo_g)
            ln2_math(tt)
            for j in range(4):
                scan_block_ch1(tt * 4 + j)
        @block.vector
        def _(v):
            emit_pending(v)
        for tt in range(4, 8):
            wo_g = wo_mm_part(tt, s_done_ch1)
            ln2_tr(tt - 1)
            wo_evac_part(tt, wo_g)
            ln2_math(tt)
        ln2_tr(7)

        H2T_DONE = sAC.n
        WO_PE_END = sPE.n
        X1_SPILL_END = sx1w.n
        PH4_DVE_END = sDV.n
        NCH = 16
        wgbuf_free = [[(sPE, WO_PE_END)], [(sPE, WO_PE_END)]]
        fch_free = [[(sPE, WO_PE_END)], [(sPE, WO_PE_END)]]
        gscr_free = [[(sPE, WO_PE_END)], [(sPE, WO_PE_END)]]
        ffn_banks = [pb[0], pb[1], pb[2], pb[3], pb[4], pb[5]]
        bank_i = [0]
        gscr_i = [0]
        fuse_wr = [0] * NCH
        wg_ld = [0] * NCH
        for ich in range(NCH):
            pp = ich % 2
            wgx = wgb[pp] if ich < NCH - 1 else wgb_alt
            wux = wub[pp] if ich < NCH - 1 else wub_alt
            @block.gpsimd
            def _(g, ich=ich, pp=pp, wgx=wgx, wux=wux):
                if ich < NCH - 1:
                    wait_all(g, wgbuf_free[pp])
                else:
                    g.wait_ge(sDV.h, PH4_DVE_END)
                    g.wait_ge(sx1w.h, X1_SPILL_END)
                if ich < 2:
                    g.wait_ge(sPE.h, WO_PE_END)
                sgq.chain(g)
                isl = slice(ich * 512, (ich + 1) * 512)
                g.dma_start(out=wgx[:].rearrange("p (ct i) -> p ct i", ct=CT),
                            in_=wg8[:, isl].rearrange("(ct p) i -> p ct i", p=128)
                            ).then_inc(sgq.h, 16)
                g.dma_start(out=wux[:].rearrange("p (ct i) -> p ct i", ct=CT),
                            in_=wu8[:, isl].rearrange("(ct p) i -> p ct i", p=128)
                            ).then_inc(sgq.h, 16)
            wg_ld[ich] = sgq.tick(32)
            fb = fch[pp]
            for tc in range(2):
                for im in range(4):
                    bg = ffn_banks[bank_i[0] % 6]; bank_i[0] += 1
                    bu = ffn_banks[bank_i[0] % 6]; bank_i[0] += 1
                    it_g = ich * 4 + im
                    pe_cnt = []
                    @block.tensor
                    def _(t, ich=ich, tc=tc, im=im, bg=bg, bu=bu, pp=pp, pe_cnt=pe_cnt):
                        if ich == 0 and tc == 0 and im == 0:
                            t.wait_ge(sAC.h, H2T_DONE)
                        if tc == 0 and im == 0:
                            t.wait_ge(sgq.h, wg_ld[ich])
                        wgx = wgb[pp] if ich < NCH - 1 else wgb_alt
                        wux = wub[pp] if ich < NCH - 1 else wub_alt
                        bg.acquire(t)
                        for ct in range(CT):
                            mm = t.matmul(bg.t[:],
                                          wgx[:, ct * 512 + im * 128: ct * 512 + (im + 1) * 128],
                                          h2T[:, ct * T + tc * 512: ct * T + tc * 512 + 512],
                                          start=(ct == 0), stop=(ct == CT - 1))
                            if ct == CT - 1:
                                mm.then_inc(sPE.h, 1)
                        pe_cnt.append(sPE.tick(1))
                        bu.acquire(t)
                        for ct in range(CT):
                            mm = t.matmul(bu.t[:],
                                          wux[:, ct * 512 + im * 128: ct * 512 + (im + 1) * 128],
                                          h2T[:, ct * T + tc * 512: ct * T + tc * 512 + 512],
                                          start=(ct == 0), stop=(ct == CT - 1))
                            if ct == CT - 1:
                                mm.then_inc(sPE.h, 1)
                        pe_cnt.append(sPE.tick(1))
                    gpe, upe = pe_cnt
                    gi = gscr_i[0] % 2; gscr_i[0] += 1
                    gsc = g_scr[gi]
                    @block.scalar
                    def _(a_, it_g=it_g, gpe=gpe, gsc=gsc, bg=bg, gi=gi):
                        wait_all(a_, gscr_free[gi])
                        a_.wait_ge(sPE.h, gpe)
                        a_.activation(out=gsc[:], in_=bg.t[:], func=AF.Silu,
                                      scale=sgate_v[:, it_g:it_g + 1]).then_inc(sAC.h, 1)
                    bg.release(sAC, sAC.tick(1))
                    ac_g = sAC.n
                    @block.vector
                    def _(v, tc=tc, im=im, it_g=it_g, upe=upe, ac_g=ac_g,
                          gsc=gsc, bu=bu, fb=fb, pp=pp):
                        if tc == 0 and im == 0:
                            wait_all(v, fch_free[pp])
                        v.wait_ge(sPE.h, upe)
                        v.wait_ge(sAC.h, ac_g)
                        slot = im * 2 + tc
                        v.scalar_tensor_tensor(out=fb[:, slot * 512:(slot + 1) * 512],
                                               in0=bu.t[:], scalar=sup_v[:, it_g:it_g + 1],
                                               in1=gsc[:], op0=ALU.mult,
                                               op1=ALU.mult).then_inc(sDV.h, 1)
                    bu.release(sDV, sDV.tick(1))
                    gscr_free[gi] = [(sDV, sDV.n)]
            dv_f = sDV.n
            @block.scalar
            def _(a_, ich=ich, pp=pp, dv_f=dv_f, fb=fb):
                a_.wait_ge(sDV.h, dv_f)
                dst = fuse_hbm[ich * 512:(ich + 1) * 512, :].rearrange(
                    "(im p) (tc t) -> p im tc t", p=128, t=512)
                sfw.chain(a_)
                a_.dma_start(out=dst, in_=fb[:].rearrange("p (im tc t) -> p im tc t",
                                                          im=4, tc=2)).then_inc(sfw.h, 16)
            fuse_wr[ich] = sfw.tick(16)
            fch_free[pp] = [(sfw, fuse_wr[ich])]
            wgbuf_free[pp] = [(sPE, sPE.n)]
            if ich == NCH - 3:
                GU13_PE = sPE.n
            if ich == NCH - 2:
                GU14_PE = sPE.n
        GU_PE_END = sPE.n
        FUSE_ALL = sfw.n
        FUSE_DVE_END = sDV.n

        dn_banks = [pb[0], pb[1], pb[2], pb[3]]
        wdb_free = [[], []]
        ftt_free = [[(sPE, GU_PE_END)], [(sPE, GU_PE_END)]]
        x1rr_free = [[(sDV, PH4_DVE_END)], [(sDV, PH4_DVE_END)]]
        ost_free = [[(sDV, PH4_DVE_END)], [(sDV, PH4_DVE_END)]]
        wd_ld = [0] * 4
        for cs in range(4):
            @block.gpsimd
            def _(g, cs=cs):
                pp = cs % 2
                wait_all(g, wdb_free[pp])
                if cs == 1:
                    g.wait_ge(sPE.h, GU_PE_END)
                    g.wait_ge(sfw.h, FUSE_ALL)
                    g.wait_ge(sDV.h, FUSE_DVE_END)
                sgq.chain(g)
                order = (1, 3, 0, 2) if cs == 0 else range(4)
                for part in order:
                    if cs == 0 and part == 1:
                        g.wait_ge(sPE.h, GU13_PE)
                    if cs == 0 and part == 0:
                        g.wait_ge(sPE.h, GU14_PE)
                    rsl = slice(part * 2048, (part + 1) * 2048)
                    g.dma_start(
                        out=wdb[pp][:].rearrange("p (it c) -> p it c", it=ITT)[:, part * 16:(part + 1) * 16, :],
                        in_=wd8[rsl, cs * 512:(cs + 1) * 512].rearrange("(it p) c -> p it c", p=128)
                    ).then_inc(sgq.h, 16)
            wd_ld[cs] = sgq.tick(64)
            for tt in range(8):
                k_i = cs * 8 + tt
                pp = k_i % 2
                fsem = sftp[pp]
                ftt_base = fsem.n
                @block.sync
                def _(s, cs=cs, tt=tt, pp=pp, k_i=k_i, fsem=fsem, ftt_base=ftt_base):
                    wait_all(s, ftt_free[pp])
                    if k_i == 0:
                        s.wait_ge(sPE.h, GU_PE_END)
                    for part in range(4):
                        if k_i == 0:
                            s.wait_ge(sfw.h, fuse_wr[4 * part + 3])
                            if part:
                                s.wait_ge(fsem.h, ftt_base + 16 * part)
                        rsl = slice(part * 2048, (part + 1) * 2048)
                        s.dma_start(
                            out=fuse_tt[pp][:].rearrange("p (it t) -> p it t", it=ITT)[:, part * 16:(part + 1) * 16, :],
                            in_=fuse_hbm[rsl, tt * 128:(tt + 1) * 128].rearrange("(it p) t -> p it t", p=128)
                        ).then_inc(fsem.h, 16)
                ftt_ld = fsem.tick(64)
                @block.sync
                def _(s, cs=cs, tt=tt, pp=pp, k_i=k_i):
                    wait_all(s, x1rr_free[pp])
                    if k_i == 0:
                        s.wait_ge(sx1w.h, X1_SPILL_END)
                        s.wait_ge(sDV.h, PH4_DVE_END)
                        s.wait_ge(sPE.h, GU_PE_END)
                    sx.chain(s)
                    s.dma_start(out=x1rr[pp][:],
                                in_=x1_hbm[tt * 128:(tt + 1) * 128, cs * 512:(cs + 1) * 512]
                                ).then_inc(sx.h, 16)
                x1_ld = sx.tick(16)
                bk = dn_banks[tt % 4]
                @block.tensor
                def _(t, cs=cs, pp=pp, bk=bk, k_i=k_i, ftt_base=ftt_base, fsem=fsem):
                    t.wait_ge(sgq.h, wd_ld[cs])
                    if k_i != 0:
                        t.wait_ge(fsem.h, ftt_base + 64)
                    bk.acquire(t)
                    for it in range(ITT):
                        if k_i == 0 and it % 16 == 0:
                            t.wait_ge(fsem.h, ftt_base + 16 * (it // 16 + 1))
                        mm = t.matmul(bk.t[:], fuse_tt[pp][:, it * 128:(it + 1) * 128],
                                      wdb[cs % 2][:, it * 512:(it + 1) * 512],
                                      start=(it == 0), stop=(it == ITT - 1))
                        if it == ITT - 1:
                            mm.then_inc(sPE.h, 1)
                pev = sPE.tick(1)
                ftt_free[pp] = [(sPE, pev)]
                ob = o_st[pp]
                @block.vector
                def _(v, cs=cs, pp=pp, bk=bk, pev=pev, ob=ob, x1_ld=x1_ld, k_i=k_i):
                    wait_all(v, ost_free[pp])
                    if k_i == 0:
                        v.wait_ge(sx1w.h, X1_SPILL_END)
                    v.wait_ge(sx.h, x1_ld)
                    v.wait_ge(sPE.h, pev)
                    v.tensor_tensor(out=ob[:], in0=bk.t[:],
                                    in1=sdown_v[:, cs * 512:(cs + 1) * 512],
                                    op=ALU.mult).then_inc(sDV.h, 1)
                    t1 = sDV.tick(1)
                    v.wait_ge(sDV.h, t1)
                    v.tensor_tensor(out=ob[:], in0=ob[:], in1=x1rr[pp][:], op=ALU.add).then_inc(sDV.h, 1)
                bk.release(sDV, sDV.tick(1))
                x1rr_free[pp] = [(sDV, sDV.n)]
                dv_o = sDV.n
                if tt == 7:
                    wdb_free[cs % 2] = [(sPE, pev), (sDV, dv_o)]
                @block.scalar
                def _(a_, cs=cs, tt=tt, dv_o=dv_o, ob=ob):
                    a_.wait_ge(sDV.h, dv_o)
                    sout.chain(a_)
                    a_.dma_start(out=out[tt * 128:(tt + 1) * 128, cs * 512:(cs + 1) * 512],
                                 in_=ob[:]).then_inc(sout.h, 16)
                ost_free[pp] = [(sout, sout.tick(16))]

        @block.scalar
        def _(a_):
            a_.wait_ge(sout.h, sout.n)

    return nc


def make_in_maps(L0_attn_x, L0_attn_mu_r, L0_attn_mu_k, L0_attn_mu_v, L0_attn_wkv_w,
                 L0_attn_wkv_u, L0_attn_Wr, L0_attn_Wk, L0_attn_Wv, L0_attn_Wo,
                 L0_ffn_w_gate, L0_ffn_w_up, L0_ffn_w_down, L0_ffn_s_gate, L0_ffn_s_up,
                 L0_ffn_s_down):
    x = np.ascontiguousarray(np.asarray(L0_attn_x, dtype=np.float32).reshape(NCORES * T, C))

    def pt(v, nt):
        return np.ascontiguousarray(np.asarray(v, dtype=np.float32).reshape(nt, 128).T)

    common = dict(
        Wr=np.ascontiguousarray(L0_attn_Wr, dtype=np.float32),
        Wk=np.ascontiguousarray(L0_attn_Wk, dtype=np.float32),
        Wv=np.ascontiguousarray(L0_attn_Wv, dtype=np.float32),
        Wo=np.ascontiguousarray(L0_attn_Wo, dtype=np.float32),
        wg8=np.ascontiguousarray(L0_ffn_w_gate, dtype=np.int8),
        wu8=np.ascontiguousarray(L0_ffn_w_up, dtype=np.int8),
        wd8=np.ascontiguousarray(L0_ffn_w_down, dtype=np.int8),
        mu_r_pt=pt(L0_attn_mu_r, CT),
        mu_k_pt=pt(L0_attn_mu_k, CT),
        mu_v_pt=pt(L0_attn_mu_v, CT),
        w_pt=pt(L0_attn_wkv_w, HT),
        u_pt=pt(L0_attn_wkv_u, HT),
        sgate_pt=pt(L0_ffn_s_gate, ITT),
        sup_pt=pt(L0_ffn_s_up, ITT),
        sdown_b=np.ascontiguousarray(
            np.broadcast_to(np.asarray(L0_ffn_s_down, dtype=np.float32), (128, C))),
        ident_h=np.eye(128, dtype=np.float32),
        iota_rev_h=np.ascontiguousarray(
            np.broadcast_to(np.arange(T - 1, -1, -1, dtype=np.float32), (128, T))),
        ones_h=np.ones((128, 512), dtype=np.float32),
    )

    in_maps = []
    for c in range(NCORES):
        if c == 0:
            seg = np.concatenate([np.zeros((1, C), np.float32), x[0:T]], axis=0)
        else:
            seg = x[c * T - 1:(c + 1) * T]
        segc = np.empty(NCORES, np.float32)
        for r in range(NCORES):
            segc[r] = 1024.0 * (c - 1 - r) if r < c else 4.0e7
        m = dict(common)
        m["x_seg"] = np.ascontiguousarray(seg)
        m["segc_h"] = np.ascontiguousarray(np.broadcast_to(segc, (128, NCORES)))
        in_maps.append(m)
    return in_maps


def kernel(**inputs):
    in_maps = make_in_maps(**inputs)
    nc = build_program()
    res = run_bass_kernel_spmd(nc, in_maps, core_ids=list(range(NCORES)))
    full = np.concatenate([np.asarray(res.results[c]["out"]) for c in range(NCORES)], axis=0)
    return full.reshape(1, NCORES * T, C).astype(np.float32)


# version 24 (final)
# speedup vs baseline: 1.0210x; 1.0210x over previous
import numpy as np
from contextlib import ExitStack

import concourse.bass as bass
import concourse.mybir as mybir
from concourse.bass_utils import run_bass_kernel_spmd

F32 = mybir.dt.float32
BF16 = mybir.dt.bfloat16
I8 = mybir.dt.int8
U8 = mybir.dt.uint8
AF = mybir.ActivationFunctionType
ALU = mybir.AluOpType

NCORES = 8
T = 1024
C = 2048
H = 2048
FF = 8192
CT = C // 128
HT = H // 128
ITT = FF // 128
EPS = 1e-5
KB = 1024


class Sem:
    def __init__(self, nc, ctx, name):
        self.h = ctx.enter_context(nc.semaphore(name))
        self.n = 0

    def tick(self, k):
        self.n += k
        return self.n

    def chain(self, eng):
        if self.n:
            eng.wait_ge(self.h, self.n)


def wait_all(eng, guards):
    for g in guards:
        if g is not None:
            s, v = g
            if v > 0:
                eng.wait_ge(s.h, v)


class Bank:
    def __init__(self, t):
        self.t = t
        self.guard = None

    def acquire(self, eng):
        wait_all(eng, [self.guard])

    def release(self, sem, val):
        self.guard = (sem, val)


def build_program():
    nc = bass.Bass("TRN2", target_bir_lowering=False, debug=False)

    x_seg = nc.dram_tensor("x_seg", [T + 1, C], F32, kind="ExternalInput").ap()
    Wmats = {n: nc.dram_tensor(n2, [2048, 2048], F32, kind="ExternalInput").ap()
             for n, n2 in (("r", "Wr"), ("k", "Wk"), ("v", "Wv"), ("o", "Wo"))}
    wg8 = nc.dram_tensor("wg8", [C, FF], I8, kind="ExternalInput").ap()
    wu8 = nc.dram_tensor("wu8", [C, FF], I8, kind="ExternalInput").ap()
    wd8 = nc.dram_tensor("wd8", [FF, C], I8, kind="ExternalInput").ap()
    mu_r_pt = nc.dram_tensor("mu_r_pt", [128, CT], F32, kind="ExternalInput").ap()
    mu_k_pt = nc.dram_tensor("mu_k_pt", [128, CT], F32, kind="ExternalInput").ap()
    mu_v_pt = nc.dram_tensor("mu_v_pt", [128, CT], F32, kind="ExternalInput").ap()
    w_pt = nc.dram_tensor("w_pt", [128, HT], F32, kind="ExternalInput").ap()
    u_pt = nc.dram_tensor("u_pt", [128, HT], F32, kind="ExternalInput").ap()
    sgate_pt = nc.dram_tensor("sgate_pt", [128, ITT], F32, kind="ExternalInput").ap()
    sup_pt = nc.dram_tensor("sup_pt", [128, ITT], F32, kind="ExternalInput").ap()
    sdown_b = nc.dram_tensor("sdown_b", [128, C], F32, kind="ExternalInput").ap()
    ident_h = nc.dram_tensor("ident_h", [128, 128], F32, kind="ExternalInput").ap()
    iota_rev_h = nc.dram_tensor("iota_rev_h", [128, T], F32, kind="ExternalInput").ap()
    ones_h = nc.dram_tensor("ones_h", [128, 512], F32, kind="ExternalInput").ap()
    segc_h = nc.dram_tensor("segc_h", [128, NCORES], F32, kind="ExternalInput").ap()
    out = nc.dram_tensor("out", [T, C], F32, kind="ExternalOutput").ap()

    wbf = {n: nc.dram_tensor(f"wbf_{n}", [2048, 2048], BF16, kind="Internal").ap()
           for n in ("r", "k", "v", "o")}
    spill_q = nc.dram_tensor("spill_q", [H, T], F32, kind="Internal").ap()
    spill_p = nc.dram_tensor("spill_p", [H, T], F32, kind="Internal").ap()
    spill_s = nc.dram_tensor("spill_s", [H, T], F32, kind="Internal").ap()
    fuse_hbm = nc.dram_tensor("fuse_hbm", [FF, T], BF16, kind="Internal").ap()
    x1_hbm = nc.dram_tensor("x1_hbm", [T, C], F32, kind="Internal").ap()
    cc_in = nc.dram_tensor("cc_in", [128, 2 * HT], F32, kind="Internal")
    cc_out = nc.dram_tensor("cc_out", [NCORES * 128, 2 * HT], F32, kind="Internal")

    ctx = ExitStack()
    with ctx:
        ARENA_BYTES = 196224
        backing = ctx.enter_context(nc.sbuf_tensor("arena", [128, ARENA_BYTES], U8))
        base = nc.lookup_mloc(backing).addr

        def alloc(name, shape, dtype, off):
            assert off % 64 == 0, name
            nbytes = int(np.prod(shape[1:])) * mybir.dt.size(dtype)
            assert off + nbytes <= ARENA_BYTES, (name, off, nbytes)
            h = nc.virtual_sbuf_tensor(name, shape, dtype)
            nc._tensor(h.name, shape, dtype, type="SB")
            mloc = nc.lookup_mloc(h)
            mloc.allocated = True
            mloc.addr = base + off
            return h

        ident = alloc("ident", [128, 128], BF16, 0)
        ones = alloc("ones", [128, 512], F32, 320)
        iota_rev = alloc("iota_rev", [128, T], F32, 2368)
        vb = [6464]
        def vec(name):
            h = alloc(name, [128, 16], F32, vb[0])
            vb[0] += 64
            return h
        mu = {"r": vec("mu_r"), "k": vec("mu_k"), "v": vec("mu_v")}
        wraw = vec("wraw"); uraw = vec("uraw")
        w_eff = vec("w_eff"); d1 = vec("d1"); eu = vec("eu")
        AinV = vec("AinV"); BinV = vec("BinV")
        Acar = vec("Acar"); Bcar = vec("Bcar")
        tmp16a = vec("tmp16a"); tmp16b = vec("tmp16b"); eps_t = vec("eps_t"); e_t = vec("e_t")
        summ = alloc("summ", [128, 2 * HT], F32, vb[0]); vb[0] += 128
        segc = alloc("segc", [128, 8], F32, vb[0]); vb[0] += 64
        stat = alloc("stat", [128, 16], F32, vb[0]); vb[0] += 64
        sgate_v = alloc("sgate_v", [128, ITT], F32, 8000)
        sup_v = alloc("sup_v", [128, ITT], F32, 8256)
        sdown_v = alloc("sdown_v", [128, C], F32, 8512)
        gath = alloc("gath", [128, NCORES * 2 * HT], F32, 16704)
        coef_all = alloc("coef_all", [128, 128], F32, 17792)
        tmp3 = alloc("tmp3", [128, 128], F32, 18368)
        e128 = alloc("e128", [128, 128], F32, 18944)

        XB = 20 * KB
        mixT = alloc("mixT", [128, CT * T], BF16, XB)
        trash2 = alloc("trash2", [128, C], BF16, XB)
        h2T = alloc("h2T", [128, CT * T], BF16, XB)
        fuse_tt = [alloc("fuse_tt0", [128, ITT * 128], BF16, XB),
                   alloc("fuse_tt1", [128, ITT * 128], BF16, XB + ITT * 128 * 2)]

        BB = 53 * KB
        hT = alloc("hT", [128, CT * (T + 1)], BF16, BB)
        WT = alloc("WT", [128, CT * 2048], BF16, BB + 33 * KB)
        SCR = BB + 99 * KB
        x_t = [alloc(f"x_t{i}", [128, C], F32, SCR + i * 8 * KB) for i in range(2)] + [
            alloc("x_t2", [128, C], F32, SCR + 20 * KB),
            alloc("x_t3", [128, C], F32, SCR + 28 * KB)]
        h_bf = alloc("h_bf", [128, C], BF16, SCR + 16 * KB)
        delta = alloc("delta", [128, T], F32, SCR)
        q_t = [alloc(f"q_t{i}", [128, T], F32, SCR + 4 * KB + i * 4 * KB) for i in range(2)]
        qrr = [alloc(f"qrr{i}", [128, T], F32, SCR + 12 * KB + i * 4 * KB) for i in range(2)]
        p_t = [alloc(f"p_t{i}", [128, T], F32, SCR + 20 * KB + i * 4 * KB) for i in range(2)]
        dpow = [alloc(f"dpow{i}", [128, T], F32, SCR + 28 * KB + i * 4 * KB) for i in range(2)]
        trash_bf = alloc("trash_bf", [128, T], BF16, SCR + 36 * KB)
        s_buf = alloc("s_buf", [128, HT * T], BF16, BB)
        WoT = alloc("WoT", [128, HT * 2048], BF16, BB + 33 * KB)
        rrq = [alloc(f"rrq{i}", [128, 512], F32, SCR + 12 * KB + i * 2 * KB) for i in range(2)]
        rrp = [alloc(f"rrp{i}", [128, 512], F32, SCR + 16 * KB + i * 2 * KB) for i in range(2)]
        rrs = [alloc(f"rrs{i}", [128, 512], F32, SCR + 20 * KB + i * 2 * KB) for i in range(2)]
        Abuf = alloc("Abuf", [128, 513], F32, SCR + 24 * KB)
        Bbuf = alloc("Bbuf", [128, 513], F32, SCR + 26 * KB + 64)
        nbuf = alloc("nbuf", [128, 512], F32, SCR + 28 * KB + 128)
        dbuf = alloc("dbuf", [128, 512], F32, SCR + 30 * KB + 128)
        rec = alloc("rec", [128, 512], F32, SCR)
        d_bc = alloc("d_bc", [128, 512], F32, SCR + 32 * KB + 192)
        x1_t = [alloc("x1_t0", [128, C], F32, SCR + 2 * KB)]
        h2_bf = alloc("h2_bf", [128, C], BF16, SCR + 34 * KB + 256)
        sig_t = alloc("sig_t", [128, T], F32, SCR + 34 * KB + 256)
        x1rr = [alloc(f"x1rr{i}", [128, 512], F32, SCR + 30 * KB + i * 2 * KB) for i in range(2)]
        o_st = [alloc(f"o_st{i}", [128, 512], F32, SCR + 34 * KB + i * 2 * KB) for i in range(2)]
        wgb = [alloc(f"wgb{i}", [128, CT * 512], BF16, BB + i * 16 * KB) for i in range(2)]
        wub = [alloc(f"wub{i}", [128, CT * 512], BF16, BB + 32 * KB + i * 16 * KB) for i in range(2)]
        wgb_alt = alloc("wgba", [128, CT * 512], BF16, SCR)
        wub_alt = alloc("wuba", [128, CT * 512], BF16, SCR + 16 * KB)
        fch = [alloc(f"fch{i}", [128, 8 * 512], BF16, BB + 64 * KB + i * 8 * KB) for i in range(2)]
        g_scr = [alloc(f"g_scr{i}", [128, 512], F32, BB + 80 * KB + i * 2 * KB) for i in range(2)]
        wdb = [alloc(f"wdb{i}", [128, ITT * 512], BF16, BB + i * 64 * KB) for i in range(2)]

        pb = [Bank(ctx.enter_context(nc.psum_tensor(f"pb{i}", [128, 512], F32)))
              for i in range(7)]
        ptr = Bank(ctx.enter_context(nc.psum_tensor("ptr", [128, 1024], BF16)))

        sx = Sem(nc, ctx, "sx")
        swt = Sem(nc, ctx, "swt")
        srr = Sem(nc, ctx, "srr")
        saq = Sem(nc, ctx, "saq")
        sgq = Sem(nc, ctx, "sgq")
        scc = Sem(nc, ctx, "scc")
        sPE = Sem(nc, ctx, "sPE")
        sDV = Sem(nc, ctx, "sDV")
        sAC = Sem(nc, ctx, "sAC")
        sout = Sem(nc, ctx, "sout")
        sspq = Sem(nc, ctx, "sspq")
        sspp = Sem(nc, ctx, "sspp")
        ssps = Sem(nc, ctx, "ssps")
        sx1w = Sem(nc, ctx, "sx1w")
        sfw = Sem(nc, ctx, "sfw")
        srrp = [Sem(nc, ctx, "srrp0"), Sem(nc, ctx, "srrp1")]
        srrs = [Sem(nc, ctx, "srrs0"), Sem(nc, ctx, "srrs1")]
        sPL = Sem(nc, ctx, "sPL")
        sftp = [Sem(nc, ctx, "sftp0"), Sem(nc, ctx, "sftp1")]

        block = ctx.enter_context(nc.Block())

        @block.gpsimd
        def _(g):
            g.dma_start(out=ident[:], in_=ident_h[:, :]).then_inc(sgq.h, 16)
            g.dma_start(out=ones[:], in_=ones_h[:, :]).then_inc(sgq.h, 16)
            g.dma_start(out=iota_rev[:], in_=iota_rev_h[:, :]).then_inc(sgq.h, 16)
            g.dma_start(out=mu["r"][:], in_=mu_r_pt[:, :]).then_inc(sgq.h, 16)
            g.dma_start(out=mu["k"][:], in_=mu_k_pt[:, :]).then_inc(sgq.h, 16)
            g.dma_start(out=mu["v"][:], in_=mu_v_pt[:, :]).then_inc(sgq.h, 16)
            g.dma_start(out=wraw[:], in_=w_pt[:, :]).then_inc(sgq.h, 16)
            g.dma_start(out=uraw[:], in_=u_pt[:, :]).then_inc(sgq.h, 16)
            g.dma_start(out=segc[:], in_=segc_h[:, :]).then_inc(sgq.h, 16)
            g.dma_start(out=sgate_v[:], in_=sgate_pt[:, :]).then_inc(sgq.h, 16)
            g.dma_start(out=sup_v[:], in_=sup_pt[:, :]).then_inc(sgq.h, 16)
            g.dma_start(out=sdown_v[:], in_=sdown_b[:, :]).then_inc(sgq.h, 16)
        SMALL_LD = sgq.tick(12 * 16)

        cast_done = {}
        cast_ct = {}
        def emit_cast(n, pe_gate=0, chunks=1):
            base = sgq.n
            @block.gpsimd
            def _(g):
                if pe_gate:
                    g.wait_ge(sPE.h, pe_gate)
                sgq.chain(g)
                if chunks == 1:
                    g.dma_start(out=wbf[n][:, :], in_=Wmats[n][:, :]).then_inc(sgq.h, 16)
                else:
                    cw = 2048 // chunks
                    for ci in range(chunks):
                        if ci:
                            g.wait_ge(sgq.h, base + 16 * ci)
                        g.dma_start(out=wbf[n][:, ci * cw:(ci + 1) * cw],
                                    in_=Wmats[n][:, ci * cw:(ci + 1) * cw]).then_inc(sgq.h, 16)
            if chunks > 1:
                cast_ct[n] = [base + 16 * (i + 1) for i in range(chunks)]
            cast_done[n] = sgq.tick(16 * chunks)
        emit_cast("k", chunks=4)

        @block.scalar
        def _(a):
            a.wait_ge(sgq.h, SMALL_LD)
            a.activation(out=tmp16a[:], in_=wraw[:], func=AF.Exp).then_inc(sAC.h, 1)
            a.activation(out=eu[:], in_=uraw[:], func=AF.Exp).then_inc(sAC.h, 1)
        AC_EXPW = sAC.tick(2)
        @block.vector
        def _(v):
            v.memset(e128[:], 2.718281828459045).then_inc(sDV.h, 1)
            v.memset(eps_t[:, 0:1], EPS).then_inc(sDV.h, 1)
            v.memset(e_t[:, 0:1], -1.0).then_inc(sDV.h, 1)
            v.wait_ge(sAC.h, AC_EXPW)
            v.tensor_scalar(out=w_eff[:], in0=tmp16a[:], scalar1=-1.0, scalar2=None,
                            op0=ALU.mult).then_inc(sDV.h, 1)
        sDV.tick(3)
        DV_WEFF = sDV.tick(1)
        @block.scalar
        def _(a):
            a.wait_ge(sDV.h, DV_WEFF)
            a.activation(out=d1[:], in_=w_eff[:], func=AF.Exp).then_inc(sAC.h, 1)
        AC_D1 = sAC.tick(1)
        @block.vector
        def _(v):
            v.wait_ge(sDV.h, DV_WEFF)
            for r in range(NCORES):
                v.tensor_scalar(out=coef_all[:, r * 16:(r + 1) * 16], in0=w_eff[:],
                                scalar1=segc[:, r:r + 1], scalar2=None,
                                op0=ALU.mult).then_inc(sDV.h, 1)
        DV_COEF_T = sDV.tick(NCORES)
        @block.gpsimd
        def _(g):
            g.wait_ge(sDV.h, DV_COEF_T)
            g.tensor_tensor(out=coef_all[:], in0=e128[:], in1=coef_all[:],
                            op=ALU.pow).then_inc(sPL.h, 1)
        PL_COEF = sPL.tick(1)

        hT_r = hT[:].rearrange("p (ct w) -> p ct w", ct=CT)
        ln1_dve_done = [0] * 9
        ln1_pe_last = [0] * 9
        x_loaded = [0] * 9
        def ln1_partA(a):
            rows = 128 if a < 8 else 1
            buf = x_t[a % 4]
            @block.sync
            def _(s, a=a, buf=buf, rows=rows):
                if a >= 4:
                    s.wait_ge(sDV.h, ln1_dve_done[a - 4])
                sx.chain(s)
                s.dma_start(out=buf[0:rows, :],
                            in_=x_seg[a * 128:a * 128 + rows, :]).then_inc(sx.h, 16)
            x_loaded[a] = sx.tick(16)
            so = (a % 2) * 8
            xa = buf[0:rows, :]
            @block.vector
            def _(v, a=a, xa=xa, rows=rows, so=so):
                r = slice(0, rows)
                v.wait_ge(sx.h, x_loaded[a])
                if a >= 2:
                    v.wait_ge(sDV.h, ln1_dve_done[a - 2])
                if a >= 1:
                    v.wait_ge(sDV.h, ln1_stt[a - 1])
                v.tensor_reduce(out=stat[r, so:so + 1], in_=xa,
                                axis=mybir.AxisListType.X, op=ALU.add).then_inc(sDV.h, 1)
                v.scalar_tensor_tensor(out=trash2[0:rows, :], in0=xa, scalar=1.0, in1=xa,
                                       op0=ALU.mult, op1=ALU.mult,
                                       accum_out=stat[r, so + 1:so + 2]).then_inc(sDV.h, 1)
                both = sDV.tick(2)
                ln1_stt[a] = both
                v.wait_ge(sDV.h, both)
                v.tensor_scalar(out=stat[r, so + 2:so + 3], in0=stat[r, so:so + 1],
                                scalar1=1.0 / C, scalar2=None, op0=ALU.mult).then_inc(sDV.h, 1)
                v.tensor_scalar(out=stat[r, so + 3:so + 4], in0=stat[r, so + 1:so + 2],
                                scalar1=1.0 / C, scalar2=None, op0=ALU.mult).then_inc(sDV.h, 1)
                m2 = sDV.tick(2)
                v.wait_ge(sDV.h, m2)
                v.tensor_tensor(out=stat[r, so + 4:so + 5], in0=stat[r, so + 2:so + 3],
                                in1=stat[r, so + 2:so + 3], op=ALU.mult).then_inc(sDV.h, 1)
                me = sDV.tick(1)
                v.wait_ge(sDV.h, me)
                v.tensor_tensor(out=stat[r, so + 5:so + 6], in0=stat[r, so + 3:so + 4],
                                in1=stat[r, so + 4:so + 5], op=ALU.subtract).then_inc(sDV.h, 1)
                sDV.tick(1)
            ln1_var[a] = sDV.n
            @block.scalar
            def _(a_, rows=rows, so=so, a=a):
                a_.wait_ge(sDV.h, ln1_var[a])
                a_.activation(out=stat[0:rows, so + 6:so + 7], in_=stat[0:rows, so + 5:so + 6],
                              func=AF.Sqrt, bias=eps_t[0:rows, 0:1]).then_inc(sAC.h, 1)
            ln1_sqrt[a] = sAC.tick(1)

        def ln1_partB(a):
            rows = 128 if a < 8 else 1
            buf = x_t[a % 4]
            so = (a % 2) * 8
            xa = buf[0:rows, :]
            @block.vector
            def _(v, a=a, xa=xa, rows=rows, so=so):
                r = slice(0, rows)
                v.wait_ge(sAC.h, ln1_sqrt[a])
                v.reciprocal(out=stat[r, so + 7:so + 8], in_=stat[r, so + 6:so + 7]).then_inc(sDV.h, 1)
                rst = sDV.tick(1)
                v.wait_ge(sDV.h, rst)
                if a >= 1:
                    v.wait_ge(sPE.h, ln1_pe_last[a - 1])
                v.tensor_scalar(out=h_bf[0:rows, :], in0=xa, scalar1=stat[r, so + 2:so + 3],
                                scalar2=stat[r, so + 7:so + 8], op0=ALU.subtract,
                                op1=ALU.mult).then_inc(sDV.h, 1)
                sDV.tick(1)
            dv_h = sDV.n
            if a < 8:
                for rnd in range(2):
                    @block.tensor
                    def _(t, a=a, rnd=rnd, dv_h=dv_h):
                        t.wait_ge(sDV.h, dv_h)
                        if a == 0 and rnd == 0:
                            t.wait_ge(sgq.h, SMALL_LD)
                        ptr.acquire(t)
                        for b in range(8):
                            ct = rnd * 8 + b
                            mm = t.transpose(ptr.t[:, b * 128:(b + 1) * 128],
                                             h_bf[:, ct * 128:(ct + 1) * 128], ident[:])
                            if b == 7:
                                mm.then_inc(sPE.h, 1)
                    pe_tr = sPE.tick(1)
                    @block.scalar
                    def _(a_, a=a, rnd=rnd, pe_tr=pe_tr):
                        a_.wait_ge(sPE.h, pe_tr)
                        a_.copy(out=hT_r[:, rnd * 8:(rnd + 1) * 8, a * 128 + 0:a * 128 + 128],
                                in_=ptr.t[:].rearrange("p (b w) -> p b w", b=8)).then_inc(sAC.h, 1)
                    ptr.release(sAC, sAC.tick(1))
                    ln1_pe_last[a] = sPE.n
            else:
                @block.tensor
                def _(t, dv_h=dv_h):
                    t.wait_ge(sDV.h, dv_h)
                    ptr.acquire(t)
                    for ct2 in range(16):
                        mm = t.transpose(ptr.t[:, 2 * ct2:2 * ct2 + 1],
                                         h_bf[0:1, ct2 * 128:(ct2 + 1) * 128], ident[0:1, 0:1])
                        if ct2 == 15:
                            mm.then_inc(sPE.h, 1)
                pe_tr = sPE.tick(1)
                @block.scalar
                def _(a_, pe_tr=pe_tr):
                    a_.wait_ge(sPE.h, pe_tr)
                    a_.copy(out=hT_r[:, :, 1024:1025],
                            in_=ptr.t[:].rearrange("p (b w) -> p b w", w=2)[:, 0:16, 0:1]
                            ).then_inc(sAC.h, 1)
                ptr.release(sAC, sAC.tick(1))
                ln1_pe_last[a] = sPE.n
            ln1_dve_done[a] = sDV.n

        ln1_var = [0] * 9
        ln1_sqrt = [0] * 9
        ln1_stt = [0] * 9
        for a in range(9):
            ln1_partA(a)
            if a >= 1:
                ln1_partB(a - 1)
        ln1_partB(8)
        LN1_DVE_END = sDV.n
        LN1_AC_END = sAC.n

        def emit_mixes(which, prev_pe_end):
            muv = mu[which]
            for ct in range(CT):
                @block.vector
                def _(v, ct=ct):
                    if ct == 0:
                        v.wait_ge(sDV.h, LN1_DVE_END)
                        v.wait_ge(sAC.h, LN1_AC_END)
                        if prev_pe_end:
                            v.wait_ge(sPE.h, prev_pe_end)
                    b0 = ct * (T + 1)
                    v.wait_ge(sDV.h, sDV.n)
                    v.tensor_tensor(out=delta[:], in0=hT[:, b0 + 1:b0 + T + 1],
                                    in1=hT[:, b0:b0 + T], op=ALU.subtract).then_inc(sDV.h, 1)
                    dd = sDV.tick(1)
                    v.wait_ge(sDV.h, dd)
                    v.scalar_tensor_tensor(out=mixT[:, ct * T:(ct + 1) * T], in0=delta[:],
                                           scalar=muv[:, ct:ct + 1], in1=hT[:, b0:b0 + T],
                                           op0=ALU.mult, op1=ALU.add).then_inc(sDV.h, 1)
                    sDV.tick(1)
            return sDV.n

        def emit_wt_loads(name, ready_after_pe):
            gates = cast_ct.get(name)
            @block.sync
            def _(s):
                if gates is None:
                    s.wait_ge(sgq.h, cast_done[name])
                if ready_after_pe:
                    s.wait_ge(sPE.h, ready_after_pe)
                swt.chain(s)
                for ct in range(CT):
                    if gates is not None:
                        s.wait_ge(sgq.h, gates[ct * len(gates) // CT])
                    s.dma_start(out=WT[:, ct * 2048:(ct + 1) * 2048],
                                in_=wbf[name][0:2048, ct * 128:(ct + 1) * 128],
                                transpose=True).then_inc(swt.h, 16)
            return swt.tick(16 * 16)

        mm_banks = [pb[0], pb[1], pb[2], pb[3]]

        def emit_proj_group(ht, mix_ready, wt_gates, first):
            res = []
            @block.tensor
            def _(t, ht=ht):
                if first:
                    t.wait_ge(sDV.h, mix_ready)
                if ht in wt_gates:
                    t.wait_ge(swt.h, wt_gates[ht])
                for tc in range(2):
                    bk = mm_banks[(ht * 2 + tc) % 4]
                    bk.acquire(t)
                    for ct in range(CT):
                        mm = t.matmul(bk.t[:],
                                      WT[:, ct * 2048 + ht * 128: ct * 2048 + (ht + 1) * 128],
                                      mixT[:, ct * T + tc * 512: ct * T + tc * 512 + 512],
                                      start=(ct == 0), stop=(ct == CT - 1))
                        if ct == CT - 1:
                            mm.then_inc(sPE.h, 1)
                    res.append((bk, sPE.tick(1)))
            return res

        mixk_done = emit_mixes("k", 0)
        wtk_done = emit_wt_loads("k", 0)
        KG_FIRST = [0]
        q_spill_done = [0] * HT
        qbuf_free = [[], []]
        dpow_free = [[], []]
        for ht in range(HT):
            grp = emit_proj_group(ht, mixk_done, {0: wtk_done}, ht == 0)
            if ht == 0:
                KG_FIRST[0] = sPE.n
                emit_cast("v", pe_gate=KG_FIRST[0], chunks=4)
                emit_cast("r", pe_gate=KG_FIRST[0], chunks=4)
            qb = q_t[ht % 2]
            dp = dpow[ht % 2]
            @block.scalar
            def _(a_, ht=ht, qb=qb, grp=grp):
                wait_all(a_, qbuf_free[ht % 2])
                a_.wait_ge(sPE.h, grp[0][1])
                a_.activation(out=qb[:, 0:512], in_=grp[0][0].t[:], func=AF.Exp).then_inc(sAC.h, 1)
                grp[0][0].release(sAC, sAC.tick(1))
                a_.wait_ge(sPE.h, grp[1][1])
                a_.activation(out=qb[:, 512:1024], in_=grp[1][0].t[:], func=AF.Exp).then_inc(sAC.h, 1)
                grp[1][0].release(sAC, sAC.tick(1))
            ac_q = sAC.n
            @block.sync
            def _(s, ht=ht, qb=qb, ac_q=ac_q):
                s.wait_ge(sAC.h, ac_q)
                sspq.chain(s)
                s.dma_start(out=spill_q[ht * 128:(ht + 1) * 128, :], in_=qb[:]).then_inc(sspq.h, 16)
            q_spill_done[ht] = sspq.tick(16)
            @block.scalar
            def _(a_, ht=ht, dp=dp):
                wait_all(a_, dpow_free[ht % 2])
                if ht < 2:
                    a_.wait_ge(sAC.h, AC_D1)
                    a_.wait_ge(sDV.h, LN1_DVE_END)
                a_.activation(out=dp[:], in_=iota_rev[:], func=AF.Exp,
                              scale=w_eff[:, ht:ht + 1]).then_inc(sAC.h, 1)
            ac_dp = sAC.tick(1)
            @block.vector
            def _(v, ht=ht, qb=qb, dp=dp, ac_q=ac_q, ac_dp=ac_dp):
                v.wait_ge(sAC.h, max(ac_q, ac_dp))
                v.wait_ge(sDV.h, sDV.n)
                v.scalar_tensor_tensor(out=trash_bf[:], in0=qb[:], scalar=1.0, in1=dp[:],
                                       op0=ALU.mult, op1=ALU.mult,
                                       accum_out=summ[:, HT + ht:HT + ht + 1]).then_inc(sDV.h, 1)
            dv_stt = sDV.tick(1)
            qbuf_free[ht % 2] = [(sDV, dv_stt), (sspq, q_spill_done[ht])]
            dpow_free[ht % 2] = [(sDV, dv_stt)]
        K_DVE_END = sDV.n
        K_PE_END = sPE.n
        K_SPILL_END = sspq.n

        mixv_done = emit_mixes("v", K_PE_END)
        wtv_done = emit_wt_loads("v", K_PE_END)
        VG_FIRST = [0]
        p_spill_done = [0] * HT
        pbuf_free = [[], []]
        qrr_free = [[(sspq, K_SPILL_END)], [(sspq, K_SPILL_END)]]
        qrr_loaded = [0] * HT
        for ht in range(HT):
            grp = emit_proj_group(ht, mixv_done, {0: wtv_done}, ht == 0)
            if ht == 0:
                VG_FIRST[0] = sPE.n
                emit_cast("o", pe_gate=VG_FIRST[0])
            @block.sync
            def _(s, ht=ht):
                wait_all(s, qrr_free[ht % 2])
                if ht == 0:
                    s.wait_ge(sspq.h, K_SPILL_END)
                srr.chain(s)
                s.dma_start(out=qrr[ht % 2][:], in_=spill_q[ht * 128:(ht + 1) * 128, :]
                            ).then_inc(srr.h, 16)
            qrr_loaded[ht] = srr.tick(16)
            pbx = p_t[ht % 2]
            dp = dpow[ht % 2]
            @block.vector
            def _(v, ht=ht, pbx=pbx, grp=grp):
                wait_all(v, pbuf_free[ht % 2])
                v.wait_ge(srr.h, qrr_loaded[ht])
                v.wait_ge(sPE.h, grp[0][1])
                v.tensor_tensor(out=pbx[:, 0:512], in0=qrr[ht % 2][:, 0:512], in1=grp[0][0].t[:],
                                op=ALU.mult).then_inc(sDV.h, 1)
                grp[0][0].release(sDV, sDV.tick(1))
                v.wait_ge(sPE.h, grp[1][1])
                v.tensor_tensor(out=pbx[:, 512:1024], in0=qrr[ht % 2][:, 512:1024],
                                in1=grp[1][0].t[:], op=ALU.mult).then_inc(sDV.h, 1)
                grp[1][0].release(sDV, sDV.tick(1))
            qrr_free[ht % 2] = [(sDV, sDV.n)]
            dv_p = sDV.n
            @block.scalar
            def _(a_, ht=ht, dp=dp, dv_p=dv_p, pbx=pbx):
                wait_all(a_, dpow_free[ht % 2])
                a_.activation(out=dp[:], in_=iota_rev[:], func=AF.Exp,
                              scale=w_eff[:, ht:ht + 1]).then_inc(sAC.h, 1)
                a_.wait_ge(sDV.h, dv_p)
                sspp.chain(a_)
                a_.dma_start(out=spill_p[ht * 128:(ht + 1) * 128, :], in_=pbx[:]).then_inc(sspp.h, 16)
            ac_dp = sAC.tick(1)
            p_spill_done[ht] = sspp.tick(16)
            @block.vector
            def _(v, ht=ht, pbx=pbx, dp=dp, ac_dp=ac_dp):
                v.wait_ge(sAC.h, ac_dp)
                v.wait_ge(sDV.h, sDV.n)
                v.scalar_tensor_tensor(out=trash_bf[:], in0=pbx[:], scalar=1.0, in1=dp[:],
                                       op0=ALU.mult, op1=ALU.mult,
                                       accum_out=summ[:, ht:ht + 1]).then_inc(sDV.h, 1)
            dv_stt = sDV.tick(1)
            pbuf_free[ht % 2] = [(sDV, dv_stt), (sspp, p_spill_done[ht])]
            dpow_free[ht % 2] = [(sDV, dv_stt)]
        V_DVE_END = sDV.n
        V_PE_END = sPE.n

        @block.gpsimd
        def _(g):
            g.wait_ge(sDV.h, V_DVE_END)
            sgq.chain(g)
            g.dma_start(out=cc_in[:, :], in_=summ[:]).then_inc(sgq.h, 16)
            ccin_ld = sgq.tick(16)
            g.wait_ge(sgq.h, ccin_ld)
            g.collective_compute(
                "AllGather", ALU.bypass,
                replica_groups=[list(range(NCORES))],
                ins=[cc_in.ap().opt()],
                outs=[cc_out.ap().opt()],
            ).then_inc(scc.h, 1)
            g.wait_ge(scc.h, scc.tick(1))
            g.dma_start(out=gath[:].rearrange("p (r f) -> p r f", r=NCORES),
                        in_=cc_out.ap().rearrange("(r p) f -> p r f", p=128)).then_inc(sgq.h, 16)
        GATH_LD = sgq.tick(16)

        mixr_done = emit_mixes("r", V_PE_END)
        wtr_done = emit_wt_loads("r", V_PE_END)
        RG_FIRST = [0]
        @block.vector
        def _(v):
            v.wait_ge(sgq.h, GATH_LD)
            v.wait_ge(sPL.h, PL_COEF)
            v.wait_ge(sDV.h, sDV.n)
            cview = coef_all[:].rearrange("p (r i) -> p i r", r=NCORES)
            gA = gath[:].rearrange("p (r f) -> p f r", r=NCORES)[:, 0:HT, :]
            gB = gath[:].rearrange("p (r f) -> p f r", r=NCORES)[:, HT:2 * HT, :]
            t3 = tmp3[:].rearrange("p (i r) -> p i r", r=NCORES)
            v.tensor_tensor(out=t3, in0=cview, in1=gA, op=ALU.mult).then_inc(sDV.h, 1)
            pA = sDV.tick(1)
            v.wait_ge(sDV.h, pA)
            v.tensor_reduce(out=AinV[:], in_=t3, axis=mybir.AxisListType.X,
                            op=ALU.add).then_inc(sDV.h, 1)
            pA2 = sDV.tick(1)
            v.wait_ge(sDV.h, pA2)
            v.tensor_tensor(out=t3, in0=cview, in1=gB, op=ALU.mult).then_inc(sDV.h, 1)
            pB = sDV.tick(1)
            v.wait_ge(sDV.h, pB)
            v.tensor_reduce(out=BinV[:], in_=t3, axis=mybir.AxisListType.X,
                            op=ALU.add).then_inc(sDV.h, 1)
            sDV.tick(1)
        CARRY_DONE = sDV.n
        s_spill_done = [0] * HT
        sgbuf_free = [[], []]
        for ht in range(HT):
            grp = emit_proj_group(ht, mixr_done, {0: wtr_done}, ht == 0)
            if ht == 0:
                RG_FIRST[0] = sPE.n
            sbx = sig_t
            @block.scalar
            def _(a_, ht=ht, sbx=sbx, grp=grp):
                wait_all(a_, sgbuf_free[0])
                if ht == 0:
                    wait_all(a_, pbuf_free[0])
                    wait_all(a_, pbuf_free[1])
                a_.wait_ge(sPE.h, grp[0][1])
                a_.activation(out=sbx[:, 0:512], in_=grp[0][0].t[:], func=AF.Sigmoid).then_inc(sAC.h, 1)
                grp[0][0].release(sAC, sAC.tick(1))
                a_.wait_ge(sPE.h, grp[1][1])
                a_.activation(out=sbx[:, 512:1024], in_=grp[1][0].t[:], func=AF.Sigmoid).then_inc(sAC.h, 1)
                grp[1][0].release(sAC, sAC.tick(1))
            ac_s = sAC.n
            @block.scalar
            def _(a_, ht=ht, sbx=sbx, ac_s=ac_s):
                a_.wait_ge(sAC.h, ac_s)
                ssps.chain(a_)
                a_.dma_start(out=spill_s[ht * 128:(ht + 1) * 128, :], in_=sbx[:]).then_inc(ssps.h, 16)
            s_spill_done[ht] = ssps.tick(16)
            sgbuf_free[0] = [(ssps, s_spill_done[ht])]
        R_PE_END = sPE.n
        SPILLQ_ALL = sspq.n
        SPILLP_ALL = sspp.n
        SPILLS_ALL = ssps.n


        rrqp_free = [[(sspq, SPILLQ_ALL), (sspp, SPILLP_ALL)],
                     [(sspq, SPILLQ_ALL), (sspp, SPILLP_ALL)]]
        rrs_free = [[], []]
        scan_first = [True]

        def scan_block(k, ch, ht):
            pp = ht % 2
            sem = srrp[pp]
            @block.sync
            def _(s, ch=ch, ht=ht, pp=pp, sem=sem):
                wait_all(s, rrqp_free[pp])
                cs_ = slice(ch * 512, ch * 512 + 512)
                rw = slice(ht * 128, (ht + 1) * 128)
                s.dma_start(out=rrq[pp][:], in_=spill_q[rw, cs_]).then_inc(sem.h, 16)
                s.dma_start(out=rrp[pp][:], in_=spill_p[rw, cs_]).then_inc(sem.h, 16)
                wait_all(s, rrs_free[pp])
                s.wait_ge(ssps.h, s_spill_done[ht])
                s.dma_start(out=rrs[pp][:], in_=spill_s[rw, cs_]).then_inc(srrs[pp].h, 16)
            rr_ld = sem.tick(32)
            rs_ld = srrs[pp].tick(16)
            @block.vector
            def _(v, k=k, ch=ch, ht=ht, pp=pp, rr_ld=rr_ld, rs_ld=rs_ld, sem=sem):
                if scan_first[0]:
                    v.wait_ge(sDV.h, CARRY_DONE)
                    v.wait_ge(sAC.h, AC_D1)
                    v.wait_ge(sgq.h, SMALL_LD)
                    scan_first[0] = False
                v.wait_ge(sem.h, rr_ld)
                v.wait_ge(sDV.h, sDV.n)
                v.tensor_scalar(out=d_bc[:], in0=ones[:], scalar1=d1[:, ht:ht + 1],
                                scalar2=None, op0=ALU.mult).then_inc(sDV.h, 1)
                srcA = AinV if ch == 0 else Acar
                srcB = BinV if ch == 0 else Bcar
                v.tensor_scalar(out=Abuf[:, 0:1], in0=srcA[:, ht:ht + 1], scalar1=0.0,
                                scalar2=None, op0=ALU.add).then_inc(sDV.h, 1)
                v.tensor_scalar(out=Bbuf[:, 0:1], in0=srcB[:, ht:ht + 1], scalar1=0.0,
                                scalar2=None, op0=ALU.add).then_inc(sDV.h, 1)
                pre = sDV.tick(3)
                v.wait_ge(sDV.h, pre)
                v.tensor_tensor_scan(out=Abuf[:, 1:513], data0=d_bc[:], data1=rrp[pp][:],
                                     initial=Abuf[:, 0:1], op0=ALU.mult,
                                     op1=ALU.add).then_inc(sDV.h, 1)
                v.tensor_tensor_scan(out=Bbuf[:, 1:513], data0=d_bc[:], data1=rrq[pp][:],
                                     initial=Bbuf[:, 0:1], op0=ALU.mult,
                                     op1=ALU.add).then_inc(sDV.h, 1)
                sc = sDV.tick(2)
                v.wait_ge(sDV.h, sc)
                if ch == 0:
                    v.tensor_scalar(out=Acar[:, ht:ht + 1], in0=Abuf[:, 512:513], scalar1=0.0,
                                    scalar2=None, op0=ALU.add).then_inc(sDV.h, 1)
                    v.tensor_scalar(out=Bcar[:, ht:ht + 1], in0=Bbuf[:, 512:513], scalar1=0.0,
                                    scalar2=None, op0=ALU.add).then_inc(sDV.h, 1)
                    sDV.tick(2)
                v.scalar_tensor_tensor(out=nbuf[:], in0=rrp[pp][:], scalar=eu[:, ht:ht + 1],
                                       in1=Abuf[:, 0:512], op0=ALU.mult,
                                       op1=ALU.add).then_inc(sDV.h, 1)
                v.scalar_tensor_tensor(out=dbuf[:], in0=rrq[pp][:], scalar=eu[:, ht:ht + 1],
                                       in1=Bbuf[:, 0:512], op0=ALU.mult,
                                       op1=ALU.add).then_inc(sDV.h, 1)
                nd = sDV.tick(2)
                rrqp_free[pp] = [(sDV, nd)]
                v.wait_ge(sDV.h, nd)
                v.reciprocal(out=rec[:], in_=dbuf[:]).then_inc(sDV.h, 1)
                rc = sDV.tick(1)
                v.wait_ge(sDV.h, rc)
                v.tensor_tensor(out=dbuf[:], in0=nbuf[:], in1=rec[:],
                                op=ALU.mult).then_inc(sDV.h, 1)
                yy = sDV.tick(1)
                v.wait_ge(sDV.h, yy)
                v.wait_ge(srrs[pp].h, rs_ld)
                v.tensor_tensor(out=s_buf[:, ht * T + ch * 512: ht * T + ch * 512 + 512],
                                in0=dbuf[:], in1=rrs[pp][:], op=ALU.mult).then_inc(sDV.h, 1)
                sDV.tick(1)
            rrs_free[pp] = [(sDV, sDV.n)]

        ch1_pend = [None]
        s_done_ch1 = [0] * HT

        def emit_pending(v):
            ht_p, pp_p, ac_p, rs_p = ch1_pend[0]
            v.wait_ge(sAC.h, ac_p)
            v.tensor_tensor(out=dbuf[:], in0=nbuf[:], in1=rec[:],
                            op=ALU.mult).then_inc(sDV.h, 1)
            yy = sDV.tick(1)
            v.wait_ge(sDV.h, yy)
            v.wait_ge(srrs[pp_p].h, rs_p)
            v.tensor_tensor(out=s_buf[:, ht_p * T + 512: ht_p * T + 1024],
                            in0=dbuf[:], in1=rrs[pp_p][:], op=ALU.mult).then_inc(sDV.h, 1)
            sDV.tick(1)
            s_done_ch1[ht_p] = sDV.n
            rrs_free[pp_p] = [(sDV, sDV.n)]
            ch1_pend[0] = None

        def scan_block_ch1(ht):
            pp = ht % 2
            sem = srrp[pp]
            @block.sync
            def _(s, ht=ht, pp=pp, sem=sem):
                wait_all(s, rrqp_free[pp])
                cs_ = slice(512, 1024)
                rw = slice(ht * 128, (ht + 1) * 128)
                s.dma_start(out=rrq[pp][:], in_=spill_q[rw, cs_]).then_inc(sem.h, 16)
                s.dma_start(out=rrp[pp][:], in_=spill_p[rw, cs_]).then_inc(sem.h, 16)
                wait_all(s, rrs_free[pp])
                s.dma_start(out=rrs[pp][:], in_=spill_s[rw, cs_]).then_inc(srrs[pp].h, 16)
            rr_ld = sem.tick(32)
            rs_ld = srrs[pp].tick(16)
            @block.vector
            def _(v, ht=ht, pp=pp, rr_ld=rr_ld, sem=sem):
                v.wait_ge(sem.h, rr_ld)
                v.wait_ge(sDV.h, sDV.n)
                v.tensor_scalar(out=d_bc[:], in0=ones[:], scalar1=d1[:, ht:ht + 1],
                                scalar2=None, op0=ALU.mult).then_inc(sDV.h, 1)
                v.tensor_scalar(out=Abuf[:, 0:1], in0=Acar[:, ht:ht + 1], scalar1=0.0,
                                scalar2=None, op0=ALU.add).then_inc(sDV.h, 1)
                v.tensor_scalar(out=Bbuf[:, 0:1], in0=Bcar[:, ht:ht + 1], scalar1=0.0,
                                scalar2=None, op0=ALU.add).then_inc(sDV.h, 1)
                pre = sDV.tick(3)
                v.wait_ge(sDV.h, pre)
                v.tensor_tensor_scan(out=Abuf[:, 1:513], data0=d_bc[:], data1=rrp[pp][:],
                                     initial=Abuf[:, 0:1], op0=ALU.mult,
                                     op1=ALU.add).then_inc(sDV.h, 1)
                v.tensor_tensor_scan(out=Bbuf[:, 1:513], data0=d_bc[:], data1=rrq[pp][:],
                                     initial=Bbuf[:, 0:1], op0=ALU.mult,
                                     op1=ALU.add).then_inc(sDV.h, 1)
                sc = sDV.tick(2)
                if ch1_pend[0] is not None:
                    emit_pending(v)
                v.wait_ge(sDV.h, sDV.n)
                v.scalar_tensor_tensor(out=nbuf[:], in0=rrp[pp][:], scalar=eu[:, ht:ht + 1],
                                       in1=Abuf[:, 0:512], op0=ALU.mult,
                                       op1=ALU.add).then_inc(sDV.h, 1)
                v.scalar_tensor_tensor(out=dbuf[:], in0=rrq[pp][:], scalar=eu[:, ht:ht + 1],
                                       in1=Bbuf[:, 0:512], op0=ALU.mult,
                                       op1=ALU.add).then_inc(sDV.h, 1)
                sDV.tick(2)
            nd = sDV.n
            lnt = rrq[pp]
            @block.scalar
            def _(a_, nd=nd, lnt=lnt):
                a_.wait_ge(sDV.h, nd)
                a_.activation(out=lnt[:], in_=dbuf[:], func=AF.Ln).then_inc(sAC.h, 1)
                a_.wait_ge(sAC.h, sAC.tick(1))
                a_.activation(out=rec[:], in_=lnt[:], func=AF.Exp,
                              scale=e_t[:, 0:1]).then_inc(sAC.h, 1)
            ac2 = sAC.tick(1)
            rrqp_free[pp] = [(sDV, nd), (sAC, ac2)]
            ch1_pend[0] = (ht, pp, ac2, rs_ld)

        h2T_r = h2T[:].rearrange("p (ct w) -> p ct w", ct=CT)
        wo_banks = [pb[0], pb[1], pb[2], pb[3]]
        ph4_pe_last = [0] * 8
        x1_free = [[(sDV, 0)]]
        wo_x1_done = [0] * 8
        h2t_done_tt = [0] * 8

        def wo_mm_part(tt, s_done):
            @block.tensor
            def _(t, tt=tt, s_done=s_done):
                for cs in range(4):
                    wo_banks[cs].acquire(t)
                for ht2 in range(HT):
                    t.wait_ge(sDV.h, s_done[ht2])
                    t.wait_ge(swt.h, WO_LD_HT[ht2])
                    for cs in range(4):
                        mm = t.matmul(wo_banks[cs].t[:],
                                      s_buf[:, ht2 * T + tt * 128: ht2 * T + (tt + 1) * 128],
                                      WoT[:, ht2 * 2048 + cs * 512: ht2 * 2048 + cs * 512 + 512],
                                      start=(ht2 == 0), stop=(ht2 == HT - 1))
                        if ht2 == HT - 1:
                            mm.then_inc(sPE.h, 1)
            return [(wo_banks[cs], sPE.tick(1)) for cs in range(4)]

        def wo_evac_part(tt, wo_g):
            x1b = x1_t[0]
            @block.sync
            def _(s, tt=tt, x1b=x1b):
                wait_all(s, x1_free[0])
                sx.chain(s)
                s.dma_start(out=x1b[:],
                            in_=x_seg[1 + tt * 128:1 + (tt + 1) * 128, :]).then_inc(sx.h, 16)
            x_ld = sx.tick(16)
            for cs in range(4):
                bk, pev = wo_g[cs]
                @block.vector
                def _(v, cs=cs, bk=bk, pev=pev, x1b=x1b, x_ld=x_ld):
                    if cs == 0:
                        v.wait_ge(sx.h, x_ld)
                        v.wait_ge(sDV.h, sDV.n)
                    v.wait_ge(sPE.h, pev)
                    v.tensor_tensor(out=x1b[:, cs * 512:cs * 512 + 512],
                                    in0=x1b[:, cs * 512:cs * 512 + 512],
                                    in1=bk.t[:], op=ALU.add).then_inc(sDV.h, 1)
                bk.release(sDV, sDV.tick(1))
            wo_x1_done[tt] = sDV.n

        ln2_dvh2 = [0] * 8

        def ln2_math(tt):
            x1b = x1_t[0]
            so = (tt % 2) * 8
            @block.scalar
            def _(a_, x1b=x1b, so=so, tt=tt):
                a_.wait_ge(sDV.h, wo_x1_done[tt])
                a_.wait_ge(sAC.h, sAC.n)
                if tt >= 1:
                    a_.wait_ge(sPE.h, ph4_pe_last[tt - 1])
                a_.activation(out=h2_bf[:], in_=x1b[:], func=AF.Square,
                              accum_out=stat[:, so + 1:so + 2]).then_inc(sAC.h, 1)
            ac_sq2 = sAC.tick(1)
            @block.vector
            def _(v, x1b=x1b, so=so, tt=tt, ac_sq2=ac_sq2):
                v.wait_ge(sDV.h, wo_x1_done[tt])
                v.tensor_reduce(out=stat[:, so:so + 1], in_=x1b[:], axis=mybir.AxisListType.X,
                                op=ALU.add).then_inc(sDV.h, 1)
                both = sDV.tick(1)
                v.wait_ge(sDV.h, both)
                v.wait_ge(sAC.h, ac_sq2)
                v.tensor_scalar(out=stat[:, so + 2:so + 3], in0=stat[:, so:so + 1],
                                scalar1=1.0 / C, scalar2=None, op0=ALU.mult).then_inc(sDV.h, 1)
                v.tensor_scalar(out=stat[:, so + 3:so + 4], in0=stat[:, so + 1:so + 2],
                                scalar1=1.0 / C, scalar2=None, op0=ALU.mult).then_inc(sDV.h, 1)
                m2 = sDV.tick(2)
                v.wait_ge(sDV.h, m2)
                v.tensor_tensor(out=stat[:, so + 4:so + 5], in0=stat[:, so + 2:so + 3],
                                in1=stat[:, so + 2:so + 3], op=ALU.mult).then_inc(sDV.h, 1)
                me = sDV.tick(1)
                v.wait_ge(sDV.h, me)
                v.tensor_tensor(out=stat[:, so + 5:so + 6], in0=stat[:, so + 3:so + 4],
                                in1=stat[:, so + 4:so + 5], op=ALU.subtract).then_inc(sDV.h, 1)
                sDV.tick(1)
            dv_var = sDV.n
            @block.scalar
            def _(a_, tt=tt, x1b=x1b, so=so, dv_var=dv_var):
                a_.wait_ge(sDV.h, dv_var)
                a_.activation(out=stat[:, so + 6:so + 7], in_=stat[:, so + 5:so + 6],
                              func=AF.Sqrt, bias=eps_t[:, 0:1]).then_inc(sAC.h, 1)
                sx1w.chain(a_)
                a_.dma_start(out=x1_hbm[tt * 128:(tt + 1) * 128, :], in_=x1b[:]).then_inc(sx1w.h, 16)
            ac_sqrt = sAC.tick(1)
            x1_sp = sx1w.tick(16)
            @block.vector
            def _(v, tt=tt, x1b=x1b, so=so, ac_sqrt=ac_sqrt):
                v.wait_ge(sAC.h, ac_sqrt)
                v.reciprocal(out=stat[:, so + 7:so + 8], in_=stat[:, so + 6:so + 7]).then_inc(sDV.h, 1)
                rst = sDV.tick(1)
                v.wait_ge(sDV.h, rst)
                if tt >= 1:
                    v.wait_ge(sPE.h, ph4_pe_last[tt - 1])
                v.tensor_scalar(out=h2_bf[:], in0=x1b[:], scalar1=stat[:, so + 2:so + 3],
                                scalar2=stat[:, so + 7:so + 8], op0=ALU.subtract,
                                op1=ALU.mult).then_inc(sDV.h, 1)
                sDV.tick(1)
            dv_h2 = sDV.n
            ln2_dvh2[tt] = dv_h2
            x1_free[0] = [(sDV, dv_h2), (sx1w, x1_sp)]

        def ln2_tr(tt):
            dv_h2 = ln2_dvh2[tt]
            for rnd in range(2):
                @block.tensor
                def _(t, rnd=rnd, dv_h2=dv_h2):
                    t.wait_ge(sDV.h, dv_h2)
                    ptr.acquire(t)
                    for b in range(8):
                        ct2 = rnd * 8 + b
                        mm = t.transpose(ptr.t[:, b * 128:(b + 1) * 128],
                                         h2_bf[:, ct2 * 128:(ct2 + 1) * 128], ident[:])
                        if b == 7:
                            mm.then_inc(sPE.h, 1)
                pe_tr = sPE.tick(1)
                @block.scalar
                def _(a_, tt=tt, rnd=rnd, pe_tr=pe_tr):
                    a_.wait_ge(sPE.h, pe_tr)
                    a_.copy(out=h2T_r[:, rnd * 8:(rnd + 1) * 8, tt * 128:(tt + 1) * 128],
                            in_=ptr.t[:].rearrange("p (b w) -> p b w", b=8)).then_inc(sAC.h, 1)
                ptr.release(sAC, sAC.tick(1))
                ph4_pe_last[tt] = sPE.n
            h2t_done_tt[tt] = sAC.n

        s_done_ch0 = [0] * HT
        for ht in range(HT):
            scan_block(ht, 0, ht)
            s_done_ch0[ht] = sDV.n
        WO_B = swt.n
        @block.scalar
        def _(a_):
            a_.wait_ge(sgq.h, cast_done["o"])
            a_.wait_ge(sPE.h, R_PE_END)
            swt.chain(a_)
            for ht2 in range(HT):
                if ht2:
                    a_.wait_ge(swt.h, WO_B + 16 * ht2)
                a_.dma_start(out=WoT[:, ht2 * 2048:(ht2 + 1) * 2048],
                             in_=wbf["o"][0:2048, ht2 * 128:(ht2 + 1) * 128],
                             transpose=True).then_inc(swt.h, 16)
        swt.tick(16 * 16)
        WO_LD_HT = [WO_B + 16 * (i + 1) for i in range(HT)]

        for tt in range(4):
            wo_g = wo_mm_part(tt, s_done_ch0)
            if tt >= 1:
                ln2_tr(tt - 1)
            wo_evac_part(tt, wo_g)
            ln2_math(tt)
            for j in range(4):
                scan_block_ch1(tt * 4 + j)
        @block.vector
        def _(v):
            emit_pending(v)
        for tt in range(4, 8):
            wo_g = wo_mm_part(tt, s_done_ch1)
            ln2_tr(tt - 1)
            wo_evac_part(tt, wo_g)
            ln2_math(tt)
        ln2_tr(7)

        H2T_DONE = sAC.n
        WO_PE_END = sPE.n
        X1_SPILL_END = sx1w.n
        PH4_DVE_END = sDV.n
        NCH = 16
        wgbuf_free = [[(sPE, WO_PE_END)], [(sPE, WO_PE_END)]]
        fch_free = [[(sPE, WO_PE_END)], [(sPE, WO_PE_END)]]
        gscr_free = [[(sPE, WO_PE_END)], [(sPE, WO_PE_END)]]
        ffn_banks = [pb[0], pb[1], pb[2], pb[3], pb[4], pb[5]]
        bank_i = [0]
        gscr_i = [0]
        fuse_wr = [0] * NCH
        wg_ld = [0] * NCH
        for ich in range(NCH):
            pp = ich % 2
            wgx = wgb[pp] if ich < NCH - 1 else wgb_alt
            wux = wub[pp] if ich < NCH - 1 else wub_alt
            @block.gpsimd
            def _(g, ich=ich, pp=pp, wgx=wgx, wux=wux):
                if ich < NCH - 1:
                    wait_all(g, wgbuf_free[pp])
                else:
                    g.wait_ge(sDV.h, PH4_DVE_END)
                    g.wait_ge(sx1w.h, X1_SPILL_END)
                if ich < 2:
                    g.wait_ge(sPE.h, WO_PE_END)
                sgq.chain(g)
                isl = slice(ich * 512, (ich + 1) * 512)
                g.dma_start(out=wgx[:].rearrange("p (ct i) -> p ct i", ct=CT),
                            in_=wg8[:, isl].rearrange("(ct p) i -> p ct i", p=128)
                            ).then_inc(sgq.h, 16)
                g.dma_start(out=wux[:].rearrange("p (ct i) -> p ct i", ct=CT),
                            in_=wu8[:, isl].rearrange("(ct p) i -> p ct i", p=128)
                            ).then_inc(sgq.h, 16)
            wg_ld[ich] = sgq.tick(32)
            fb = fch[pp]
            for tc in range(2):
                for im in range(4):
                    bg = ffn_banks[bank_i[0] % 6]; bank_i[0] += 1
                    bu = ffn_banks[bank_i[0] % 6]; bank_i[0] += 1
                    it_g = ich * 4 + im
                    pe_cnt = []
                    @block.tensor
                    def _(t, ich=ich, tc=tc, im=im, bg=bg, bu=bu, pp=pp, pe_cnt=pe_cnt):
                        if ich == 0 and tc == 0 and im == 0:
                            t.wait_ge(sAC.h, H2T_DONE)
                        if tc == 0 and im == 0:
                            t.wait_ge(sgq.h, wg_ld[ich])
                        wgx = wgb[pp] if ich < NCH - 1 else wgb_alt
                        wux = wub[pp] if ich < NCH - 1 else wub_alt
                        bg.acquire(t)
                        for ct in range(CT):
                            mm = t.matmul(bg.t[:],
                                          wgx[:, ct * 512 + im * 128: ct * 512 + (im + 1) * 128],
                                          h2T[:, ct * T + tc * 512: ct * T + tc * 512 + 512],
                                          start=(ct == 0), stop=(ct == CT - 1))
                            if ct == CT - 1:
                                mm.then_inc(sPE.h, 1)
                        pe_cnt.append(sPE.tick(1))
                        bu.acquire(t)
                        for ct in range(CT):
                            mm = t.matmul(bu.t[:],
                                          wux[:, ct * 512 + im * 128: ct * 512 + (im + 1) * 128],
                                          h2T[:, ct * T + tc * 512: ct * T + tc * 512 + 512],
                                          start=(ct == 0), stop=(ct == CT - 1))
                            if ct == CT - 1:
                                mm.then_inc(sPE.h, 1)
                        pe_cnt.append(sPE.tick(1))
                    gpe, upe = pe_cnt
                    gi = gscr_i[0] % 2; gscr_i[0] += 1
                    gsc = g_scr[gi]
                    @block.scalar
                    def _(a_, it_g=it_g, gpe=gpe, gsc=gsc, bg=bg, gi=gi):
                        wait_all(a_, gscr_free[gi])
                        a_.wait_ge(sPE.h, gpe)
                        a_.activation(out=gsc[:], in_=bg.t[:], func=AF.Silu,
                                      scale=sgate_v[:, it_g:it_g + 1]).then_inc(sAC.h, 1)
                    bg.release(sAC, sAC.tick(1))
                    ac_g = sAC.n
                    @block.vector
                    def _(v, tc=tc, im=im, it_g=it_g, upe=upe, ac_g=ac_g,
                          gsc=gsc, bu=bu, fb=fb, pp=pp):
                        if tc == 0 and im == 0:
                            wait_all(v, fch_free[pp])
                        v.wait_ge(sPE.h, upe)
                        v.wait_ge(sAC.h, ac_g)
                        slot = im * 2 + tc
                        v.scalar_tensor_tensor(out=fb[:, slot * 512:(slot + 1) * 512],
                                               in0=bu.t[:], scalar=sup_v[:, it_g:it_g + 1],
                                               in1=gsc[:], op0=ALU.mult,
                                               op1=ALU.mult).then_inc(sDV.h, 1)
                    bu.release(sDV, sDV.tick(1))
                    gscr_free[gi] = [(sDV, sDV.n)]
            dv_f = sDV.n
            @block.scalar
            def _(a_, ich=ich, pp=pp, dv_f=dv_f, fb=fb):
                a_.wait_ge(sDV.h, dv_f)
                dst = fuse_hbm[ich * 512:(ich + 1) * 512, :].rearrange(
                    "(im p) (tc t) -> p im tc t", p=128, t=512)
                sfw.chain(a_)
                a_.dma_start(out=dst, in_=fb[:].rearrange("p (im tc t) -> p im tc t",
                                                          im=4, tc=2)).then_inc(sfw.h, 16)
            fuse_wr[ich] = sfw.tick(16)
            fch_free[pp] = [(sfw, fuse_wr[ich])]
            wgbuf_free[pp] = [(sPE, sPE.n)]
            if ich == NCH - 3:
                GU13_PE = sPE.n
            if ich == NCH - 2:
                GU14_PE = sPE.n
        GU_PE_END = sPE.n
        FUSE_ALL = sfw.n
        FUSE_DVE_END = sDV.n

        dn_banks = [pb[0], pb[1], pb[2], pb[3]]
        wdb_free = [[], []]
        ftt_free = [[(sPE, GU_PE_END)], [(sPE, GU_PE_END)]]
        x1rr_free = [[(sDV, PH4_DVE_END)], [(sDV, PH4_DVE_END)]]
        ost_free = [[(sDV, PH4_DVE_END)], [(sDV, PH4_DVE_END)]]
        wd_ld = [0] * 4
        for cs in range(4):
            @block.gpsimd
            def _(g, cs=cs):
                pp = cs % 2
                wait_all(g, wdb_free[pp])
                if cs == 1:
                    g.wait_ge(sPE.h, GU_PE_END)
                    g.wait_ge(sfw.h, FUSE_ALL)
                    g.wait_ge(sDV.h, FUSE_DVE_END)
                sgq.chain(g)
                order = (1, 3, 0, 2) if cs == 0 else range(4)
                for part in order:
                    if cs == 0 and part == 1:
                        g.wait_ge(sPE.h, GU13_PE)
                    if cs == 0 and part == 0:
                        g.wait_ge(sPE.h, GU14_PE)
                    rsl = slice(part * 2048, (part + 1) * 2048)
                    g.dma_start(
                        out=wdb[pp][:].rearrange("p (it c) -> p it c", it=ITT)[:, part * 16:(part + 1) * 16, :],
                        in_=wd8[rsl, cs * 512:(cs + 1) * 512].rearrange("(it p) c -> p it c", p=128)
                    ).then_inc(sgq.h, 16)
            wd_ld[cs] = sgq.tick(64)
            for tt in range(8):
                k_i = cs * 8 + tt
                pp = k_i % 2
                fsem = sftp[pp]
                ftt_base = fsem.n
                @block.sync
                def _(s, cs=cs, tt=tt, pp=pp, k_i=k_i, fsem=fsem, ftt_base=ftt_base):
                    wait_all(s, ftt_free[pp])
                    if k_i == 0:
                        s.wait_ge(sPE.h, GU_PE_END)
                    for part in range(4):
                        if k_i == 0:
                            s.wait_ge(sfw.h, fuse_wr[4 * part + 3])
                            if part:
                                s.wait_ge(fsem.h, ftt_base + 16 * part)
                        rsl = slice(part * 2048, (part + 1) * 2048)
                        s.dma_start(
                            out=fuse_tt[pp][:].rearrange("p (it t) -> p it t", it=ITT)[:, part * 16:(part + 1) * 16, :],
                            in_=fuse_hbm[rsl, tt * 128:(tt + 1) * 128].rearrange("(it p) t -> p it t", p=128)
                        ).then_inc(fsem.h, 16)
                ftt_ld = fsem.tick(64)
                @block.sync
                def _(s, cs=cs, tt=tt, pp=pp, k_i=k_i):
                    wait_all(s, x1rr_free[pp])
                    if k_i == 0:
                        s.wait_ge(sx1w.h, X1_SPILL_END)
                        s.wait_ge(sDV.h, PH4_DVE_END)
                        s.wait_ge(sPE.h, GU_PE_END)
                    sx.chain(s)
                    s.dma_start(out=x1rr[pp][:],
                                in_=x1_hbm[tt * 128:(tt + 1) * 128, cs * 512:(cs + 1) * 512]
                                ).then_inc(sx.h, 16)
                x1_ld = sx.tick(16)
                bk = dn_banks[tt % 4]
                @block.tensor
                def _(t, cs=cs, pp=pp, bk=bk, k_i=k_i, ftt_base=ftt_base, fsem=fsem):
                    t.wait_ge(sgq.h, wd_ld[cs])
                    if k_i != 0:
                        t.wait_ge(fsem.h, ftt_base + 64)
                    bk.acquire(t)
                    for it in range(ITT):
                        if k_i == 0 and it % 16 == 0:
                            t.wait_ge(fsem.h, ftt_base + 16 * (it // 16 + 1))
                        mm = t.matmul(bk.t[:], fuse_tt[pp][:, it * 128:(it + 1) * 128],
                                      wdb[cs % 2][:, it * 512:(it + 1) * 512],
                                      start=(it == 0), stop=(it == ITT - 1))
                        if it == ITT - 1:
                            mm.then_inc(sPE.h, 1)
                pev = sPE.tick(1)
                ftt_free[pp] = [(sPE, pev)]
                ob = o_st[pp]
                @block.vector
                def _(v, cs=cs, pp=pp, bk=bk, pev=pev, ob=ob, x1_ld=x1_ld, k_i=k_i):
                    wait_all(v, ost_free[pp])
                    if k_i == 0:
                        v.wait_ge(sx1w.h, X1_SPILL_END)
                    v.wait_ge(sx.h, x1_ld)
                    v.wait_ge(sPE.h, pev)
                    v.tensor_tensor(out=ob[:], in0=bk.t[:],
                                    in1=sdown_v[:, cs * 512:(cs + 1) * 512],
                                    op=ALU.mult).then_inc(sDV.h, 1)
                    t1 = sDV.tick(1)
                    v.wait_ge(sDV.h, t1)
                    v.tensor_tensor(out=ob[:], in0=ob[:], in1=x1rr[pp][:], op=ALU.add).then_inc(sDV.h, 1)
                bk.release(sDV, sDV.tick(1))
                x1rr_free[pp] = [(sDV, sDV.n)]
                dv_o = sDV.n
                if tt == 7:
                    wdb_free[cs % 2] = [(sPE, pev), (sDV, dv_o)]
                @block.scalar
                def _(a_, cs=cs, tt=tt, dv_o=dv_o, ob=ob):
                    a_.wait_ge(sDV.h, dv_o)
                    sout.chain(a_)
                    a_.dma_start(out=out[tt * 128:(tt + 1) * 128, cs * 512:(cs + 1) * 512],
                                 in_=ob[:]).then_inc(sout.h, 16)
                ost_free[pp] = [(sout, sout.tick(16))]

        @block.scalar
        def _(a_):
            a_.wait_ge(sout.h, sout.n)

    return nc


def make_in_maps(L0_attn_x, L0_attn_mu_r, L0_attn_mu_k, L0_attn_mu_v, L0_attn_wkv_w,
                 L0_attn_wkv_u, L0_attn_Wr, L0_attn_Wk, L0_attn_Wv, L0_attn_Wo,
                 L0_ffn_w_gate, L0_ffn_w_up, L0_ffn_w_down, L0_ffn_s_gate, L0_ffn_s_up,
                 L0_ffn_s_down):
    x = np.ascontiguousarray(np.asarray(L0_attn_x, dtype=np.float32).reshape(NCORES * T, C))

    def pt(v, nt):
        return np.ascontiguousarray(np.asarray(v, dtype=np.float32).reshape(nt, 128).T)

    common = dict(
        Wr=np.ascontiguousarray(L0_attn_Wr, dtype=np.float32),
        Wk=np.ascontiguousarray(L0_attn_Wk, dtype=np.float32),
        Wv=np.ascontiguousarray(L0_attn_Wv, dtype=np.float32),
        Wo=np.ascontiguousarray(L0_attn_Wo, dtype=np.float32),
        wg8=np.ascontiguousarray(L0_ffn_w_gate, dtype=np.int8),
        wu8=np.ascontiguousarray(L0_ffn_w_up, dtype=np.int8),
        wd8=np.ascontiguousarray(L0_ffn_w_down, dtype=np.int8),
        mu_r_pt=pt(L0_attn_mu_r, CT),
        mu_k_pt=pt(L0_attn_mu_k, CT),
        mu_v_pt=pt(L0_attn_mu_v, CT),
        w_pt=pt(L0_attn_wkv_w, HT),
        u_pt=pt(L0_attn_wkv_u, HT),
        sgate_pt=pt(L0_ffn_s_gate, ITT),
        sup_pt=pt(L0_ffn_s_up, ITT),
        sdown_b=np.ascontiguousarray(
            np.broadcast_to(np.asarray(L0_ffn_s_down, dtype=np.float32), (128, C))),
        ident_h=np.eye(128, dtype=np.float32),
        iota_rev_h=np.ascontiguousarray(
            np.broadcast_to(np.arange(T - 1, -1, -1, dtype=np.float32), (128, T))),
        ones_h=np.ones((128, 512), dtype=np.float32),
    )

    in_maps = []
    for c in range(NCORES):
        if c == 0:
            seg = np.concatenate([np.zeros((1, C), np.float32), x[0:T]], axis=0)
        else:
            seg = x[c * T - 1:(c + 1) * T]
        segc = np.empty(NCORES, np.float32)
        for r in range(NCORES):
            segc[r] = 1024.0 * (c - 1 - r) if r < c else 4.0e7
        m = dict(common)
        m["x_seg"] = np.ascontiguousarray(seg)
        m["segc_h"] = np.ascontiguousarray(np.broadcast_to(segc, (128, NCORES)))
        in_maps.append(m)
    return in_maps


def kernel(**inputs):
    in_maps = make_in_maps(**inputs)
    nc = build_program()
    res = run_bass_kernel_spmd(nc, in_maps, core_ids=list(range(NCORES)))
    full = np.concatenate([np.asarray(res.results[c]["out"]) for c in range(NCORES)], axis=0)
    return full.reshape(1, NCORES * T, C).astype(np.float32)


# version 25
# speedup vs baseline: 1.0284x; 1.0072x over previous
import numpy as np
from contextlib import ExitStack

import concourse.bass as bass
import concourse.mybir as mybir
from concourse.bass_utils import run_bass_kernel_spmd

F32 = mybir.dt.float32
BF16 = mybir.dt.bfloat16
I8 = mybir.dt.int8
U8 = mybir.dt.uint8
AF = mybir.ActivationFunctionType
ALU = mybir.AluOpType

NCORES = 8
T = 1024
C = 2048
H = 2048
FF = 8192
CT = C // 128
HT = H // 128
ITT = FF // 128
EPS = 1e-5
KB = 1024


class Sem:
    def __init__(self, nc, ctx, name):
        self.h = ctx.enter_context(nc.semaphore(name))
        self.n = 0

    def tick(self, k):
        self.n += k
        return self.n

    def chain(self, eng):
        if self.n:
            eng.wait_ge(self.h, self.n)


def wait_all(eng, guards):
    for g in guards:
        if g is not None:
            s, v = g
            if v > 0:
                eng.wait_ge(s.h, v)


class Bank:
    def __init__(self, t):
        self.t = t
        self.guard = None

    def acquire(self, eng):
        wait_all(eng, [self.guard])

    def release(self, sem, val):
        self.guard = (sem, val)


def build_program():
    nc = bass.Bass("TRN2", target_bir_lowering=False, debug=False)

    x_seg = nc.dram_tensor("x_seg", [T + 1, C], F32, kind="ExternalInput").ap()
    Wmats = {n: nc.dram_tensor(n2, [2048, 2048], F32, kind="ExternalInput").ap()
             for n, n2 in (("r", "Wr"), ("k", "Wk"), ("v", "Wv"), ("o", "Wo"))}
    wg8 = nc.dram_tensor("wg8", [C, FF], I8, kind="ExternalInput").ap()
    wu8 = nc.dram_tensor("wu8", [C, FF], I8, kind="ExternalInput").ap()
    wd8 = nc.dram_tensor("wd8", [FF, C], I8, kind="ExternalInput").ap()
    mu_r_pt = nc.dram_tensor("mu_r_pt", [128, CT], F32, kind="ExternalInput").ap()
    mu_k_pt = nc.dram_tensor("mu_k_pt", [128, CT], F32, kind="ExternalInput").ap()
    mu_v_pt = nc.dram_tensor("mu_v_pt", [128, CT], F32, kind="ExternalInput").ap()
    w_pt = nc.dram_tensor("w_pt", [128, HT], F32, kind="ExternalInput").ap()
    u_pt = nc.dram_tensor("u_pt", [128, HT], F32, kind="ExternalInput").ap()
    sgate_pt = nc.dram_tensor("sgate_pt", [128, ITT], F32, kind="ExternalInput").ap()
    sup_pt = nc.dram_tensor("sup_pt", [128, ITT], F32, kind="ExternalInput").ap()
    sdown_b = nc.dram_tensor("sdown_b", [128, C], F32, kind="ExternalInput").ap()
    ident_h = nc.dram_tensor("ident_h", [128, 128], F32, kind="ExternalInput").ap()
    iota_rev_h = nc.dram_tensor("iota_rev_h", [128, T], F32, kind="ExternalInput").ap()
    ones_h = nc.dram_tensor("ones_h", [128, 512], F32, kind="ExternalInput").ap()
    segc_h = nc.dram_tensor("segc_h", [128, NCORES], F32, kind="ExternalInput").ap()
    out = nc.dram_tensor("out", [T, C], F32, kind="ExternalOutput").ap()

    wbf = {n: nc.dram_tensor(f"wbf_{n}", [2048, 2048], BF16, kind="Internal").ap()
           for n in ("r", "k", "v", "o")}
    spill_q = nc.dram_tensor("spill_q", [H, T], F32, kind="Internal").ap()
    spill_p = nc.dram_tensor("spill_p", [H, T], F32, kind="Internal").ap()
    spill_s = nc.dram_tensor("spill_s", [H, T], F32, kind="Internal").ap()
    fuse_hbm = nc.dram_tensor("fuse_hbm", [FF, T], BF16, kind="Internal").ap()
    x1_hbm = nc.dram_tensor("x1_hbm", [T, C], F32, kind="Internal").ap()
    cc_in = nc.dram_tensor("cc_in", [128, 2 * HT], F32, kind="Internal")
    cc_out = nc.dram_tensor("cc_out", [NCORES * 128, 2 * HT], F32, kind="Internal")

    ctx = ExitStack()
    with ctx:
        ARENA_BYTES = 196224
        backing = ctx.enter_context(nc.sbuf_tensor("arena", [128, ARENA_BYTES], U8))
        base = nc.lookup_mloc(backing).addr

        def alloc(name, shape, dtype, off):
            assert off % 64 == 0, name
            nbytes = int(np.prod(shape[1:])) * mybir.dt.size(dtype)
            assert off + nbytes <= ARENA_BYTES, (name, off, nbytes)
            h = nc.virtual_sbuf_tensor(name, shape, dtype)
            nc._tensor(h.name, shape, dtype, type="SB")
            mloc = nc.lookup_mloc(h)
            mloc.allocated = True
            mloc.addr = base + off
            return h

        ident = alloc("ident", [128, 128], BF16, 0)
        ones = alloc("ones", [128, 512], F32, 320)
        iota_rev = alloc("iota_rev", [128, T], F32, 2368)
        vb = [6464]
        def vec(name):
            h = alloc(name, [128, 16], F32, vb[0])
            vb[0] += 64
            return h
        mu = {"r": vec("mu_r"), "k": vec("mu_k"), "v": vec("mu_v")}
        wraw = vec("wraw"); uraw = vec("uraw")
        w_eff = vec("w_eff"); d1 = vec("d1"); eu = vec("eu")
        AinV = vec("AinV"); BinV = vec("BinV")
        Acar = vec("Acar"); Bcar = vec("Bcar")
        tmp16a = vec("tmp16a"); tmp16b = vec("tmp16b"); eps_t = vec("eps_t"); e_t = vec("e_t")
        summ = alloc("summ", [128, 2 * HT], F32, vb[0]); vb[0] += 128
        segc = alloc("segc", [128, 8], F32, vb[0]); vb[0] += 64
        stat = alloc("stat", [128, 16], F32, vb[0]); vb[0] += 64
        sgate_v = alloc("sgate_v", [128, ITT], F32, 8000)
        sup_v = alloc("sup_v", [128, ITT], F32, 8256)
        sdown_v = alloc("sdown_v", [128, C], F32, 8512)
        gath = alloc("gath", [128, NCORES * 2 * HT], F32, 16704)
        coef_all = alloc("coef_all", [128, 128], F32, 17792)
        tmp3 = alloc("tmp3", [128, 128], F32, 18368)
        e128 = alloc("e128", [128, 128], F32, 18944)

        XB = 20 * KB
        mixT = alloc("mixT", [128, CT * T], BF16, XB)
        trash2 = alloc("trash2", [128, C], BF16, XB)
        h2T = alloc("h2T", [128, CT * T], BF16, XB)
        fuse_tt = [alloc("fuse_tt0", [128, ITT * 128], BF16, XB),
                   alloc("fuse_tt1", [128, ITT * 128], BF16, XB + ITT * 128 * 2)]

        BB = 53 * KB
        hT = alloc("hT", [128, CT * (T + 1)], BF16, BB)
        WT = alloc("WT", [128, CT * 2048], BF16, BB + 33 * KB)
        SCR = BB + 99 * KB
        x_t = [alloc(f"x_t{i}", [128, C], F32, SCR + i * 8 * KB) for i in range(2)] + [
            alloc("x_t2", [128, C], F32, SCR + 20 * KB),
            alloc("x_t3", [128, C], F32, SCR + 28 * KB)]
        h_bf = alloc("h_bf", [128, C], BF16, SCR + 16 * KB)
        delta = alloc("delta", [128, T], F32, SCR)
        q_t = [alloc(f"q_t{i}", [128, T], F32, SCR + 4 * KB + i * 4 * KB) for i in range(2)]
        qrr = [alloc(f"qrr{i}", [128, T], F32, SCR + 12 * KB + i * 4 * KB) for i in range(2)]
        p_t = [alloc(f"p_t{i}", [128, T], F32, SCR + 20 * KB + i * 4 * KB) for i in range(2)]
        dpow = [alloc(f"dpow{i}", [128, T], F32, SCR + 28 * KB + i * 4 * KB) for i in range(2)]
        trash_bf = alloc("trash_bf", [128, T], BF16, SCR + 36 * KB)
        s_buf = alloc("s_buf", [128, HT * T], BF16, BB)
        WoT = alloc("WoT", [128, HT * 2048], BF16, BB + 33 * KB)
        rrq = [alloc(f"rrq{i}", [128, 512], F32, SCR + 12 * KB + i * 2 * KB) for i in range(2)]
        rrp = [alloc(f"rrp{i}", [128, 512], F32, SCR + 16 * KB + i * 2 * KB) for i in range(2)]
        rrs = [alloc(f"rrs{i}", [128, 512], F32, SCR + 20 * KB + i * 2 * KB) for i in range(2)]
        Abuf = alloc("Abuf", [128, 513], F32, SCR + 24 * KB)
        Bbuf = alloc("Bbuf", [128, 513], F32, SCR + 26 * KB + 64)
        nbuf = alloc("nbuf", [128, 512], F32, SCR + 28 * KB + 128)
        dbuf = alloc("dbuf", [128, 512], F32, SCR + 30 * KB + 128)
        rec = alloc("rec", [128, 512], F32, SCR)
        d_bc = alloc("d_bc", [128, 512], F32, SCR + 32 * KB + 192)
        x1_t = [alloc("x1_t0", [128, C], F32, SCR + 2 * KB)]
        h2_bf = alloc("h2_bf", [128, C], BF16, SCR + 34 * KB + 256)
        sig_t = alloc("sig_t", [128, T], F32, SCR + 34 * KB + 256)
        x1rr = [alloc(f"x1rr{i}", [128, 512], F32, SCR + 30 * KB + i * 2 * KB) for i in range(2)]
        o_st = [alloc(f"o_st{i}", [128, 512], F32, SCR + 34 * KB + i * 2 * KB) for i in range(2)]
        wgb = [alloc(f"wgb{i}", [128, CT * 512], BF16, BB + i * 16 * KB) for i in range(2)]
        wub = [alloc(f"wub{i}", [128, CT * 512], BF16, BB + 32 * KB + i * 16 * KB) for i in range(2)]
        wgb_alt = alloc("wgba", [128, CT * 512], BF16, SCR)
        wub_alt = alloc("wuba", [128, CT * 512], BF16, SCR + 16 * KB)
        fch = [alloc(f"fch{i}", [128, 8 * 512], BF16, BB + 64 * KB + i * 8 * KB) for i in range(2)]
        g_scr = [alloc(f"g_scr{i}", [128, 512], F32, BB + 80 * KB + i * 2 * KB) for i in range(2)]
        wdb = [alloc(f"wdb{i}", [128, ITT * 512], BF16, BB + i * 64 * KB) for i in range(2)]

        pb = [Bank(ctx.enter_context(nc.psum_tensor(f"pb{i}", [128, 512], F32)))
              for i in range(7)]
        ptr = Bank(ctx.enter_context(nc.psum_tensor("ptr", [128, 1024], BF16)))

        sx = Sem(nc, ctx, "sx")
        swt = Sem(nc, ctx, "swt")
        srr = Sem(nc, ctx, "srr")
        saq = Sem(nc, ctx, "saq")
        sgq = Sem(nc, ctx, "sgq")
        scc = Sem(nc, ctx, "scc")
        sPE = Sem(nc, ctx, "sPE")
        sDV = Sem(nc, ctx, "sDV")
        sAC = Sem(nc, ctx, "sAC")
        sout = Sem(nc, ctx, "sout")
        sspq = Sem(nc, ctx, "sspq")
        sspp = Sem(nc, ctx, "sspp")
        ssps = Sem(nc, ctx, "ssps")
        sx1w = Sem(nc, ctx, "sx1w")
        sfw = Sem(nc, ctx, "sfw")
        srrp = [Sem(nc, ctx, "srrp0"), Sem(nc, ctx, "srrp1")]
        srrs = [Sem(nc, ctx, "srrs0"), Sem(nc, ctx, "srrs1")]
        sPL = Sem(nc, ctx, "sPL")
        sftp = [Sem(nc, ctx, "sftp0"), Sem(nc, ctx, "sftp1")]

        block = ctx.enter_context(nc.Block())

        @block.gpsimd
        def _(g):
            g.dma_start(out=ident[:], in_=ident_h[:, :]).then_inc(sgq.h, 16)
            g.dma_start(out=ones[:], in_=ones_h[:, :]).then_inc(sgq.h, 16)
            g.dma_start(out=iota_rev[:], in_=iota_rev_h[:, :]).then_inc(sgq.h, 16)
            g.dma_start(out=mu["r"][:], in_=mu_r_pt[:, :]).then_inc(sgq.h, 16)
            g.dma_start(out=mu["k"][:], in_=mu_k_pt[:, :]).then_inc(sgq.h, 16)
            g.dma_start(out=mu["v"][:], in_=mu_v_pt[:, :]).then_inc(sgq.h, 16)
            g.dma_start(out=wraw[:], in_=w_pt[:, :]).then_inc(sgq.h, 16)
            g.dma_start(out=uraw[:], in_=u_pt[:, :]).then_inc(sgq.h, 16)
            g.dma_start(out=segc[:], in_=segc_h[:, :]).then_inc(sgq.h, 16)
            g.dma_start(out=sgate_v[:], in_=sgate_pt[:, :]).then_inc(sgq.h, 16)
            g.dma_start(out=sup_v[:], in_=sup_pt[:, :]).then_inc(sgq.h, 16)
            g.dma_start(out=sdown_v[:], in_=sdown_b[:, :]).then_inc(sgq.h, 16)
        SMALL_LD = sgq.tick(12 * 16)

        cast_done = {}
        cast_ct = {}
        def emit_cast(n, pe_gate=0, chunks=1):
            base = sgq.n
            @block.gpsimd
            def _(g):
                if pe_gate:
                    g.wait_ge(sPE.h, pe_gate)
                sgq.chain(g)
                if chunks == 1:
                    g.dma_start(out=wbf[n][:, :], in_=Wmats[n][:, :]).then_inc(sgq.h, 16)
                else:
                    cw = 2048 // chunks
                    for ci in range(chunks):
                        if ci:
                            g.wait_ge(sgq.h, base + 16 * ci)
                        g.dma_start(out=wbf[n][:, ci * cw:(ci + 1) * cw],
                                    in_=Wmats[n][:, ci * cw:(ci + 1) * cw]).then_inc(sgq.h, 16)
            if chunks > 1:
                cast_ct[n] = [base + 16 * (i + 1) for i in range(chunks)]
            cast_done[n] = sgq.tick(16 * chunks)
        emit_cast("k", chunks=8)

        @block.scalar
        def _(a):
            a.wait_ge(sgq.h, SMALL_LD)
            a.activation(out=tmp16a[:], in_=wraw[:], func=AF.Exp).then_inc(sAC.h, 1)
            a.activation(out=eu[:], in_=uraw[:], func=AF.Exp).then_inc(sAC.h, 1)
        AC_EXPW = sAC.tick(2)
        @block.vector
        def _(v):
            v.memset(e128[:], 2.718281828459045).then_inc(sDV.h, 1)
            v.memset(eps_t[:, 0:1], EPS).then_inc(sDV.h, 1)
            v.memset(e_t[:, 0:1], -1.0).then_inc(sDV.h, 1)
            v.wait_ge(sAC.h, AC_EXPW)
            v.tensor_scalar(out=w_eff[:], in0=tmp16a[:], scalar1=-1.0, scalar2=None,
                            op0=ALU.mult).then_inc(sDV.h, 1)
        sDV.tick(3)
        DV_WEFF = sDV.tick(1)
        @block.scalar
        def _(a):
            a.wait_ge(sDV.h, DV_WEFF)
            a.activation(out=d1[:], in_=w_eff[:], func=AF.Exp).then_inc(sAC.h, 1)
        AC_D1 = sAC.tick(1)
        @block.vector
        def _(v):
            v.wait_ge(sDV.h, DV_WEFF)
            for r in range(NCORES):
                v.tensor_scalar(out=coef_all[:, r * 16:(r + 1) * 16], in0=w_eff[:],
                                scalar1=segc[:, r:r + 1], scalar2=None,
                                op0=ALU.mult).then_inc(sDV.h, 1)
        DV_COEF_T = sDV.tick(NCORES)
        @block.gpsimd
        def _(g):
            g.wait_ge(sDV.h, DV_COEF_T)
            g.tensor_tensor(out=coef_all[:], in0=e128[:], in1=coef_all[:],
                            op=ALU.pow).then_inc(sPL.h, 1)
        PL_COEF = sPL.tick(1)

        hT_r = hT[:].rearrange("p (ct w) -> p ct w", ct=CT)
        ln1_dve_done = [0] * 9
        ln1_pe_last = [0] * 9
        x_loaded = [0] * 9
        def ln1_partA(a):
            rows = 128 if a < 8 else 1
            buf = x_t[a % 4]
            @block.sync
            def _(s, a=a, buf=buf, rows=rows):
                if a >= 4:
                    s.wait_ge(sDV.h, ln1_dve_done[a - 4])
                sx.chain(s)
                s.dma_start(out=buf[0:rows, :],
                            in_=x_seg[a * 128:a * 128 + rows, :]).then_inc(sx.h, 16)
            x_loaded[a] = sx.tick(16)
            so = (a % 2) * 8
            xa = buf[0:rows, :]
            @block.vector
            def _(v, a=a, xa=xa, rows=rows, so=so):
                r = slice(0, rows)
                v.wait_ge(sx.h, x_loaded[a])
                if a >= 2:
                    v.wait_ge(sDV.h, ln1_dve_done[a - 2])
                if a >= 1:
                    v.wait_ge(sDV.h, ln1_stt[a - 1])
                v.tensor_reduce(out=stat[r, so:so + 1], in_=xa,
                                axis=mybir.AxisListType.X, op=ALU.add).then_inc(sDV.h, 1)
                v.scalar_tensor_tensor(out=trash2[0:rows, :], in0=xa, scalar=1.0, in1=xa,
                                       op0=ALU.mult, op1=ALU.mult,
                                       accum_out=stat[r, so + 1:so + 2]).then_inc(sDV.h, 1)
                both = sDV.tick(2)
                ln1_stt[a] = both
                v.wait_ge(sDV.h, both)
                v.tensor_scalar(out=stat[r, so + 2:so + 3], in0=stat[r, so:so + 1],
                                scalar1=1.0 / C, scalar2=None, op0=ALU.mult).then_inc(sDV.h, 1)
                v.tensor_scalar(out=stat[r, so + 3:so + 4], in0=stat[r, so + 1:so + 2],
                                scalar1=1.0 / C, scalar2=None, op0=ALU.mult).then_inc(sDV.h, 1)
                m2 = sDV.tick(2)
                v.wait_ge(sDV.h, m2)
                v.tensor_tensor(out=stat[r, so + 4:so + 5], in0=stat[r, so + 2:so + 3],
                                in1=stat[r, so + 2:so + 3], op=ALU.mult).then_inc(sDV.h, 1)
                me = sDV.tick(1)
                v.wait_ge(sDV.h, me)
                v.tensor_tensor(out=stat[r, so + 5:so + 6], in0=stat[r, so + 3:so + 4],
                                in1=stat[r, so + 4:so + 5], op=ALU.subtract).then_inc(sDV.h, 1)
                sDV.tick(1)
            ln1_var[a] = sDV.n
            @block.scalar
            def _(a_, rows=rows, so=so, a=a):
                a_.wait_ge(sDV.h, ln1_var[a])
                a_.activation(out=stat[0:rows, so + 6:so + 7], in_=stat[0:rows, so + 5:so + 6],
                              func=AF.Sqrt, bias=eps_t[0:rows, 0:1]).then_inc(sAC.h, 1)
            ln1_sqrt[a] = sAC.tick(1)

        def ln1_partB(a):
            rows = 128 if a < 8 else 1
            buf = x_t[a % 4]
            so = (a % 2) * 8
            xa = buf[0:rows, :]
            @block.vector
            def _(v, a=a, xa=xa, rows=rows, so=so):
                r = slice(0, rows)
                v.wait_ge(sAC.h, ln1_sqrt[a])
                v.reciprocal(out=stat[r, so + 7:so + 8], in_=stat[r, so + 6:so + 7]).then_inc(sDV.h, 1)
                rst = sDV.tick(1)
                v.wait_ge(sDV.h, rst)
                if a >= 1:
                    v.wait_ge(sPE.h, ln1_pe_last[a - 1])
                v.tensor_scalar(out=h_bf[0:rows, :], in0=xa, scalar1=stat[r, so + 2:so + 3],
                                scalar2=stat[r, so + 7:so + 8], op0=ALU.subtract,
                                op1=ALU.mult).then_inc(sDV.h, 1)
                sDV.tick(1)
            dv_h = sDV.n
            if a < 8:
                for rnd in range(2):
                    @block.tensor
                    def _(t, a=a, rnd=rnd, dv_h=dv_h):
                        t.wait_ge(sDV.h, dv_h)
                        if a == 0 and rnd == 0:
                            t.wait_ge(sgq.h, SMALL_LD)
                        ptr.acquire(t)
                        for b in range(8):
                            ct = rnd * 8 + b
                            mm = t.transpose(ptr.t[:, b * 128:(b + 1) * 128],
                                             h_bf[:, ct * 128:(ct + 1) * 128], ident[:])
                            if b == 7:
                                mm.then_inc(sPE.h, 1)
                    pe_tr = sPE.tick(1)
                    @block.scalar
                    def _(a_, a=a, rnd=rnd, pe_tr=pe_tr):
                        a_.wait_ge(sPE.h, pe_tr)
                        a_.copy(out=hT_r[:, rnd * 8:(rnd + 1) * 8, a * 128 + 0:a * 128 + 128],
                                in_=ptr.t[:].rearrange("p (b w) -> p b w", b=8)).then_inc(sAC.h, 1)
                    ptr.release(sAC, sAC.tick(1))
                    ln1_pe_last[a] = sPE.n
            else:
                @block.tensor
                def _(t, dv_h=dv_h):
                    t.wait_ge(sDV.h, dv_h)
                    ptr.acquire(t)
                    for ct2 in range(16):
                        mm = t.transpose(ptr.t[:, 2 * ct2:2 * ct2 + 1],
                                         h_bf[0:1, ct2 * 128:(ct2 + 1) * 128], ident[0:1, 0:1])
                        if ct2 == 15:
                            mm.then_inc(sPE.h, 1)
                pe_tr = sPE.tick(1)
                @block.scalar
                def _(a_, pe_tr=pe_tr):
                    a_.wait_ge(sPE.h, pe_tr)
                    a_.copy(out=hT_r[:, :, 1024:1025],
                            in_=ptr.t[:].rearrange("p (b w) -> p b w", w=2)[:, 0:16, 0:1]
                            ).then_inc(sAC.h, 1)
                ptr.release(sAC, sAC.tick(1))
                ln1_pe_last[a] = sPE.n
            ln1_dve_done[a] = sDV.n

        ln1_var = [0] * 9
        ln1_sqrt = [0] * 9
        ln1_stt = [0] * 9
        for a in range(9):
            ln1_partA(a)
            if a >= 1:
                ln1_partB(a - 1)
        ln1_partB(8)
        LN1_DVE_END = sDV.n
        LN1_AC_END = sAC.n

        def emit_mixes(which, prev_pe_end):
            muv = mu[which]
            for ct in range(CT):
                @block.vector
                def _(v, ct=ct):
                    if ct == 0:
                        v.wait_ge(sDV.h, LN1_DVE_END)
                        v.wait_ge(sAC.h, LN1_AC_END)
                        if prev_pe_end:
                            v.wait_ge(sPE.h, prev_pe_end)
                    b0 = ct * (T + 1)
                    v.wait_ge(sDV.h, sDV.n)
                    v.tensor_tensor(out=delta[:], in0=hT[:, b0 + 1:b0 + T + 1],
                                    in1=hT[:, b0:b0 + T], op=ALU.subtract).then_inc(sDV.h, 1)
                    dd = sDV.tick(1)
                    v.wait_ge(sDV.h, dd)
                    v.scalar_tensor_tensor(out=mixT[:, ct * T:(ct + 1) * T], in0=delta[:],
                                           scalar=muv[:, ct:ct + 1], in1=hT[:, b0:b0 + T],
                                           op0=ALU.mult, op1=ALU.add).then_inc(sDV.h, 1)
                    sDV.tick(1)
            return sDV.n

        def emit_wt_loads(name, ready_after_pe):
            gates = cast_ct.get(name)
            @block.sync
            def _(s):
                if gates is None:
                    s.wait_ge(sgq.h, cast_done[name])
                if ready_after_pe:
                    s.wait_ge(sPE.h, ready_after_pe)
                swt.chain(s)
                for ct in range(CT):
                    if gates is not None:
                        s.wait_ge(sgq.h, gates[ct * len(gates) // CT])
                    s.dma_start(out=WT[:, ct * 2048:(ct + 1) * 2048],
                                in_=wbf[name][0:2048, ct * 128:(ct + 1) * 128],
                                transpose=True).then_inc(swt.h, 16)
            return swt.tick(16 * 16)

        mm_banks = [pb[0], pb[1], pb[2], pb[3]]

        def emit_proj_group(ht, mix_ready, wt_gates, first):
            res = []
            @block.tensor
            def _(t, ht=ht):
                if first:
                    t.wait_ge(sDV.h, mix_ready)
                if ht in wt_gates:
                    t.wait_ge(swt.h, wt_gates[ht])
                for tc in range(2):
                    bk = mm_banks[(ht * 2 + tc) % 4]
                    bk.acquire(t)
                    for ct in range(CT):
                        mm = t.matmul(bk.t[:],
                                      WT[:, ct * 2048 + ht * 128: ct * 2048 + (ht + 1) * 128],
                                      mixT[:, ct * T + tc * 512: ct * T + tc * 512 + 512],
                                      start=(ct == 0), stop=(ct == CT - 1))
                        if ct == CT - 1:
                            mm.then_inc(sPE.h, 1)
                    res.append((bk, sPE.tick(1)))
            return res

        mixk_done = emit_mixes("k", 0)
        wtk_done = emit_wt_loads("k", 0)
        KG_FIRST = [0]
        q_spill_done = [0] * HT
        qbuf_free = [[], []]
        dpow_free = [[], []]
        for ht in range(HT):
            grp = emit_proj_group(ht, mixk_done, {0: wtk_done}, ht == 0)
            if ht == 0:
                KG_FIRST[0] = sPE.n
                emit_cast("v", pe_gate=KG_FIRST[0], chunks=4)
                emit_cast("r", pe_gate=KG_FIRST[0], chunks=4)
            qb = q_t[ht % 2]
            dp = dpow[ht % 2]
            @block.scalar
            def _(a_, ht=ht, qb=qb, grp=grp):
                wait_all(a_, qbuf_free[ht % 2])
                a_.wait_ge(sPE.h, grp[0][1])
                a_.activation(out=qb[:, 0:512], in_=grp[0][0].t[:], func=AF.Exp).then_inc(sAC.h, 1)
                grp[0][0].release(sAC, sAC.tick(1))
                a_.wait_ge(sPE.h, grp[1][1])
                a_.activation(out=qb[:, 512:1024], in_=grp[1][0].t[:], func=AF.Exp).then_inc(sAC.h, 1)
                grp[1][0].release(sAC, sAC.tick(1))
            ac_q = sAC.n
            @block.sync
            def _(s, ht=ht, qb=qb, ac_q=ac_q):
                s.wait_ge(sAC.h, ac_q)
                sspq.chain(s)
                s.dma_start(out=spill_q[ht * 128:(ht + 1) * 128, :], in_=qb[:]).then_inc(sspq.h, 16)
            q_spill_done[ht] = sspq.tick(16)
            @block.scalar
            def _(a_, ht=ht, dp=dp):
                wait_all(a_, dpow_free[ht % 2])
                if ht < 2:
                    a_.wait_ge(sAC.h, AC_D1)
                    a_.wait_ge(sDV.h, LN1_DVE_END)
                a_.activation(out=dp[:], in_=iota_rev[:], func=AF.Exp,
                              scale=w_eff[:, ht:ht + 1]).then_inc(sAC.h, 1)
            ac_dp = sAC.tick(1)
            @block.vector
            def _(v, ht=ht, qb=qb, dp=dp, ac_q=ac_q, ac_dp=ac_dp):
                v.wait_ge(sAC.h, max(ac_q, ac_dp))
                v.wait_ge(sDV.h, sDV.n)
                v.scalar_tensor_tensor(out=trash_bf[:], in0=qb[:], scalar=1.0, in1=dp[:],
                                       op0=ALU.mult, op1=ALU.mult,
                                       accum_out=summ[:, HT + ht:HT + ht + 1]).then_inc(sDV.h, 1)
            dv_stt = sDV.tick(1)
            qbuf_free[ht % 2] = [(sDV, dv_stt), (sspq, q_spill_done[ht])]
            dpow_free[ht % 2] = [(sDV, dv_stt)]
        K_DVE_END = sDV.n
        K_PE_END = sPE.n
        K_SPILL_END = sspq.n

        mixv_done = emit_mixes("v", K_PE_END)
        wtv_done = emit_wt_loads("v", K_PE_END)
        VG_FIRST = [0]
        p_spill_done = [0] * HT
        pbuf_free = [[], []]
        qrr_free = [[(sspq, K_SPILL_END)], [(sspq, K_SPILL_END)]]
        qrr_loaded = [0] * HT
        for ht in range(HT):
            grp = emit_proj_group(ht, mixv_done, {0: wtv_done}, ht == 0)
            if ht == 0:
                VG_FIRST[0] = sPE.n
                emit_cast("o", pe_gate=VG_FIRST[0])
            @block.sync
            def _(s, ht=ht):
                wait_all(s, qrr_free[ht % 2])
                if ht == 0:
                    s.wait_ge(sspq.h, K_SPILL_END)
                srr.chain(s)
                s.dma_start(out=qrr[ht % 2][:], in_=spill_q[ht * 128:(ht + 1) * 128, :]
                            ).then_inc(srr.h, 16)
            qrr_loaded[ht] = srr.tick(16)
            pbx = p_t[ht % 2]
            dp = dpow[ht % 2]
            @block.vector
            def _(v, ht=ht, pbx=pbx, grp=grp):
                wait_all(v, pbuf_free[ht % 2])
                v.wait_ge(srr.h, qrr_loaded[ht])
                v.wait_ge(sPE.h, grp[0][1])
                v.tensor_tensor(out=pbx[:, 0:512], in0=qrr[ht % 2][:, 0:512], in1=grp[0][0].t[:],
                                op=ALU.mult).then_inc(sDV.h, 1)
                grp[0][0].release(sDV, sDV.tick(1))
                v.wait_ge(sPE.h, grp[1][1])
                v.tensor_tensor(out=pbx[:, 512:1024], in0=qrr[ht % 2][:, 512:1024],
                                in1=grp[1][0].t[:], op=ALU.mult).then_inc(sDV.h, 1)
                grp[1][0].release(sDV, sDV.tick(1))
            qrr_free[ht % 2] = [(sDV, sDV.n)]
            dv_p = sDV.n
            @block.scalar
            def _(a_, ht=ht, dp=dp, dv_p=dv_p, pbx=pbx):
                wait_all(a_, dpow_free[ht % 2])
                a_.activation(out=dp[:], in_=iota_rev[:], func=AF.Exp,
                              scale=w_eff[:, ht:ht + 1]).then_inc(sAC.h, 1)
                a_.wait_ge(sDV.h, dv_p)
                sspp.chain(a_)
                a_.dma_start(out=spill_p[ht * 128:(ht + 1) * 128, :], in_=pbx[:]).then_inc(sspp.h, 16)
            ac_dp = sAC.tick(1)
            p_spill_done[ht] = sspp.tick(16)
            @block.vector
            def _(v, ht=ht, pbx=pbx, dp=dp, ac_dp=ac_dp):
                v.wait_ge(sAC.h, ac_dp)
                v.wait_ge(sDV.h, sDV.n)
                v.scalar_tensor_tensor(out=trash_bf[:], in0=pbx[:], scalar=1.0, in1=dp[:],
                                       op0=ALU.mult, op1=ALU.mult,
                                       accum_out=summ[:, ht:ht + 1]).then_inc(sDV.h, 1)
            dv_stt = sDV.tick(1)
            pbuf_free[ht % 2] = [(sDV, dv_stt), (sspp, p_spill_done[ht])]
            dpow_free[ht % 2] = [(sDV, dv_stt)]
        V_DVE_END = sDV.n
        V_PE_END = sPE.n

        @block.gpsimd
        def _(g):
            g.wait_ge(sDV.h, V_DVE_END)
            sgq.chain(g)
            g.dma_start(out=cc_in[:, :], in_=summ[:]).then_inc(sgq.h, 16)
            ccin_ld = sgq.tick(16)
            g.wait_ge(sgq.h, ccin_ld)
            g.collective_compute(
                "AllGather", ALU.bypass,
                replica_groups=[list(range(NCORES))],
                ins=[cc_in.ap().opt()],
                outs=[cc_out.ap().opt()],
            ).then_inc(scc.h, 1)
            g.wait_ge(scc.h, scc.tick(1))
            g.dma_start(out=gath[:].rearrange("p (r f) -> p r f", r=NCORES),
                        in_=cc_out.ap().rearrange("(r p) f -> p r f", p=128)).then_inc(sgq.h, 16)
        GATH_LD = sgq.tick(16)

        mixr_done = emit_mixes("r", V_PE_END)
        wtr_done = emit_wt_loads("r", V_PE_END)
        RG_FIRST = [0]
        @block.vector
        def _(v):
            v.wait_ge(sgq.h, GATH_LD)
            v.wait_ge(sPL.h, PL_COEF)
            v.wait_ge(sDV.h, sDV.n)
            cview = coef_all[:].rearrange("p (r i) -> p i r", r=NCORES)
            gA = gath[:].rearrange("p (r f) -> p f r", r=NCORES)[:, 0:HT, :]
            gB = gath[:].rearrange("p (r f) -> p f r", r=NCORES)[:, HT:2 * HT, :]
            t3 = tmp3[:].rearrange("p (i r) -> p i r", r=NCORES)
            v.tensor_tensor(out=t3, in0=cview, in1=gA, op=ALU.mult).then_inc(sDV.h, 1)
            pA = sDV.tick(1)
            v.wait_ge(sDV.h, pA)
            v.tensor_reduce(out=AinV[:], in_=t3, axis=mybir.AxisListType.X,
                            op=ALU.add).then_inc(sDV.h, 1)
            pA2 = sDV.tick(1)
            v.wait_ge(sDV.h, pA2)
            v.tensor_tensor(out=t3, in0=cview, in1=gB, op=ALU.mult).then_inc(sDV.h, 1)
            pB = sDV.tick(1)
            v.wait_ge(sDV.h, pB)
            v.tensor_reduce(out=BinV[:], in_=t3, axis=mybir.AxisListType.X,
                            op=ALU.add).then_inc(sDV.h, 1)
            sDV.tick(1)
        CARRY_DONE = sDV.n
        s_spill_done = [0] * HT
        sgbuf_free = [[], []]
        for ht in range(HT):
            grp = emit_proj_group(ht, mixr_done, {0: wtr_done}, ht == 0)
            if ht == 0:
                RG_FIRST[0] = sPE.n
            sbx = sig_t
            @block.scalar
            def _(a_, ht=ht, sbx=sbx, grp=grp):
                wait_all(a_, sgbuf_free[0])
                if ht == 0:
                    wait_all(a_, pbuf_free[0])
                    wait_all(a_, pbuf_free[1])
                a_.wait_ge(sPE.h, grp[0][1])
                a_.activation(out=sbx[:, 0:512], in_=grp[0][0].t[:], func=AF.Sigmoid).then_inc(sAC.h, 1)
                grp[0][0].release(sAC, sAC.tick(1))
                a_.wait_ge(sPE.h, grp[1][1])
                a_.activation(out=sbx[:, 512:1024], in_=grp[1][0].t[:], func=AF.Sigmoid).then_inc(sAC.h, 1)
                grp[1][0].release(sAC, sAC.tick(1))
            ac_s = sAC.n
            @block.scalar
            def _(a_, ht=ht, sbx=sbx, ac_s=ac_s):
                a_.wait_ge(sAC.h, ac_s)
                ssps.chain(a_)
                a_.dma_start(out=spill_s[ht * 128:(ht + 1) * 128, :], in_=sbx[:]).then_inc(ssps.h, 16)
            s_spill_done[ht] = ssps.tick(16)
            sgbuf_free[0] = [(ssps, s_spill_done[ht])]
        R_PE_END = sPE.n
        SPILLQ_ALL = sspq.n
        SPILLP_ALL = sspp.n
        SPILLS_ALL = ssps.n


        rrqp_free = [[(sspq, SPILLQ_ALL), (sspp, SPILLP_ALL)],
                     [(sspq, SPILLQ_ALL), (sspp, SPILLP_ALL)]]
        rrs_free = [[], []]
        scan_first = [True]

        def scan_block(k, ch, ht):
            pp = ht % 2
            sem = srrp[pp]
            @block.sync
            def _(s, ch=ch, ht=ht, pp=pp, sem=sem):
                wait_all(s, rrqp_free[pp])
                cs_ = slice(ch * 512, ch * 512 + 512)
                rw = slice(ht * 128, (ht + 1) * 128)
                s.dma_start(out=rrq[pp][:], in_=spill_q[rw, cs_]).then_inc(sem.h, 16)
                s.dma_start(out=rrp[pp][:], in_=spill_p[rw, cs_]).then_inc(sem.h, 16)
                wait_all(s, rrs_free[pp])
                s.wait_ge(ssps.h, s_spill_done[ht])
                s.dma_start(out=rrs[pp][:], in_=spill_s[rw, cs_]).then_inc(srrs[pp].h, 16)
            rr_ld = sem.tick(32)
            rs_ld = srrs[pp].tick(16)
            @block.vector
            def _(v, k=k, ch=ch, ht=ht, pp=pp, rr_ld=rr_ld, rs_ld=rs_ld, sem=sem):
                if scan_first[0]:
                    v.wait_ge(sDV.h, CARRY_DONE)
                    v.wait_ge(sAC.h, AC_D1)
                    v.wait_ge(sgq.h, SMALL_LD)
                    scan_first[0] = False
                v.wait_ge(sem.h, rr_ld)
                v.wait_ge(sDV.h, sDV.n)
                v.tensor_scalar(out=d_bc[:], in0=ones[:], scalar1=d1[:, ht:ht + 1],
                                scalar2=None, op0=ALU.mult).then_inc(sDV.h, 1)
                srcA = AinV if ch == 0 else Acar
                srcB = BinV if ch == 0 else Bcar
                v.tensor_scalar(out=Abuf[:, 0:1], in0=srcA[:, ht:ht + 1], scalar1=0.0,
                                scalar2=None, op0=ALU.add).then_inc(sDV.h, 1)
                v.tensor_scalar(out=Bbuf[:, 0:1], in0=srcB[:, ht:ht + 1], scalar1=0.0,
                                scalar2=None, op0=ALU.add).then_inc(sDV.h, 1)
                pre = sDV.tick(3)
                v.wait_ge(sDV.h, pre)
                v.tensor_tensor_scan(out=Abuf[:, 1:513], data0=d_bc[:], data1=rrp[pp][:],
                                     initial=Abuf[:, 0:1], op0=ALU.mult,
                                     op1=ALU.add).then_inc(sDV.h, 1)
                v.tensor_tensor_scan(out=Bbuf[:, 1:513], data0=d_bc[:], data1=rrq[pp][:],
                                     initial=Bbuf[:, 0:1], op0=ALU.mult,
                                     op1=ALU.add).then_inc(sDV.h, 1)
                sc = sDV.tick(2)
                v.wait_ge(sDV.h, sc)
                if ch == 0:
                    v.tensor_scalar(out=Acar[:, ht:ht + 1], in0=Abuf[:, 512:513], scalar1=0.0,
                                    scalar2=None, op0=ALU.add).then_inc(sDV.h, 1)
                    v.tensor_scalar(out=Bcar[:, ht:ht + 1], in0=Bbuf[:, 512:513], scalar1=0.0,
                                    scalar2=None, op0=ALU.add).then_inc(sDV.h, 1)
                    sDV.tick(2)
                v.scalar_tensor_tensor(out=nbuf[:], in0=rrp[pp][:], scalar=eu[:, ht:ht + 1],
                                       in1=Abuf[:, 0:512], op0=ALU.mult,
                                       op1=ALU.add).then_inc(sDV.h, 1)
                v.scalar_tensor_tensor(out=dbuf[:], in0=rrq[pp][:], scalar=eu[:, ht:ht + 1],
                                       in1=Bbuf[:, 0:512], op0=ALU.mult,
                                       op1=ALU.add).then_inc(sDV.h, 1)
                nd = sDV.tick(2)
                rrqp_free[pp] = [(sDV, nd)]
                v.wait_ge(sDV.h, nd)
                v.reciprocal(out=rec[:], in_=dbuf[:]).then_inc(sDV.h, 1)
                rc = sDV.tick(1)
                v.wait_ge(sDV.h, rc)
                v.tensor_tensor(out=dbuf[:], in0=nbuf[:], in1=rec[:],
                                op=ALU.mult).then_inc(sDV.h, 1)
                yy = sDV.tick(1)
                v.wait_ge(sDV.h, yy)
                v.wait_ge(srrs[pp].h, rs_ld)
                v.tensor_tensor(out=s_buf[:, ht * T + ch * 512: ht * T + ch * 512 + 512],
                                in0=dbuf[:], in1=rrs[pp][:], op=ALU.mult).then_inc(sDV.h, 1)
                sDV.tick(1)
            rrs_free[pp] = [(sDV, sDV.n)]

        ch1_pend = [None]
        s_done_ch1 = [0] * HT

        def emit_pending(v):
            ht_p, pp_p, ac_p, rs_p = ch1_pend[0]
            v.wait_ge(sAC.h, ac_p)
            v.tensor_tensor(out=dbuf[:], in0=nbuf[:], in1=rec[:],
                            op=ALU.mult).then_inc(sDV.h, 1)
            yy = sDV.tick(1)
            v.wait_ge(sDV.h, yy)
            v.wait_ge(srrs[pp_p].h, rs_p)
            v.tensor_tensor(out=s_buf[:, ht_p * T + 512: ht_p * T + 1024],
                            in0=dbuf[:], in1=rrs[pp_p][:], op=ALU.mult).then_inc(sDV.h, 1)
            sDV.tick(1)
            s_done_ch1[ht_p] = sDV.n
            rrs_free[pp_p] = [(sDV, sDV.n)]
            ch1_pend[0] = None

        def scan_block_ch1(ht):
            pp = ht % 2
            sem = srrp[pp]
            @block.sync
            def _(s, ht=ht, pp=pp, sem=sem):
                wait_all(s, rrqp_free[pp])
                cs_ = slice(512, 1024)
                rw = slice(ht * 128, (ht + 1) * 128)
                s.dma_start(out=rrq[pp][:], in_=spill_q[rw, cs_]).then_inc(sem.h, 16)
                s.dma_start(out=rrp[pp][:], in_=spill_p[rw, cs_]).then_inc(sem.h, 16)
                wait_all(s, rrs_free[pp])
                s.dma_start(out=rrs[pp][:], in_=spill_s[rw, cs_]).then_inc(srrs[pp].h, 16)
            rr_ld = sem.tick(32)
            rs_ld = srrs[pp].tick(16)
            @block.vector
            def _(v, ht=ht, pp=pp, rr_ld=rr_ld, sem=sem):
                v.wait_ge(sem.h, rr_ld)
                v.wait_ge(sDV.h, sDV.n)
                v.tensor_scalar(out=d_bc[:], in0=ones[:], scalar1=d1[:, ht:ht + 1],
                                scalar2=None, op0=ALU.mult).then_inc(sDV.h, 1)
                v.tensor_scalar(out=Abuf[:, 0:1], in0=Acar[:, ht:ht + 1], scalar1=0.0,
                                scalar2=None, op0=ALU.add).then_inc(sDV.h, 1)
                v.tensor_scalar(out=Bbuf[:, 0:1], in0=Bcar[:, ht:ht + 1], scalar1=0.0,
                                scalar2=None, op0=ALU.add).then_inc(sDV.h, 1)
                pre = sDV.tick(3)
                v.wait_ge(sDV.h, pre)
                v.tensor_tensor_scan(out=Abuf[:, 1:513], data0=d_bc[:], data1=rrp[pp][:],
                                     initial=Abuf[:, 0:1], op0=ALU.mult,
                                     op1=ALU.add).then_inc(sDV.h, 1)
                v.tensor_tensor_scan(out=Bbuf[:, 1:513], data0=d_bc[:], data1=rrq[pp][:],
                                     initial=Bbuf[:, 0:1], op0=ALU.mult,
                                     op1=ALU.add).then_inc(sDV.h, 1)
                sc = sDV.tick(2)
                if ch1_pend[0] is not None:
                    emit_pending(v)
                v.wait_ge(sDV.h, sDV.n)
                v.scalar_tensor_tensor(out=nbuf[:], in0=rrp[pp][:], scalar=eu[:, ht:ht + 1],
                                       in1=Abuf[:, 0:512], op0=ALU.mult,
                                       op1=ALU.add).then_inc(sDV.h, 1)
                v.scalar_tensor_tensor(out=dbuf[:], in0=rrq[pp][:], scalar=eu[:, ht:ht + 1],
                                       in1=Bbuf[:, 0:512], op0=ALU.mult,
                                       op1=ALU.add).then_inc(sDV.h, 1)
                sDV.tick(2)
            nd = sDV.n
            lnt = rrq[pp]
            @block.scalar
            def _(a_, nd=nd, lnt=lnt):
                a_.wait_ge(sDV.h, nd)
                a_.activation(out=lnt[:], in_=dbuf[:], func=AF.Ln).then_inc(sAC.h, 1)
                a_.wait_ge(sAC.h, sAC.tick(1))
                a_.activation(out=rec[:], in_=lnt[:], func=AF.Exp,
                              scale=e_t[:, 0:1]).then_inc(sAC.h, 1)
            ac2 = sAC.tick(1)
            rrqp_free[pp] = [(sDV, nd), (sAC, ac2)]
            ch1_pend[0] = (ht, pp, ac2, rs_ld)

        h2T_r = h2T[:].rearrange("p (ct w) -> p ct w", ct=CT)
        wo_banks = [pb[0], pb[1], pb[2], pb[3]]
        ph4_pe_last = [0] * 8
        x1_free = [[(sDV, 0)]]
        wo_x1_done = [0] * 8
        h2t_done_tt = [0] * 8

        def wo_mm_part(tt, s_done):
            @block.tensor
            def _(t, tt=tt, s_done=s_done):
                for cs in range(4):
                    wo_banks[cs].acquire(t)
                for ht2 in range(HT):
                    t.wait_ge(sDV.h, s_done[ht2])
                    t.wait_ge(swt.h, WO_LD_HT[ht2])
                    for cs in range(4):
                        mm = t.matmul(wo_banks[cs].t[:],
                                      s_buf[:, ht2 * T + tt * 128: ht2 * T + (tt + 1) * 128],
                                      WoT[:, ht2 * 2048 + cs * 512: ht2 * 2048 + cs * 512 + 512],
                                      start=(ht2 == 0), stop=(ht2 == HT - 1))
                        if ht2 == HT - 1:
                            mm.then_inc(sPE.h, 1)
            return [(wo_banks[cs], sPE.tick(1)) for cs in range(4)]

        def wo_evac_part(tt, wo_g):
            x1b = x1_t[0]
            @block.sync
            def _(s, tt=tt, x1b=x1b):
                wait_all(s, x1_free[0])
                sx.chain(s)
                s.dma_start(out=x1b[:],
                            in_=x_seg[1 + tt * 128:1 + (tt + 1) * 128, :]).then_inc(sx.h, 16)
            x_ld = sx.tick(16)
            for cs in range(4):
                bk, pev = wo_g[cs]
                @block.vector
                def _(v, cs=cs, bk=bk, pev=pev, x1b=x1b, x_ld=x_ld):
                    if cs == 0:
                        v.wait_ge(sx.h, x_ld)
                        v.wait_ge(sDV.h, sDV.n)
                    v.wait_ge(sPE.h, pev)
                    v.tensor_tensor(out=x1b[:, cs * 512:cs * 512 + 512],
                                    in0=x1b[:, cs * 512:cs * 512 + 512],
                                    in1=bk.t[:], op=ALU.add).then_inc(sDV.h, 1)
                bk.release(sDV, sDV.tick(1))
            wo_x1_done[tt] = sDV.n

        ln2_dvh2 = [0] * 8

        def ln2_math(tt):
            x1b = x1_t[0]
            so = (tt % 2) * 8
            @block.scalar
            def _(a_, x1b=x1b, so=so, tt=tt):
                a_.wait_ge(sDV.h, wo_x1_done[tt])
                a_.wait_ge(sAC.h, sAC.n)
                if tt >= 1:
                    a_.wait_ge(sPE.h, ph4_pe_last[tt - 1])
                a_.activation(out=h2_bf[:], in_=x1b[:], func=AF.Square,
                              accum_out=stat[:, so + 1:so + 2]).then_inc(sAC.h, 1)
            ac_sq2 = sAC.tick(1)
            @block.vector
            def _(v, x1b=x1b, so=so, tt=tt, ac_sq2=ac_sq2):
                v.wait_ge(sDV.h, wo_x1_done[tt])
                v.tensor_reduce(out=stat[:, so:so + 1], in_=x1b[:], axis=mybir.AxisListType.X,
                                op=ALU.add).then_inc(sDV.h, 1)
                both = sDV.tick(1)
                v.wait_ge(sDV.h, both)
                v.wait_ge(sAC.h, ac_sq2)
                v.tensor_scalar(out=stat[:, so + 2:so + 3], in0=stat[:, so:so + 1],
                                scalar1=1.0 / C, scalar2=None, op0=ALU.mult).then_inc(sDV.h, 1)
                v.tensor_scalar(out=stat[:, so + 3:so + 4], in0=stat[:, so + 1:so + 2],
                                scalar1=1.0 / C, scalar2=None, op0=ALU.mult).then_inc(sDV.h, 1)
                m2 = sDV.tick(2)
                v.wait_ge(sDV.h, m2)
                v.tensor_tensor(out=stat[:, so + 4:so + 5], in0=stat[:, so + 2:so + 3],
                                in1=stat[:, so + 2:so + 3], op=ALU.mult).then_inc(sDV.h, 1)
                me = sDV.tick(1)
                v.wait_ge(sDV.h, me)
                v.tensor_tensor(out=stat[:, so + 5:so + 6], in0=stat[:, so + 3:so + 4],
                                in1=stat[:, so + 4:so + 5], op=ALU.subtract).then_inc(sDV.h, 1)
                sDV.tick(1)
            dv_var = sDV.n
            @block.scalar
            def _(a_, tt=tt, x1b=x1b, so=so, dv_var=dv_var):
                a_.wait_ge(sDV.h, dv_var)
                a_.activation(out=stat[:, so + 6:so + 7], in_=stat[:, so + 5:so + 6],
                              func=AF.Sqrt, bias=eps_t[:, 0:1]).then_inc(sAC.h, 1)
                sx1w.chain(a_)
                a_.dma_start(out=x1_hbm[tt * 128:(tt + 1) * 128, :], in_=x1b[:]).then_inc(sx1w.h, 16)
            ac_sqrt = sAC.tick(1)
            x1_sp = sx1w.tick(16)
            @block.vector
            def _(v, tt=tt, x1b=x1b, so=so, ac_sqrt=ac_sqrt):
                v.wait_ge(sAC.h, ac_sqrt)
                v.reciprocal(out=stat[:, so + 7:so + 8], in_=stat[:, so + 6:so + 7]).then_inc(sDV.h, 1)
                rst = sDV.tick(1)
                v.wait_ge(sDV.h, rst)
                if tt >= 1:
                    v.wait_ge(sPE.h, ph4_pe_last[tt - 1])
                v.tensor_scalar(out=h2_bf[:], in0=x1b[:], scalar1=stat[:, so + 2:so + 3],
                                scalar2=stat[:, so + 7:so + 8], op0=ALU.subtract,
                                op1=ALU.mult).then_inc(sDV.h, 1)
                sDV.tick(1)
            dv_h2 = sDV.n
            ln2_dvh2[tt] = dv_h2
            x1_free[0] = [(sDV, dv_h2), (sx1w, x1_sp)]

        def ln2_tr(tt):
            dv_h2 = ln2_dvh2[tt]
            for rnd in range(2):
                @block.tensor
                def _(t, rnd=rnd, dv_h2=dv_h2):
                    t.wait_ge(sDV.h, dv_h2)
                    ptr.acquire(t)
                    for b in range(8):
                        ct2 = rnd * 8 + b
                        mm = t.transpose(ptr.t[:, b * 128:(b + 1) * 128],
                                         h2_bf[:, ct2 * 128:(ct2 + 1) * 128], ident[:])
                        if b == 7:
                            mm.then_inc(sPE.h, 1)
                pe_tr = sPE.tick(1)
                @block.scalar
                def _(a_, tt=tt, rnd=rnd, pe_tr=pe_tr):
                    a_.wait_ge(sPE.h, pe_tr)
                    a_.copy(out=h2T_r[:, rnd * 8:(rnd + 1) * 8, tt * 128:(tt + 1) * 128],
                            in_=ptr.t[:].rearrange("p (b w) -> p b w", b=8)).then_inc(sAC.h, 1)
                ptr.release(sAC, sAC.tick(1))
                ph4_pe_last[tt] = sPE.n
            h2t_done_tt[tt] = sAC.n

        s_done_ch0 = [0] * HT
        for ht in range(HT):
            scan_block(ht, 0, ht)
            s_done_ch0[ht] = sDV.n
        WO_B = swt.n
        @block.scalar
        def _(a_):
            a_.wait_ge(sgq.h, cast_done["o"])
            a_.wait_ge(sPE.h, R_PE_END)
            swt.chain(a_)
            for ht2 in range(HT):
                if ht2:
                    a_.wait_ge(swt.h, WO_B + 16 * ht2)
                a_.dma_start(out=WoT[:, ht2 * 2048:(ht2 + 1) * 2048],
                             in_=wbf["o"][0:2048, ht2 * 128:(ht2 + 1) * 128],
                             transpose=True).then_inc(swt.h, 16)
        swt.tick(16 * 16)
        WO_LD_HT = [WO_B + 16 * (i + 1) for i in range(HT)]

        for tt in range(4):
            wo_g = wo_mm_part(tt, s_done_ch0)
            if tt >= 1:
                ln2_tr(tt - 1)
            wo_evac_part(tt, wo_g)
            ln2_math(tt)
            for j in range(4):
                scan_block_ch1(tt * 4 + j)
        @block.vector
        def _(v):
            emit_pending(v)
        for tt in range(4, 8):
            wo_g = wo_mm_part(tt, s_done_ch1)
            ln2_tr(tt - 1)
            wo_evac_part(tt, wo_g)
            ln2_math(tt)
        ln2_tr(7)

        H2T_DONE = sAC.n
        WO_PE_END = sPE.n
        X1_SPILL_END = sx1w.n
        PH4_DVE_END = sDV.n
        NCH = 16
        wgbuf_free = [[(sPE, WO_PE_END)], [(sPE, WO_PE_END)]]
        fch_free = [[(sPE, WO_PE_END)], [(sPE, WO_PE_END)]]
        gscr_free = [[(sPE, WO_PE_END)], [(sPE, WO_PE_END)]]
        ffn_banks = [pb[0], pb[1], pb[2], pb[3], pb[4], pb[5]]
        bank_i = [0]
        gscr_i = [0]
        fuse_wr = [0] * NCH
        wg_ld = [0] * NCH
        for ich in range(NCH):
            pp = ich % 2
            wgx = wgb[pp] if ich < NCH - 1 else wgb_alt
            wux = wub[pp] if ich < NCH - 1 else wub_alt
            @block.gpsimd
            def _(g, ich=ich, pp=pp, wgx=wgx, wux=wux):
                if ich < NCH - 1:
                    wait_all(g, wgbuf_free[pp])
                else:
                    g.wait_ge(sDV.h, PH4_DVE_END)
                    g.wait_ge(sx1w.h, X1_SPILL_END)
                if ich < 2:
                    g.wait_ge(sPE.h, WO_PE_END)
                sgq.chain(g)
                isl = slice(ich * 512, (ich + 1) * 512)
                g.dma_start(out=wgx[:].rearrange("p (ct i) -> p ct i", ct=CT),
                            in_=wg8[:, isl].rearrange("(ct p) i -> p ct i", p=128)
                            ).then_inc(sgq.h, 16)
                g.dma_start(out=wux[:].rearrange("p (ct i) -> p ct i", ct=CT),
                            in_=wu8[:, isl].rearrange("(ct p) i -> p ct i", p=128)
                            ).then_inc(sgq.h, 16)
            wg_ld[ich] = sgq.tick(32)
            fb = fch[pp]
            for tc in range(2):
                for im in range(4):
                    bg = ffn_banks[bank_i[0] % 6]; bank_i[0] += 1
                    bu = ffn_banks[bank_i[0] % 6]; bank_i[0] += 1
                    it_g = ich * 4 + im
                    pe_cnt = []
                    @block.tensor
                    def _(t, ich=ich, tc=tc, im=im, bg=bg, bu=bu, pp=pp, pe_cnt=pe_cnt):
                        if ich == 0 and tc == 0 and im == 0:
                            t.wait_ge(sAC.h, H2T_DONE)
                        if tc == 0 and im == 0:
                            t.wait_ge(sgq.h, wg_ld[ich])
                        wgx = wgb[pp] if ich < NCH - 1 else wgb_alt
                        wux = wub[pp] if ich < NCH - 1 else wub_alt
                        bg.acquire(t)
                        for ct in range(CT):
                            mm = t.matmul(bg.t[:],
                                          wgx[:, ct * 512 + im * 128: ct * 512 + (im + 1) * 128],
                                          h2T[:, ct * T + tc * 512: ct * T + tc * 512 + 512],
                                          start=(ct == 0), stop=(ct == CT - 1))
                            if ct == CT - 1:
                                mm.then_inc(sPE.h, 1)
                        pe_cnt.append(sPE.tick(1))
                        bu.acquire(t)
                        for ct in range(CT):
                            mm = t.matmul(bu.t[:],
                                          wux[:, ct * 512 + im * 128: ct * 512 + (im + 1) * 128],
                                          h2T[:, ct * T + tc * 512: ct * T + tc * 512 + 512],
                                          start=(ct == 0), stop=(ct == CT - 1))
                            if ct == CT - 1:
                                mm.then_inc(sPE.h, 1)
                        pe_cnt.append(sPE.tick(1))
                    gpe, upe = pe_cnt
                    gi = gscr_i[0] % 2; gscr_i[0] += 1
                    gsc = g_scr[gi]
                    @block.scalar
                    def _(a_, it_g=it_g, gpe=gpe, gsc=gsc, bg=bg, gi=gi):
                        wait_all(a_, gscr_free[gi])
                        a_.wait_ge(sPE.h, gpe)
                        a_.activation(out=gsc[:], in_=bg.t[:], func=AF.Silu,
                                      scale=sgate_v[:, it_g:it_g + 1]).then_inc(sAC.h, 1)
                    bg.release(sAC, sAC.tick(1))
                    ac_g = sAC.n
                    @block.vector
                    def _(v, tc=tc, im=im, it_g=it_g, upe=upe, ac_g=ac_g,
                          gsc=gsc, bu=bu, fb=fb, pp=pp):
                        if tc == 0 and im == 0:
                            wait_all(v, fch_free[pp])
                        v.wait_ge(sPE.h, upe)
                        v.wait_ge(sAC.h, ac_g)
                        slot = im * 2 + tc
                        v.scalar_tensor_tensor(out=fb[:, slot * 512:(slot + 1) * 512],
                                               in0=bu.t[:], scalar=sup_v[:, it_g:it_g + 1],
                                               in1=gsc[:], op0=ALU.mult,
                                               op1=ALU.mult).then_inc(sDV.h, 1)
                    bu.release(sDV, sDV.tick(1))
                    gscr_free[gi] = [(sDV, sDV.n)]
            dv_f = sDV.n
            @block.scalar
            def _(a_, ich=ich, pp=pp, dv_f=dv_f, fb=fb):
                a_.wait_ge(sDV.h, dv_f)
                dst = fuse_hbm[ich * 512:(ich + 1) * 512, :].rearrange(
                    "(im p) (tc t) -> p im tc t", p=128, t=512)
                sfw.chain(a_)
                a_.dma_start(out=dst, in_=fb[:].rearrange("p (im tc t) -> p im tc t",
                                                          im=4, tc=2)).then_inc(sfw.h, 16)
            fuse_wr[ich] = sfw.tick(16)
            fch_free[pp] = [(sfw, fuse_wr[ich])]
            wgbuf_free[pp] = [(sPE, sPE.n)]
            if ich == NCH - 3:
                GU13_PE = sPE.n
            if ich == NCH - 2:
                GU14_PE = sPE.n
        GU_PE_END = sPE.n
        FUSE_ALL = sfw.n
        FUSE_DVE_END = sDV.n

        dn_banks = [pb[0], pb[1], pb[2], pb[3]]
        wdb_free = [[], []]
        ftt_free = [[(sPE, GU_PE_END)], [(sPE, GU_PE_END)]]
        x1rr_free = [[(sDV, PH4_DVE_END)], [(sDV, PH4_DVE_END)]]
        ost_free = [[(sDV, PH4_DVE_END)], [(sDV, PH4_DVE_END)]]
        wd_ld = [0] * 4
        for cs in range(4):
            @block.gpsimd
            def _(g, cs=cs):
                pp = cs % 2
                wait_all(g, wdb_free[pp])
                if cs == 1:
                    g.wait_ge(sPE.h, GU_PE_END)
                    g.wait_ge(sfw.h, FUSE_ALL)
                    g.wait_ge(sDV.h, FUSE_DVE_END)
                sgq.chain(g)
                order = (1, 3, 0, 2) if cs == 0 else range(4)
                for part in order:
                    if cs == 0 and part == 1:
                        g.wait_ge(sPE.h, GU13_PE)
                    if cs == 0 and part == 0:
                        g.wait_ge(sPE.h, GU14_PE)
                    rsl = slice(part * 2048, (part + 1) * 2048)
                    g.dma_start(
                        out=wdb[pp][:].rearrange("p (it c) -> p it c", it=ITT)[:, part * 16:(part + 1) * 16, :],
                        in_=wd8[rsl, cs * 512:(cs + 1) * 512].rearrange("(it p) c -> p it c", p=128)
                    ).then_inc(sgq.h, 16)
            wd_ld[cs] = sgq.tick(64)
            for tt in range(8):
                k_i = cs * 8 + tt
                pp = k_i % 2
                fsem = sftp[pp]
                ftt_base = fsem.n
                @block.sync
                def _(s, cs=cs, tt=tt, pp=pp, k_i=k_i, fsem=fsem, ftt_base=ftt_base):
                    wait_all(s, ftt_free[pp])
                    if k_i == 0:
                        s.wait_ge(sPE.h, GU_PE_END)
                    for part in range(4):
                        if k_i == 0:
                            s.wait_ge(sfw.h, fuse_wr[4 * part + 3])
                            if part:
                                s.wait_ge(fsem.h, ftt_base + 16 * part)
                        rsl = slice(part * 2048, (part + 1) * 2048)
                        s.dma_start(
                            out=fuse_tt[pp][:].rearrange("p (it t) -> p it t", it=ITT)[:, part * 16:(part + 1) * 16, :],
                            in_=fuse_hbm[rsl, tt * 128:(tt + 1) * 128].rearrange("(it p) t -> p it t", p=128)
                        ).then_inc(fsem.h, 16)
                ftt_ld = fsem.tick(64)
                @block.sync
                def _(s, cs=cs, tt=tt, pp=pp, k_i=k_i):
                    wait_all(s, x1rr_free[pp])
                    if k_i == 0:
                        s.wait_ge(sx1w.h, X1_SPILL_END)
                        s.wait_ge(sDV.h, PH4_DVE_END)
                        s.wait_ge(sPE.h, GU_PE_END)
                    sx.chain(s)
                    s.dma_start(out=x1rr[pp][:],
                                in_=x1_hbm[tt * 128:(tt + 1) * 128, cs * 512:(cs + 1) * 512]
                                ).then_inc(sx.h, 16)
                x1_ld = sx.tick(16)
                bk = dn_banks[tt % 4]
                @block.tensor
                def _(t, cs=cs, pp=pp, bk=bk, k_i=k_i, ftt_base=ftt_base, fsem=fsem):
                    t.wait_ge(sgq.h, wd_ld[cs])
                    if k_i != 0:
                        t.wait_ge(fsem.h, ftt_base + 64)
                    bk.acquire(t)
                    for it in range(ITT):
                        if k_i == 0 and it % 16 == 0:
                            t.wait_ge(fsem.h, ftt_base + 16 * (it // 16 + 1))
                        mm = t.matmul(bk.t[:], fuse_tt[pp][:, it * 128:(it + 1) * 128],
                                      wdb[cs % 2][:, it * 512:(it + 1) * 512],
                                      start=(it == 0), stop=(it == ITT - 1))
                        if it == ITT - 1:
                            mm.then_inc(sPE.h, 1)
                pev = sPE.tick(1)
                ftt_free[pp] = [(sPE, pev)]
                ob = o_st[pp]
                @block.vector
                def _(v, cs=cs, pp=pp, bk=bk, pev=pev, ob=ob, x1_ld=x1_ld, k_i=k_i):
                    wait_all(v, ost_free[pp])
                    if k_i == 0:
                        v.wait_ge(sx1w.h, X1_SPILL_END)
                    v.wait_ge(sx.h, x1_ld)
                    v.wait_ge(sPE.h, pev)
                    v.tensor_tensor(out=ob[:], in0=bk.t[:],
                                    in1=sdown_v[:, cs * 512:(cs + 1) * 512],
                                    op=ALU.mult).then_inc(sDV.h, 1)
                    t1 = sDV.tick(1)
                    v.wait_ge(sDV.h, t1)
                    v.tensor_tensor(out=ob[:], in0=ob[:], in1=x1rr[pp][:], op=ALU.add).then_inc(sDV.h, 1)
                bk.release(sDV, sDV.tick(1))
                x1rr_free[pp] = [(sDV, sDV.n)]
                dv_o = sDV.n
                if tt == 7:
                    wdb_free[cs % 2] = [(sPE, pev), (sDV, dv_o)]
                @block.scalar
                def _(a_, cs=cs, tt=tt, dv_o=dv_o, ob=ob):
                    a_.wait_ge(sDV.h, dv_o)
                    sout.chain(a_)
                    a_.dma_start(out=out[tt * 128:(tt + 1) * 128, cs * 512:(cs + 1) * 512],
                                 in_=ob[:]).then_inc(sout.h, 16)
                ost_free[pp] = [(sout, sout.tick(16))]

        @block.scalar
        def _(a_):
            a_.wait_ge(sout.h, sout.n)

    return nc


def make_in_maps(L0_attn_x, L0_attn_mu_r, L0_attn_mu_k, L0_attn_mu_v, L0_attn_wkv_w,
                 L0_attn_wkv_u, L0_attn_Wr, L0_attn_Wk, L0_attn_Wv, L0_attn_Wo,
                 L0_ffn_w_gate, L0_ffn_w_up, L0_ffn_w_down, L0_ffn_s_gate, L0_ffn_s_up,
                 L0_ffn_s_down):
    x = np.ascontiguousarray(np.asarray(L0_attn_x, dtype=np.float32).reshape(NCORES * T, C))

    def pt(v, nt):
        return np.ascontiguousarray(np.asarray(v, dtype=np.float32).reshape(nt, 128).T)

    common = dict(
        Wr=np.ascontiguousarray(L0_attn_Wr, dtype=np.float32),
        Wk=np.ascontiguousarray(L0_attn_Wk, dtype=np.float32),
        Wv=np.ascontiguousarray(L0_attn_Wv, dtype=np.float32),
        Wo=np.ascontiguousarray(L0_attn_Wo, dtype=np.float32),
        wg8=np.ascontiguousarray(L0_ffn_w_gate, dtype=np.int8),
        wu8=np.ascontiguousarray(L0_ffn_w_up, dtype=np.int8),
        wd8=np.ascontiguousarray(L0_ffn_w_down, dtype=np.int8),
        mu_r_pt=pt(L0_attn_mu_r, CT),
        mu_k_pt=pt(L0_attn_mu_k, CT),
        mu_v_pt=pt(L0_attn_mu_v, CT),
        w_pt=pt(L0_attn_wkv_w, HT),
        u_pt=pt(L0_attn_wkv_u, HT),
        sgate_pt=pt(L0_ffn_s_gate, ITT),
        sup_pt=pt(L0_ffn_s_up, ITT),
        sdown_b=np.ascontiguousarray(
            np.broadcast_to(np.asarray(L0_ffn_s_down, dtype=np.float32), (128, C))),
        ident_h=np.eye(128, dtype=np.float32),
        iota_rev_h=np.ascontiguousarray(
            np.broadcast_to(np.arange(T - 1, -1, -1, dtype=np.float32), (128, T))),
        ones_h=np.ones((128, 512), dtype=np.float32),
    )

    in_maps = []
    for c in range(NCORES):
        if c == 0:
            seg = np.concatenate([np.zeros((1, C), np.float32), x[0:T]], axis=0)
        else:
            seg = x[c * T - 1:(c + 1) * T]
        segc = np.empty(NCORES, np.float32)
        for r in range(NCORES):
            segc[r] = 1024.0 * (c - 1 - r) if r < c else 4.0e7
        m = dict(common)
        m["x_seg"] = np.ascontiguousarray(seg)
        m["segc_h"] = np.ascontiguousarray(np.broadcast_to(segc, (128, NCORES)))
        in_maps.append(m)
    return in_maps


def kernel(**inputs):
    in_maps = make_in_maps(**inputs)
    nc = build_program()
    res = run_bass_kernel_spmd(nc, in_maps, core_ids=list(range(NCORES)))
    full = np.concatenate([np.asarray(res.results[c]["out"]) for c in range(NCORES)], axis=0)
    return full.reshape(1, NCORES * T, C).astype(np.float32)


# version 26
# speedup vs baseline: 1.0292x; 1.0008x over previous
import numpy as np
from contextlib import ExitStack

import concourse.bass as bass
import concourse.mybir as mybir
from concourse.bass_utils import run_bass_kernel_spmd

F32 = mybir.dt.float32
BF16 = mybir.dt.bfloat16
I8 = mybir.dt.int8
U8 = mybir.dt.uint8
AF = mybir.ActivationFunctionType
ALU = mybir.AluOpType

NCORES = 8
T = 1024
C = 2048
H = 2048
FF = 8192
CT = C // 128
HT = H // 128
ITT = FF // 128
EPS = 1e-5
KB = 1024


class Sem:
    def __init__(self, nc, ctx, name):
        self.h = ctx.enter_context(nc.semaphore(name))
        self.n = 0

    def tick(self, k):
        self.n += k
        return self.n

    def chain(self, eng):
        if self.n:
            eng.wait_ge(self.h, self.n)


def wait_all(eng, guards):
    for g in guards:
        if g is not None:
            s, v = g
            if v > 0:
                eng.wait_ge(s.h, v)


class Bank:
    def __init__(self, t):
        self.t = t
        self.guard = None

    def acquire(self, eng):
        wait_all(eng, [self.guard])

    def release(self, sem, val):
        self.guard = (sem, val)


def build_program():
    nc = bass.Bass("TRN2", target_bir_lowering=False, debug=False)

    x_seg = nc.dram_tensor("x_seg", [T + 1, C], F32, kind="ExternalInput").ap()
    Wmats = {n: nc.dram_tensor(n2, [2048, 2048], F32, kind="ExternalInput").ap()
             for n, n2 in (("r", "Wr"), ("k", "Wk"), ("v", "Wv"), ("o", "Wo"))}
    wg8 = nc.dram_tensor("wg8", [C, FF], I8, kind="ExternalInput").ap()
    wu8 = nc.dram_tensor("wu8", [C, FF], I8, kind="ExternalInput").ap()
    wd8 = nc.dram_tensor("wd8", [FF, C], I8, kind="ExternalInput").ap()
    mu_r_pt = nc.dram_tensor("mu_r_pt", [128, CT], F32, kind="ExternalInput").ap()
    mu_k_pt = nc.dram_tensor("mu_k_pt", [128, CT], F32, kind="ExternalInput").ap()
    mu_v_pt = nc.dram_tensor("mu_v_pt", [128, CT], F32, kind="ExternalInput").ap()
    w_pt = nc.dram_tensor("w_pt", [128, HT], F32, kind="ExternalInput").ap()
    u_pt = nc.dram_tensor("u_pt", [128, HT], F32, kind="ExternalInput").ap()
    sgate_pt = nc.dram_tensor("sgate_pt", [128, ITT], F32, kind="ExternalInput").ap()
    sup_pt = nc.dram_tensor("sup_pt", [128, ITT], F32, kind="ExternalInput").ap()
    sdown_b = nc.dram_tensor("sdown_b", [128, C], F32, kind="ExternalInput").ap()
    ident_h = nc.dram_tensor("ident_h", [128, 128], F32, kind="ExternalInput").ap()
    iota_rev_h = nc.dram_tensor("iota_rev_h", [128, T], F32, kind="ExternalInput").ap()
    ones_h = nc.dram_tensor("ones_h", [128, 512], F32, kind="ExternalInput").ap()
    segc_h = nc.dram_tensor("segc_h", [128, NCORES], F32, kind="ExternalInput").ap()
    out = nc.dram_tensor("out", [T, C], F32, kind="ExternalOutput").ap()

    wbf = {n: nc.dram_tensor(f"wbf_{n}", [2048, 2048], BF16, kind="Internal").ap()
           for n in ("r", "k", "v", "o")}
    spill_q = nc.dram_tensor("spill_q", [H, T], F32, kind="Internal").ap()
    spill_p = nc.dram_tensor("spill_p", [H, T], F32, kind="Internal").ap()
    spill_s = nc.dram_tensor("spill_s", [H, T], F32, kind="Internal").ap()
    fuse_hbm = nc.dram_tensor("fuse_hbm", [FF, T], BF16, kind="Internal").ap()
    x1_hbm = nc.dram_tensor("x1_hbm", [T, C], F32, kind="Internal").ap()
    cc_in = nc.dram_tensor("cc_in", [128, 2 * HT], F32, kind="Internal")
    cc_out = nc.dram_tensor("cc_out", [NCORES * 128, 2 * HT], F32, kind="Internal")

    ctx = ExitStack()
    with ctx:
        ARENA_BYTES = 196224
        backing = ctx.enter_context(nc.sbuf_tensor("arena", [128, ARENA_BYTES], U8))
        base = nc.lookup_mloc(backing).addr

        def alloc(name, shape, dtype, off):
            assert off % 64 == 0, name
            nbytes = int(np.prod(shape[1:])) * mybir.dt.size(dtype)
            assert off + nbytes <= ARENA_BYTES, (name, off, nbytes)
            h = nc.virtual_sbuf_tensor(name, shape, dtype)
            nc._tensor(h.name, shape, dtype, type="SB")
            mloc = nc.lookup_mloc(h)
            mloc.allocated = True
            mloc.addr = base + off
            return h

        ident = alloc("ident", [128, 128], BF16, 0)
        ones = alloc("ones", [128, 512], F32, 320)
        iota_rev = alloc("iota_rev", [128, T], F32, 2368)
        vb = [6464]
        def vec(name):
            h = alloc(name, [128, 16], F32, vb[0])
            vb[0] += 64
            return h
        mu = {"r": vec("mu_r"), "k": vec("mu_k"), "v": vec("mu_v")}
        wraw = vec("wraw"); uraw = vec("uraw")
        w_eff = vec("w_eff"); d1 = vec("d1"); eu = vec("eu")
        AinV = vec("AinV"); BinV = vec("BinV")
        Acar = vec("Acar"); Bcar = vec("Bcar")
        tmp16a = vec("tmp16a"); tmp16b = vec("tmp16b"); eps_t = vec("eps_t"); e_t = vec("e_t")
        summ = alloc("summ", [128, 2 * HT], F32, vb[0]); vb[0] += 128
        segc = alloc("segc", [128, 8], F32, vb[0]); vb[0] += 64
        stat = alloc("stat", [128, 16], F32, vb[0]); vb[0] += 64
        sgate_v = alloc("sgate_v", [128, ITT], F32, 8000)
        sup_v = alloc("sup_v", [128, ITT], F32, 8256)
        sdown_v = alloc("sdown_v", [128, C], F32, 8512)
        gath = alloc("gath", [128, NCORES * 2 * HT], F32, 16704)
        coef_all = alloc("coef_all", [128, 128], F32, 17792)
        tmp3 = alloc("tmp3", [128, 128], F32, 18368)
        e128 = alloc("e128", [128, 128], F32, 18944)

        XB = 20 * KB
        mixT = alloc("mixT", [128, CT * T], BF16, XB)
        trash2 = alloc("trash2", [128, C], BF16, XB)
        h2T = alloc("h2T", [128, CT * T], BF16, XB)
        fuse_tt = [alloc("fuse_tt0", [128, ITT * 128], BF16, XB),
                   alloc("fuse_tt1", [128, ITT * 128], BF16, XB + ITT * 128 * 2)]

        BB = 53 * KB
        hT = alloc("hT", [128, CT * (T + 1)], BF16, BB)
        WT = alloc("WT", [128, CT * 2048], BF16, BB + 33 * KB)
        SCR = BB + 99 * KB
        x_t = [alloc(f"x_t{i}", [128, C], F32, SCR + i * 8 * KB) for i in range(2)] + [
            alloc("x_t2", [128, C], F32, SCR + 20 * KB),
            alloc("x_t3", [128, C], F32, SCR + 28 * KB)]
        h_bf = alloc("h_bf", [128, C], BF16, SCR + 16 * KB)
        delta = alloc("delta", [128, T], F32, SCR)
        q_t = [alloc(f"q_t{i}", [128, T], F32, SCR + 4 * KB + i * 4 * KB) for i in range(2)]
        qrr = [alloc(f"qrr{i}", [128, T], F32, SCR + 12 * KB + i * 4 * KB) for i in range(2)]
        p_t = [alloc(f"p_t{i}", [128, T], F32, SCR + 20 * KB + i * 4 * KB) for i in range(2)]
        dpow = [alloc(f"dpow{i}", [128, T], F32, SCR + 28 * KB + i * 4 * KB) for i in range(2)]
        trash_bf = alloc("trash_bf", [128, T], BF16, SCR + 36 * KB)
        s_buf = alloc("s_buf", [128, HT * T], BF16, BB)
        WoT = alloc("WoT", [128, HT * 2048], BF16, BB + 33 * KB)
        rrq = [alloc(f"rrq{i}", [128, 512], F32, SCR + 12 * KB + i * 2 * KB) for i in range(2)]
        rrp = [alloc(f"rrp{i}", [128, 512], F32, SCR + 16 * KB + i * 2 * KB) for i in range(2)]
        rrs = [alloc(f"rrs{i}", [128, 512], F32, SCR + 20 * KB + i * 2 * KB) for i in range(2)]
        Abuf = alloc("Abuf", [128, 513], F32, SCR + 24 * KB)
        Bbuf = alloc("Bbuf", [128, 513], F32, SCR + 26 * KB + 64)
        nbuf = alloc("nbuf", [128, 512], F32, SCR + 28 * KB + 128)
        dbuf = alloc("dbuf", [128, 512], F32, SCR + 30 * KB + 128)
        rec = alloc("rec", [128, 512], F32, SCR)
        d_bc = alloc("d_bc", [128, 512], F32, SCR + 32 * KB + 192)
        x1_t = [alloc("x1_t0", [128, C], F32, SCR + 2 * KB)]
        h2_bf = alloc("h2_bf", [128, C], BF16, SCR + 34 * KB + 256)
        sig_t = alloc("sig_t", [128, T], F32, SCR + 34 * KB + 256)
        x1rr = [alloc(f"x1rr{i}", [128, 512], F32, SCR + 30 * KB + i * 2 * KB) for i in range(2)]
        o_st = [alloc(f"o_st{i}", [128, 512], F32, SCR + 34 * KB + i * 2 * KB) for i in range(2)]
        wgb = [alloc(f"wgb{i}", [128, CT * 512], BF16, BB + i * 16 * KB) for i in range(2)]
        wub = [alloc(f"wub{i}", [128, CT * 512], BF16, BB + 32 * KB + i * 16 * KB) for i in range(2)]
        wgb_alt = alloc("wgba", [128, CT * 512], BF16, SCR)
        wub_alt = alloc("wuba", [128, CT * 512], BF16, SCR + 16 * KB)
        fch = [alloc(f"fch{i}", [128, 8 * 512], BF16, BB + 64 * KB + i * 8 * KB) for i in range(2)]
        g_scr = [alloc(f"g_scr{i}", [128, 512], F32, BB + 80 * KB + i * 2 * KB) for i in range(2)]
        wdb = [alloc(f"wdb{i}", [128, ITT * 512], BF16, BB + i * 64 * KB) for i in range(2)]

        pb = [Bank(ctx.enter_context(nc.psum_tensor(f"pb{i}", [128, 512], F32)))
              for i in range(7)]
        ptr = Bank(ctx.enter_context(nc.psum_tensor("ptr", [128, 1024], BF16)))

        sx = Sem(nc, ctx, "sx")
        swt = Sem(nc, ctx, "swt")
        srr = Sem(nc, ctx, "srr")
        saq = Sem(nc, ctx, "saq")
        sgq = Sem(nc, ctx, "sgq")
        scc = Sem(nc, ctx, "scc")
        sPE = Sem(nc, ctx, "sPE")
        sDV = Sem(nc, ctx, "sDV")
        sAC = Sem(nc, ctx, "sAC")
        sout = Sem(nc, ctx, "sout")
        sspq = Sem(nc, ctx, "sspq")
        sspp = Sem(nc, ctx, "sspp")
        ssps = Sem(nc, ctx, "ssps")
        sx1w = Sem(nc, ctx, "sx1w")
        sfw = Sem(nc, ctx, "sfw")
        srrp = [Sem(nc, ctx, "srrp0"), Sem(nc, ctx, "srrp1")]
        srrs = [Sem(nc, ctx, "srrs0"), Sem(nc, ctx, "srrs1")]
        sPL = Sem(nc, ctx, "sPL")
        sftp = [Sem(nc, ctx, "sftp0"), Sem(nc, ctx, "sftp1")]

        block = ctx.enter_context(nc.Block())

        @block.gpsimd
        def _(g):
            g.dma_start(out=ident[:], in_=ident_h[:, :]).then_inc(sgq.h, 16)
            g.dma_start(out=ones[:], in_=ones_h[:, :]).then_inc(sgq.h, 16)
            g.dma_start(out=iota_rev[:], in_=iota_rev_h[:, :]).then_inc(sgq.h, 16)
            g.dma_start(out=mu["r"][:], in_=mu_r_pt[:, :]).then_inc(sgq.h, 16)
            g.dma_start(out=mu["k"][:], in_=mu_k_pt[:, :]).then_inc(sgq.h, 16)
            g.dma_start(out=mu["v"][:], in_=mu_v_pt[:, :]).then_inc(sgq.h, 16)
            g.dma_start(out=wraw[:], in_=w_pt[:, :]).then_inc(sgq.h, 16)
            g.dma_start(out=uraw[:], in_=u_pt[:, :]).then_inc(sgq.h, 16)
            g.dma_start(out=segc[:], in_=segc_h[:, :]).then_inc(sgq.h, 16)
            g.dma_start(out=sgate_v[:], in_=sgate_pt[:, :]).then_inc(sgq.h, 16)
            g.dma_start(out=sup_v[:], in_=sup_pt[:, :]).then_inc(sgq.h, 16)
            g.dma_start(out=sdown_v[:], in_=sdown_b[:, :]).then_inc(sgq.h, 16)
        SMALL_LD = sgq.tick(12 * 16)

        cast_done = {}
        cast_ct = {}
        def emit_cast(n, pe_gate=0, chunks=1):
            base = sgq.n
            @block.gpsimd
            def _(g):
                if pe_gate:
                    g.wait_ge(sPE.h, pe_gate)
                sgq.chain(g)
                if chunks == 1:
                    g.dma_start(out=wbf[n][:, :], in_=Wmats[n][:, :]).then_inc(sgq.h, 16)
                else:
                    cw = 2048 // chunks
                    for ci in range(chunks):
                        if ci:
                            g.wait_ge(sgq.h, base + 16 * ci)
                        g.dma_start(out=wbf[n][:, ci * cw:(ci + 1) * cw],
                                    in_=Wmats[n][:, ci * cw:(ci + 1) * cw]).then_inc(sgq.h, 16)
            if chunks > 1:
                cast_ct[n] = [base + 16 * (i + 1) for i in range(chunks)]
            cast_done[n] = sgq.tick(16 * chunks)
        emit_cast("k", chunks=4)

        @block.scalar
        def _(a):
            a.wait_ge(sgq.h, SMALL_LD)
            a.activation(out=tmp16a[:], in_=wraw[:], func=AF.Exp).then_inc(sAC.h, 1)
            a.activation(out=eu[:], in_=uraw[:], func=AF.Exp).then_inc(sAC.h, 1)
        AC_EXPW = sAC.tick(2)
        @block.vector
        def _(v):
            v.memset(e128[:], 2.718281828459045).then_inc(sDV.h, 1)
            v.memset(eps_t[:, 0:1], EPS).then_inc(sDV.h, 1)
            v.memset(e_t[:, 0:1], -1.0).then_inc(sDV.h, 1)
            v.wait_ge(sAC.h, AC_EXPW)
            v.tensor_scalar(out=w_eff[:], in0=tmp16a[:], scalar1=-1.0, scalar2=None,
                            op0=ALU.mult).then_inc(sDV.h, 1)
        sDV.tick(3)
        DV_WEFF = sDV.tick(1)
        @block.scalar
        def _(a):
            a.wait_ge(sDV.h, DV_WEFF)
            a.activation(out=d1[:], in_=w_eff[:], func=AF.Exp).then_inc(sAC.h, 1)
        AC_D1 = sAC.tick(1)
        @block.vector
        def _(v):
            v.wait_ge(sDV.h, DV_WEFF)
            for r in range(NCORES):
                v.tensor_scalar(out=coef_all[:, r * 16:(r + 1) * 16], in0=w_eff[:],
                                scalar1=segc[:, r:r + 1], scalar2=None,
                                op0=ALU.mult).then_inc(sDV.h, 1)
        DV_COEF_T = sDV.tick(NCORES)
        @block.gpsimd
        def _(g):
            g.wait_ge(sDV.h, DV_COEF_T)
            g.tensor_tensor(out=coef_all[:], in0=e128[:], in1=coef_all[:],
                            op=ALU.pow).then_inc(sPL.h, 1)
        PL_COEF = sPL.tick(1)

        hT_r = hT[:].rearrange("p (ct w) -> p ct w", ct=CT)
        ln1_dve_done = [0] * 9
        ln1_pe_last = [0] * 9
        x_loaded = [0] * 9
        def ln1_partA(a):
            rows = 128 if a < 8 else 1
            buf = x_t[a % 4]
            @block.sync
            def _(s, a=a, buf=buf, rows=rows):
                if a >= 4:
                    s.wait_ge(sDV.h, ln1_dve_done[a - 4])
                sx.chain(s)
                s.dma_start(out=buf[0:rows, :],
                            in_=x_seg[a * 128:a * 128 + rows, :]).then_inc(sx.h, 16)
            x_loaded[a] = sx.tick(16)
            so = (a % 2) * 8
            xa = buf[0:rows, :]
            @block.vector
            def _(v, a=a, xa=xa, rows=rows, so=so):
                r = slice(0, rows)
                v.wait_ge(sx.h, x_loaded[a])
                if a >= 2:
                    v.wait_ge(sDV.h, ln1_dve_done[a - 2])
                if a >= 1:
                    v.wait_ge(sDV.h, ln1_stt[a - 1])
                v.tensor_reduce(out=stat[r, so:so + 1], in_=xa,
                                axis=mybir.AxisListType.X, op=ALU.add).then_inc(sDV.h, 1)
                v.scalar_tensor_tensor(out=trash2[0:rows, :], in0=xa, scalar=1.0, in1=xa,
                                       op0=ALU.mult, op1=ALU.mult,
                                       accum_out=stat[r, so + 1:so + 2]).then_inc(sDV.h, 1)
                both = sDV.tick(2)
                ln1_stt[a] = both
                v.wait_ge(sDV.h, both)
                v.tensor_scalar(out=stat[r, so + 2:so + 3], in0=stat[r, so:so + 1],
                                scalar1=1.0 / C, scalar2=None, op0=ALU.mult).then_inc(sDV.h, 1)
                v.tensor_scalar(out=stat[r, so + 3:so + 4], in0=stat[r, so + 1:so + 2],
                                scalar1=1.0 / C, scalar2=None, op0=ALU.mult).then_inc(sDV.h, 1)
                m2 = sDV.tick(2)
                v.wait_ge(sDV.h, m2)
                v.tensor_tensor(out=stat[r, so + 4:so + 5], in0=stat[r, so + 2:so + 3],
                                in1=stat[r, so + 2:so + 3], op=ALU.mult).then_inc(sDV.h, 1)
                me = sDV.tick(1)
                v.wait_ge(sDV.h, me)
                v.tensor_tensor(out=stat[r, so + 5:so + 6], in0=stat[r, so + 3:so + 4],
                                in1=stat[r, so + 4:so + 5], op=ALU.subtract).then_inc(sDV.h, 1)
                sDV.tick(1)
            ln1_var[a] = sDV.n
            @block.scalar
            def _(a_, rows=rows, so=so, a=a):
                a_.wait_ge(sDV.h, ln1_var[a])
                a_.activation(out=stat[0:rows, so + 6:so + 7], in_=stat[0:rows, so + 5:so + 6],
                              func=AF.Sqrt, bias=eps_t[0:rows, 0:1]).then_inc(sAC.h, 1)
            ln1_sqrt[a] = sAC.tick(1)

        def ln1_partB(a):
            rows = 128 if a < 8 else 1
            buf = x_t[a % 4]
            so = (a % 2) * 8
            xa = buf[0:rows, :]
            @block.vector
            def _(v, a=a, xa=xa, rows=rows, so=so):
                r = slice(0, rows)
                v.wait_ge(sAC.h, ln1_sqrt[a])
                v.reciprocal(out=stat[r, so + 7:so + 8], in_=stat[r, so + 6:so + 7]).then_inc(sDV.h, 1)
                rst = sDV.tick(1)
                v.wait_ge(sDV.h, rst)
                if a >= 1:
                    v.wait_ge(sPE.h, ln1_pe_last[a - 1])
                v.tensor_scalar(out=h_bf[0:rows, :], in0=xa, scalar1=stat[r, so + 2:so + 3],
                                scalar2=stat[r, so + 7:so + 8], op0=ALU.subtract,
                                op1=ALU.mult).then_inc(sDV.h, 1)
                sDV.tick(1)
            dv_h = sDV.n
            if a < 8:
                for rnd in range(2):
                    @block.tensor
                    def _(t, a=a, rnd=rnd, dv_h=dv_h):
                        t.wait_ge(sDV.h, dv_h)
                        if a == 0 and rnd == 0:
                            t.wait_ge(sgq.h, SMALL_LD)
                        ptr.acquire(t)
                        for b in range(8):
                            ct = rnd * 8 + b
                            mm = t.transpose(ptr.t[:, b * 128:(b + 1) * 128],
                                             h_bf[:, ct * 128:(ct + 1) * 128], ident[:])
                            if b == 7:
                                mm.then_inc(sPE.h, 1)
                    pe_tr = sPE.tick(1)
                    @block.scalar
                    def _(a_, a=a, rnd=rnd, pe_tr=pe_tr):
                        a_.wait_ge(sPE.h, pe_tr)
                        a_.copy(out=hT_r[:, rnd * 8:(rnd + 1) * 8, a * 128 + 0:a * 128 + 128],
                                in_=ptr.t[:].rearrange("p (b w) -> p b w", b=8)).then_inc(sAC.h, 1)
                    ptr.release(sAC, sAC.tick(1))
                    ln1_pe_last[a] = sPE.n
            else:
                @block.tensor
                def _(t, dv_h=dv_h):
                    t.wait_ge(sDV.h, dv_h)
                    ptr.acquire(t)
                    for ct2 in range(16):
                        mm = t.transpose(ptr.t[:, 2 * ct2:2 * ct2 + 1],
                                         h_bf[0:1, ct2 * 128:(ct2 + 1) * 128], ident[0:1, 0:1])
                        if ct2 == 15:
                            mm.then_inc(sPE.h, 1)
                pe_tr = sPE.tick(1)
                @block.scalar
                def _(a_, pe_tr=pe_tr):
                    a_.wait_ge(sPE.h, pe_tr)
                    a_.copy(out=hT_r[:, :, 1024:1025],
                            in_=ptr.t[:].rearrange("p (b w) -> p b w", w=2)[:, 0:16, 0:1]
                            ).then_inc(sAC.h, 1)
                ptr.release(sAC, sAC.tick(1))
                ln1_pe_last[a] = sPE.n
            ln1_dve_done[a] = sDV.n

        ln1_var = [0] * 9
        ln1_sqrt = [0] * 9
        ln1_stt = [0] * 9
        for a in range(9):
            ln1_partA(a)
            if a >= 1:
                ln1_partB(a - 1)
        ln1_partB(8)
        LN1_DVE_END = sDV.n
        LN1_AC_END = sAC.n

        def emit_mixes(which, prev_pe_end):
            muv = mu[which]
            for ct in range(CT):
                @block.vector
                def _(v, ct=ct):
                    if ct == 0:
                        v.wait_ge(sDV.h, LN1_DVE_END)
                        v.wait_ge(sAC.h, LN1_AC_END)
                        if prev_pe_end:
                            v.wait_ge(sPE.h, prev_pe_end)
                    b0 = ct * (T + 1)
                    v.wait_ge(sDV.h, sDV.n)
                    v.tensor_tensor(out=delta[:], in0=hT[:, b0 + 1:b0 + T + 1],
                                    in1=hT[:, b0:b0 + T], op=ALU.subtract).then_inc(sDV.h, 1)
                    dd = sDV.tick(1)
                    v.wait_ge(sDV.h, dd)
                    v.scalar_tensor_tensor(out=mixT[:, ct * T:(ct + 1) * T], in0=delta[:],
                                           scalar=muv[:, ct:ct + 1], in1=hT[:, b0:b0 + T],
                                           op0=ALU.mult, op1=ALU.add).then_inc(sDV.h, 1)
                    sDV.tick(1)
            return sDV.n

        def emit_wt_loads(name, ready_after_pe):
            gates = cast_ct.get(name)
            @block.sync
            def _(s):
                if gates is None:
                    s.wait_ge(sgq.h, cast_done[name])
                if ready_after_pe:
                    s.wait_ge(sPE.h, ready_after_pe)
                swt.chain(s)
                for ct in range(CT):
                    if gates is not None:
                        s.wait_ge(sgq.h, gates[ct * len(gates) // CT])
                    s.dma_start(out=WT[:, ct * 2048:(ct + 1) * 2048],
                                in_=wbf[name][0:2048, ct * 128:(ct + 1) * 128],
                                transpose=True).then_inc(swt.h, 16)
            return swt.tick(16 * 16)

        mm_banks = [pb[0], pb[1], pb[2], pb[3]]

        def emit_proj_group(ht, mix_ready, wt_gates, first):
            res = []
            @block.tensor
            def _(t, ht=ht):
                if first:
                    t.wait_ge(sDV.h, mix_ready)
                if ht in wt_gates:
                    t.wait_ge(swt.h, wt_gates[ht])
                for tc in range(2):
                    bk = mm_banks[(ht * 2 + tc) % 4]
                    bk.acquire(t)
                    for ct in range(CT):
                        mm = t.matmul(bk.t[:],
                                      WT[:, ct * 2048 + ht * 128: ct * 2048 + (ht + 1) * 128],
                                      mixT[:, ct * T + tc * 512: ct * T + tc * 512 + 512],
                                      start=(ct == 0), stop=(ct == CT - 1))
                        if ct == CT - 1:
                            mm.then_inc(sPE.h, 1)
                    res.append((bk, sPE.tick(1)))
            return res

        mixk_done = emit_mixes("k", 0)
        wtk_done = emit_wt_loads("k", 0)
        KG_FIRST = [0]
        q_spill_done = [0] * HT
        qbuf_free = [[], []]
        dpow_free = [[], []]
        for ht in range(HT):
            grp = emit_proj_group(ht, mixk_done, {0: wtk_done}, ht == 0)
            if ht == 0:
                KG_FIRST[0] = sPE.n
                emit_cast("v", pe_gate=KG_FIRST[0], chunks=2)
                emit_cast("r", pe_gate=KG_FIRST[0], chunks=2)
            qb = q_t[ht % 2]
            dp = dpow[ht % 2]
            @block.scalar
            def _(a_, ht=ht, qb=qb, grp=grp):
                wait_all(a_, qbuf_free[ht % 2])
                a_.wait_ge(sPE.h, grp[0][1])
                a_.activation(out=qb[:, 0:512], in_=grp[0][0].t[:], func=AF.Exp).then_inc(sAC.h, 1)
                grp[0][0].release(sAC, sAC.tick(1))
                a_.wait_ge(sPE.h, grp[1][1])
                a_.activation(out=qb[:, 512:1024], in_=grp[1][0].t[:], func=AF.Exp).then_inc(sAC.h, 1)
                grp[1][0].release(sAC, sAC.tick(1))
            ac_q = sAC.n
            @block.sync
            def _(s, ht=ht, qb=qb, ac_q=ac_q):
                s.wait_ge(sAC.h, ac_q)
                sspq.chain(s)
                s.dma_start(out=spill_q[ht * 128:(ht + 1) * 128, :], in_=qb[:]).then_inc(sspq.h, 16)
            q_spill_done[ht] = sspq.tick(16)
            @block.scalar
            def _(a_, ht=ht, dp=dp):
                wait_all(a_, dpow_free[ht % 2])
                if ht < 2:
                    a_.wait_ge(sAC.h, AC_D1)
                    a_.wait_ge(sDV.h, LN1_DVE_END)
                a_.activation(out=dp[:], in_=iota_rev[:], func=AF.Exp,
                              scale=w_eff[:, ht:ht + 1]).then_inc(sAC.h, 1)
            ac_dp = sAC.tick(1)
            @block.vector
            def _(v, ht=ht, qb=qb, dp=dp, ac_q=ac_q, ac_dp=ac_dp):
                v.wait_ge(sAC.h, max(ac_q, ac_dp))
                v.wait_ge(sDV.h, sDV.n)
                v.scalar_tensor_tensor(out=trash_bf[:], in0=qb[:], scalar=1.0, in1=dp[:],
                                       op0=ALU.mult, op1=ALU.mult,
                                       accum_out=summ[:, HT + ht:HT + ht + 1]).then_inc(sDV.h, 1)
            dv_stt = sDV.tick(1)
            qbuf_free[ht % 2] = [(sDV, dv_stt), (sspq, q_spill_done[ht])]
            dpow_free[ht % 2] = [(sDV, dv_stt)]
        K_DVE_END = sDV.n
        K_PE_END = sPE.n
        K_SPILL_END = sspq.n

        mixv_done = emit_mixes("v", K_PE_END)
        wtv_done = emit_wt_loads("v", K_PE_END)
        VG_FIRST = [0]
        p_spill_done = [0] * HT
        pbuf_free = [[], []]
        qrr_free = [[(sspq, K_SPILL_END)], [(sspq, K_SPILL_END)]]
        qrr_loaded = [0] * HT
        for ht in range(HT):
            grp = emit_proj_group(ht, mixv_done, {0: wtv_done}, ht == 0)
            if ht == 0:
                VG_FIRST[0] = sPE.n
                emit_cast("o", pe_gate=VG_FIRST[0])
            @block.sync
            def _(s, ht=ht):
                wait_all(s, qrr_free[ht % 2])
                if ht == 0:
                    s.wait_ge(sspq.h, K_SPILL_END)
                srr.chain(s)
                s.dma_start(out=qrr[ht % 2][:], in_=spill_q[ht * 128:(ht + 1) * 128, :]
                            ).then_inc(srr.h, 16)
            qrr_loaded[ht] = srr.tick(16)
            pbx = p_t[ht % 2]
            dp = dpow[ht % 2]
            @block.vector
            def _(v, ht=ht, pbx=pbx, grp=grp):
                wait_all(v, pbuf_free[ht % 2])
                v.wait_ge(srr.h, qrr_loaded[ht])
                v.wait_ge(sPE.h, grp[0][1])
                v.tensor_tensor(out=pbx[:, 0:512], in0=qrr[ht % 2][:, 0:512], in1=grp[0][0].t[:],
                                op=ALU.mult).then_inc(sDV.h, 1)
                grp[0][0].release(sDV, sDV.tick(1))
                v.wait_ge(sPE.h, grp[1][1])
                v.tensor_tensor(out=pbx[:, 512:1024], in0=qrr[ht % 2][:, 512:1024],
                                in1=grp[1][0].t[:], op=ALU.mult).then_inc(sDV.h, 1)
                grp[1][0].release(sDV, sDV.tick(1))
            qrr_free[ht % 2] = [(sDV, sDV.n)]
            dv_p = sDV.n
            @block.scalar
            def _(a_, ht=ht, dp=dp, dv_p=dv_p, pbx=pbx):
                wait_all(a_, dpow_free[ht % 2])
                a_.activation(out=dp[:], in_=iota_rev[:], func=AF.Exp,
                              scale=w_eff[:, ht:ht + 1]).then_inc(sAC.h, 1)
                a_.wait_ge(sDV.h, dv_p)
                sspp.chain(a_)
                a_.dma_start(out=spill_p[ht * 128:(ht + 1) * 128, :], in_=pbx[:]).then_inc(sspp.h, 16)
            ac_dp = sAC.tick(1)
            p_spill_done[ht] = sspp.tick(16)
            @block.vector
            def _(v, ht=ht, pbx=pbx, dp=dp, ac_dp=ac_dp):
                v.wait_ge(sAC.h, ac_dp)
                v.wait_ge(sDV.h, sDV.n)
                v.scalar_tensor_tensor(out=trash_bf[:], in0=pbx[:], scalar=1.0, in1=dp[:],
                                       op0=ALU.mult, op1=ALU.mult,
                                       accum_out=summ[:, ht:ht + 1]).then_inc(sDV.h, 1)
            dv_stt = sDV.tick(1)
            pbuf_free[ht % 2] = [(sDV, dv_stt), (sspp, p_spill_done[ht])]
            dpow_free[ht % 2] = [(sDV, dv_stt)]
        V_DVE_END = sDV.n
        V_PE_END = sPE.n

        @block.gpsimd
        def _(g):
            g.wait_ge(sDV.h, V_DVE_END)
            sgq.chain(g)
            g.dma_start(out=cc_in[:, :], in_=summ[:]).then_inc(sgq.h, 16)
            ccin_ld = sgq.tick(16)
            g.wait_ge(sgq.h, ccin_ld)
            g.collective_compute(
                "AllGather", ALU.bypass,
                replica_groups=[list(range(NCORES))],
                ins=[cc_in.ap().opt()],
                outs=[cc_out.ap().opt()],
            ).then_inc(scc.h, 1)
            g.wait_ge(scc.h, scc.tick(1))
            g.dma_start(out=gath[:].rearrange("p (r f) -> p r f", r=NCORES),
                        in_=cc_out.ap().rearrange("(r p) f -> p r f", p=128)).then_inc(sgq.h, 16)
        GATH_LD = sgq.tick(16)

        mixr_done = emit_mixes("r", V_PE_END)
        wtr_done = emit_wt_loads("r", V_PE_END)
        RG_FIRST = [0]
        @block.vector
        def _(v):
            v.wait_ge(sgq.h, GATH_LD)
            v.wait_ge(sPL.h, PL_COEF)
            v.wait_ge(sDV.h, sDV.n)
            cview = coef_all[:].rearrange("p (r i) -> p i r", r=NCORES)
            gA = gath[:].rearrange("p (r f) -> p f r", r=NCORES)[:, 0:HT, :]
            gB = gath[:].rearrange("p (r f) -> p f r", r=NCORES)[:, HT:2 * HT, :]
            t3 = tmp3[:].rearrange("p (i r) -> p i r", r=NCORES)
            v.tensor_tensor(out=t3, in0=cview, in1=gA, op=ALU.mult).then_inc(sDV.h, 1)
            pA = sDV.tick(1)
            v.wait_ge(sDV.h, pA)
            v.tensor_reduce(out=AinV[:], in_=t3, axis=mybir.AxisListType.X,
                            op=ALU.add).then_inc(sDV.h, 1)
            pA2 = sDV.tick(1)
            v.wait_ge(sDV.h, pA2)
            v.tensor_tensor(out=t3, in0=cview, in1=gB, op=ALU.mult).then_inc(sDV.h, 1)
            pB = sDV.tick(1)
            v.wait_ge(sDV.h, pB)
            v.tensor_reduce(out=BinV[:], in_=t3, axis=mybir.AxisListType.X,
                            op=ALU.add).then_inc(sDV.h, 1)
            sDV.tick(1)
        CARRY_DONE = sDV.n
        s_spill_done = [0] * HT
        sgbuf_free = [[], []]
        for ht in range(HT):
            grp = emit_proj_group(ht, mixr_done, {0: wtr_done}, ht == 0)
            if ht == 0:
                RG_FIRST[0] = sPE.n
            sbx = sig_t
            @block.scalar
            def _(a_, ht=ht, sbx=sbx, grp=grp):
                wait_all(a_, sgbuf_free[0])
                if ht == 0:
                    wait_all(a_, pbuf_free[0])
                    wait_all(a_, pbuf_free[1])
                a_.wait_ge(sPE.h, grp[0][1])
                a_.activation(out=sbx[:, 0:512], in_=grp[0][0].t[:], func=AF.Sigmoid).then_inc(sAC.h, 1)
                grp[0][0].release(sAC, sAC.tick(1))
                a_.wait_ge(sPE.h, grp[1][1])
                a_.activation(out=sbx[:, 512:1024], in_=grp[1][0].t[:], func=AF.Sigmoid).then_inc(sAC.h, 1)
                grp[1][0].release(sAC, sAC.tick(1))
            ac_s = sAC.n
            @block.scalar
            def _(a_, ht=ht, sbx=sbx, ac_s=ac_s):
                a_.wait_ge(sAC.h, ac_s)
                ssps.chain(a_)
                a_.dma_start(out=spill_s[ht * 128:(ht + 1) * 128, :], in_=sbx[:]).then_inc(ssps.h, 16)
            s_spill_done[ht] = ssps.tick(16)
            sgbuf_free[0] = [(ssps, s_spill_done[ht])]
        R_PE_END = sPE.n
        SPILLQ_ALL = sspq.n
        SPILLP_ALL = sspp.n
        SPILLS_ALL = ssps.n


        rrqp_free = [[(sspq, SPILLQ_ALL), (sspp, SPILLP_ALL)],
                     [(sspq, SPILLQ_ALL), (sspp, SPILLP_ALL)]]
        rrs_free = [[], []]
        scan_first = [True]

        def scan_block(k, ch, ht):
            pp = ht % 2
            sem = srrp[pp]
            @block.sync
            def _(s, ch=ch, ht=ht, pp=pp, sem=sem):
                wait_all(s, rrqp_free[pp])
                cs_ = slice(ch * 512, ch * 512 + 512)
                rw = slice(ht * 128, (ht + 1) * 128)
                s.dma_start(out=rrq[pp][:], in_=spill_q[rw, cs_]).then_inc(sem.h, 16)
                s.dma_start(out=rrp[pp][:], in_=spill_p[rw, cs_]).then_inc(sem.h, 16)
                wait_all(s, rrs_free[pp])
                s.wait_ge(ssps.h, s_spill_done[ht])
                s.dma_start(out=rrs[pp][:], in_=spill_s[rw, cs_]).then_inc(srrs[pp].h, 16)
            rr_ld = sem.tick(32)
            rs_ld = srrs[pp].tick(16)
            @block.vector
            def _(v, k=k, ch=ch, ht=ht, pp=pp, rr_ld=rr_ld, rs_ld=rs_ld, sem=sem):
                if scan_first[0]:
                    v.wait_ge(sDV.h, CARRY_DONE)
                    v.wait_ge(sAC.h, AC_D1)
                    v.wait_ge(sgq.h, SMALL_LD)
                    scan_first[0] = False
                v.wait_ge(sem.h, rr_ld)
                v.wait_ge(sDV.h, sDV.n)
                v.tensor_scalar(out=d_bc[:], in0=ones[:], scalar1=d1[:, ht:ht + 1],
                                scalar2=None, op0=ALU.mult).then_inc(sDV.h, 1)
                srcA = AinV if ch == 0 else Acar
                srcB = BinV if ch == 0 else Bcar
                v.tensor_scalar(out=Abuf[:, 0:1], in0=srcA[:, ht:ht + 1], scalar1=0.0,
                                scalar2=None, op0=ALU.add).then_inc(sDV.h, 1)
                v.tensor_scalar(out=Bbuf[:, 0:1], in0=srcB[:, ht:ht + 1], scalar1=0.0,
                                scalar2=None, op0=ALU.add).then_inc(sDV.h, 1)
                pre = sDV.tick(3)
                v.wait_ge(sDV.h, pre)
                v.tensor_tensor_scan(out=Abuf[:, 1:513], data0=d_bc[:], data1=rrp[pp][:],
                                     initial=Abuf[:, 0:1], op0=ALU.mult,
                                     op1=ALU.add).then_inc(sDV.h, 1)
                v.tensor_tensor_scan(out=Bbuf[:, 1:513], data0=d_bc[:], data1=rrq[pp][:],
                                     initial=Bbuf[:, 0:1], op0=ALU.mult,
                                     op1=ALU.add).then_inc(sDV.h, 1)
                sc = sDV.tick(2)
                v.wait_ge(sDV.h, sc)
                if ch == 0:
                    v.tensor_scalar(out=Acar[:, ht:ht + 1], in0=Abuf[:, 512:513], scalar1=0.0,
                                    scalar2=None, op0=ALU.add).then_inc(sDV.h, 1)
                    v.tensor_scalar(out=Bcar[:, ht:ht + 1], in0=Bbuf[:, 512:513], scalar1=0.0,
                                    scalar2=None, op0=ALU.add).then_inc(sDV.h, 1)
                    sDV.tick(2)
                v.scalar_tensor_tensor(out=nbuf[:], in0=rrp[pp][:], scalar=eu[:, ht:ht + 1],
                                       in1=Abuf[:, 0:512], op0=ALU.mult,
                                       op1=ALU.add).then_inc(sDV.h, 1)
                v.scalar_tensor_tensor(out=dbuf[:], in0=rrq[pp][:], scalar=eu[:, ht:ht + 1],
                                       in1=Bbuf[:, 0:512], op0=ALU.mult,
                                       op1=ALU.add).then_inc(sDV.h, 1)
                nd = sDV.tick(2)
                rrqp_free[pp] = [(sDV, nd)]
                v.wait_ge(sDV.h, nd)
                v.reciprocal(out=rec[:], in_=dbuf[:]).then_inc(sDV.h, 1)
                rc = sDV.tick(1)
                v.wait_ge(sDV.h, rc)
                v.tensor_tensor(out=dbuf[:], in0=nbuf[:], in1=rec[:],
                                op=ALU.mult).then_inc(sDV.h, 1)
                yy = sDV.tick(1)
                v.wait_ge(sDV.h, yy)
                v.wait_ge(srrs[pp].h, rs_ld)
                v.tensor_tensor(out=s_buf[:, ht * T + ch * 512: ht * T + ch * 512 + 512],
                                in0=dbuf[:], in1=rrs[pp][:], op=ALU.mult).then_inc(sDV.h, 1)
                sDV.tick(1)
            rrs_free[pp] = [(sDV, sDV.n)]

        ch1_pend = [None]
        s_done_ch1 = [0] * HT

        def emit_pending(v):
            ht_p, pp_p, ac_p, rs_p = ch1_pend[0]
            v.wait_ge(sAC.h, ac_p)
            v.tensor_tensor(out=dbuf[:], in0=nbuf[:], in1=rec[:],
                            op=ALU.mult).then_inc(sDV.h, 1)
            yy = sDV.tick(1)
            v.wait_ge(sDV.h, yy)
            v.wait_ge(srrs[pp_p].h, rs_p)
            v.tensor_tensor(out=s_buf[:, ht_p * T + 512: ht_p * T + 1024],
                            in0=dbuf[:], in1=rrs[pp_p][:], op=ALU.mult).then_inc(sDV.h, 1)
            sDV.tick(1)
            s_done_ch1[ht_p] = sDV.n
            rrs_free[pp_p] = [(sDV, sDV.n)]
            ch1_pend[0] = None

        def scan_block_ch1(ht):
            pp = ht % 2
            sem = srrp[pp]
            @block.sync
            def _(s, ht=ht, pp=pp, sem=sem):
                wait_all(s, rrqp_free[pp])
                cs_ = slice(512, 1024)
                rw = slice(ht * 128, (ht + 1) * 128)
                s.dma_start(out=rrq[pp][:], in_=spill_q[rw, cs_]).then_inc(sem.h, 16)
                s.dma_start(out=rrp[pp][:], in_=spill_p[rw, cs_]).then_inc(sem.h, 16)
                wait_all(s, rrs_free[pp])
                s.dma_start(out=rrs[pp][:], in_=spill_s[rw, cs_]).then_inc(srrs[pp].h, 16)
            rr_ld = sem.tick(32)
            rs_ld = srrs[pp].tick(16)
            @block.vector
            def _(v, ht=ht, pp=pp, rr_ld=rr_ld, sem=sem):
                v.wait_ge(sem.h, rr_ld)
                v.wait_ge(sDV.h, sDV.n)
                v.tensor_scalar(out=d_bc[:], in0=ones[:], scalar1=d1[:, ht:ht + 1],
                                scalar2=None, op0=ALU.mult).then_inc(sDV.h, 1)
                v.tensor_scalar(out=Abuf[:, 0:1], in0=Acar[:, ht:ht + 1], scalar1=0.0,
                                scalar2=None, op0=ALU.add).then_inc(sDV.h, 1)
                v.tensor_scalar(out=Bbuf[:, 0:1], in0=Bcar[:, ht:ht + 1], scalar1=0.0,
                                scalar2=None, op0=ALU.add).then_inc(sDV.h, 1)
                pre = sDV.tick(3)
                v.wait_ge(sDV.h, pre)
                v.tensor_tensor_scan(out=Abuf[:, 1:513], data0=d_bc[:], data1=rrp[pp][:],
                                     initial=Abuf[:, 0:1], op0=ALU.mult,
                                     op1=ALU.add).then_inc(sDV.h, 1)
                v.tensor_tensor_scan(out=Bbuf[:, 1:513], data0=d_bc[:], data1=rrq[pp][:],
                                     initial=Bbuf[:, 0:1], op0=ALU.mult,
                                     op1=ALU.add).then_inc(sDV.h, 1)
                sc = sDV.tick(2)
                if ch1_pend[0] is not None:
                    emit_pending(v)
                v.wait_ge(sDV.h, sDV.n)
                v.scalar_tensor_tensor(out=nbuf[:], in0=rrp[pp][:], scalar=eu[:, ht:ht + 1],
                                       in1=Abuf[:, 0:512], op0=ALU.mult,
                                       op1=ALU.add).then_inc(sDV.h, 1)
                v.scalar_tensor_tensor(out=dbuf[:], in0=rrq[pp][:], scalar=eu[:, ht:ht + 1],
                                       in1=Bbuf[:, 0:512], op0=ALU.mult,
                                       op1=ALU.add).then_inc(sDV.h, 1)
                sDV.tick(2)
            nd = sDV.n
            lnt = rrq[pp]
            @block.scalar
            def _(a_, nd=nd, lnt=lnt):
                a_.wait_ge(sDV.h, nd)
                a_.activation(out=lnt[:], in_=dbuf[:], func=AF.Ln).then_inc(sAC.h, 1)
                a_.wait_ge(sAC.h, sAC.tick(1))
                a_.activation(out=rec[:], in_=lnt[:], func=AF.Exp,
                              scale=e_t[:, 0:1]).then_inc(sAC.h, 1)
            ac2 = sAC.tick(1)
            rrqp_free[pp] = [(sDV, nd), (sAC, ac2)]
            ch1_pend[0] = (ht, pp, ac2, rs_ld)

        h2T_r = h2T[:].rearrange("p (ct w) -> p ct w", ct=CT)
        wo_banks = [pb[0], pb[1], pb[2], pb[3]]
        ph4_pe_last = [0] * 8
        x1_free = [[(sDV, 0)]]
        wo_x1_done = [0] * 8
        h2t_done_tt = [0] * 8

        def wo_mm_part(tt, s_done):
            @block.tensor
            def _(t, tt=tt, s_done=s_done):
                for cs in range(4):
                    wo_banks[cs].acquire(t)
                for ht2 in range(HT):
                    t.wait_ge(sDV.h, s_done[ht2])
                    t.wait_ge(swt.h, WO_LD_HT[ht2])
                    for cs in range(4):
                        mm = t.matmul(wo_banks[cs].t[:],
                                      s_buf[:, ht2 * T + tt * 128: ht2 * T + (tt + 1) * 128],
                                      WoT[:, ht2 * 2048 + cs * 512: ht2 * 2048 + cs * 512 + 512],
                                      start=(ht2 == 0), stop=(ht2 == HT - 1))
                        if ht2 == HT - 1:
                            mm.then_inc(sPE.h, 1)
            return [(wo_banks[cs], sPE.tick(1)) for cs in range(4)]

        def wo_evac_part(tt, wo_g):
            x1b = x1_t[0]
            @block.sync
            def _(s, tt=tt, x1b=x1b):
                wait_all(s, x1_free[0])
                sx.chain(s)
                s.dma_start(out=x1b[:],
                            in_=x_seg[1 + tt * 128:1 + (tt + 1) * 128, :]).then_inc(sx.h, 16)
            x_ld = sx.tick(16)
            for cs in range(4):
                bk, pev = wo_g[cs]
                @block.vector
                def _(v, cs=cs, bk=bk, pev=pev, x1b=x1b, x_ld=x_ld):
                    if cs == 0:
                        v.wait_ge(sx.h, x_ld)
                        v.wait_ge(sDV.h, sDV.n)
                    v.wait_ge(sPE.h, pev)
                    v.tensor_tensor(out=x1b[:, cs * 512:cs * 512 + 512],
                                    in0=x1b[:, cs * 512:cs * 512 + 512],
                                    in1=bk.t[:], op=ALU.add).then_inc(sDV.h, 1)
                bk.release(sDV, sDV.tick(1))
            wo_x1_done[tt] = sDV.n

        ln2_dvh2 = [0] * 8

        def ln2_math(tt):
            x1b = x1_t[0]
            so = (tt % 2) * 8
            @block.scalar
            def _(a_, x1b=x1b, so=so, tt=tt):
                a_.wait_ge(sDV.h, wo_x1_done[tt])
                a_.wait_ge(sAC.h, sAC.n)
                if tt >= 1:
                    a_.wait_ge(sPE.h, ph4_pe_last[tt - 1])
                a_.activation(out=h2_bf[:], in_=x1b[:], func=AF.Square,
                              accum_out=stat[:, so + 1:so + 2]).then_inc(sAC.h, 1)
            ac_sq2 = sAC.tick(1)
            @block.vector
            def _(v, x1b=x1b, so=so, tt=tt, ac_sq2=ac_sq2):
                v.wait_ge(sDV.h, wo_x1_done[tt])
                v.tensor_reduce(out=stat[:, so:so + 1], in_=x1b[:], axis=mybir.AxisListType.X,
                                op=ALU.add).then_inc(sDV.h, 1)
                both = sDV.tick(1)
                v.wait_ge(sDV.h, both)
                v.wait_ge(sAC.h, ac_sq2)
                v.tensor_scalar(out=stat[:, so + 2:so + 3], in0=stat[:, so:so + 1],
                                scalar1=1.0 / C, scalar2=None, op0=ALU.mult).then_inc(sDV.h, 1)
                v.tensor_scalar(out=stat[:, so + 3:so + 4], in0=stat[:, so + 1:so + 2],
                                scalar1=1.0 / C, scalar2=None, op0=ALU.mult).then_inc(sDV.h, 1)
                m2 = sDV.tick(2)
                v.wait_ge(sDV.h, m2)
                v.tensor_tensor(out=stat[:, so + 4:so + 5], in0=stat[:, so + 2:so + 3],
                                in1=stat[:, so + 2:so + 3], op=ALU.mult).then_inc(sDV.h, 1)
                me = sDV.tick(1)
                v.wait_ge(sDV.h, me)
                v.tensor_tensor(out=stat[:, so + 5:so + 6], in0=stat[:, so + 3:so + 4],
                                in1=stat[:, so + 4:so + 5], op=ALU.subtract).then_inc(sDV.h, 1)
                sDV.tick(1)
            dv_var = sDV.n
            @block.scalar
            def _(a_, tt=tt, x1b=x1b, so=so, dv_var=dv_var):
                a_.wait_ge(sDV.h, dv_var)
                a_.activation(out=stat[:, so + 6:so + 7], in_=stat[:, so + 5:so + 6],
                              func=AF.Sqrt, bias=eps_t[:, 0:1]).then_inc(sAC.h, 1)
                sx1w.chain(a_)
                a_.dma_start(out=x1_hbm[tt * 128:(tt + 1) * 128, :], in_=x1b[:]).then_inc(sx1w.h, 16)
            ac_sqrt = sAC.tick(1)
            x1_sp = sx1w.tick(16)
            @block.vector
            def _(v, tt=tt, x1b=x1b, so=so, ac_sqrt=ac_sqrt):
                v.wait_ge(sAC.h, ac_sqrt)
                v.reciprocal(out=stat[:, so + 7:so + 8], in_=stat[:, so + 6:so + 7]).then_inc(sDV.h, 1)
                rst = sDV.tick(1)
                v.wait_ge(sDV.h, rst)
                if tt >= 1:
                    v.wait_ge(sPE.h, ph4_pe_last[tt - 1])
                v.tensor_scalar(out=h2_bf[:], in0=x1b[:], scalar1=stat[:, so + 2:so + 3],
                                scalar2=stat[:, so + 7:so + 8], op0=ALU.subtract,
                                op1=ALU.mult).then_inc(sDV.h, 1)
                sDV.tick(1)
            dv_h2 = sDV.n
            ln2_dvh2[tt] = dv_h2
            x1_free[0] = [(sDV, dv_h2), (sx1w, x1_sp)]

        def ln2_tr(tt):
            dv_h2 = ln2_dvh2[tt]
            for rnd in range(2):
                @block.tensor
                def _(t, rnd=rnd, dv_h2=dv_h2):
                    t.wait_ge(sDV.h, dv_h2)
                    ptr.acquire(t)
                    for b in range(8):
                        ct2 = rnd * 8 + b
                        mm = t.transpose(ptr.t[:, b * 128:(b + 1) * 128],
                                         h2_bf[:, ct2 * 128:(ct2 + 1) * 128], ident[:])
                        if b == 7:
                            mm.then_inc(sPE.h, 1)
                pe_tr = sPE.tick(1)
                @block.scalar
                def _(a_, tt=tt, rnd=rnd, pe_tr=pe_tr):
                    a_.wait_ge(sPE.h, pe_tr)
                    a_.copy(out=h2T_r[:, rnd * 8:(rnd + 1) * 8, tt * 128:(tt + 1) * 128],
                            in_=ptr.t[:].rearrange("p (b w) -> p b w", b=8)).then_inc(sAC.h, 1)
                ptr.release(sAC, sAC.tick(1))
                ph4_pe_last[tt] = sPE.n
            h2t_done_tt[tt] = sAC.n

        s_done_ch0 = [0] * HT
        for ht in range(HT):
            scan_block(ht, 0, ht)
            s_done_ch0[ht] = sDV.n
        WO_B = swt.n
        @block.scalar
        def _(a_):
            a_.wait_ge(sgq.h, cast_done["o"])
            a_.wait_ge(sPE.h, R_PE_END)
            swt.chain(a_)
            for ht2 in range(HT):
                if ht2:
                    a_.wait_ge(swt.h, WO_B + 16 * ht2)
                a_.dma_start(out=WoT[:, ht2 * 2048:(ht2 + 1) * 2048],
                             in_=wbf["o"][0:2048, ht2 * 128:(ht2 + 1) * 128],
                             transpose=True).then_inc(swt.h, 16)
        swt.tick(16 * 16)
        WO_LD_HT = [WO_B + 16 * (i + 1) for i in range(HT)]

        for tt in range(4):
            wo_g = wo_mm_part(tt, s_done_ch0)
            if tt >= 1:
                ln2_tr(tt - 1)
            wo_evac_part(tt, wo_g)
            ln2_math(tt)
            for j in range(4):
                scan_block_ch1(tt * 4 + j)
        @block.vector
        def _(v):
            emit_pending(v)
        for tt in range(4, 8):
            wo_g = wo_mm_part(tt, s_done_ch1)
            ln2_tr(tt - 1)
            wo_evac_part(tt, wo_g)
            ln2_math(tt)
        ln2_tr(7)

        H2T_DONE = sAC.n
        WO_PE_END = sPE.n
        X1_SPILL_END = sx1w.n
        PH4_DVE_END = sDV.n
        NCH = 16
        wgbuf_free = [[(sPE, WO_PE_END)], [(sPE, WO_PE_END)]]
        fch_free = [[(sPE, WO_PE_END)], [(sPE, WO_PE_END)]]
        gscr_free = [[(sPE, WO_PE_END)], [(sPE, WO_PE_END)]]
        ffn_banks = [pb[0], pb[1], pb[2], pb[3], pb[4], pb[5]]
        bank_i = [0]
        gscr_i = [0]
        fuse_wr = [0] * NCH
        wg_ld = [0] * NCH
        for ich in range(NCH):
            pp = ich % 2
            wgx = wgb[pp] if ich < NCH - 1 else wgb_alt
            wux = wub[pp] if ich < NCH - 1 else wub_alt
            @block.gpsimd
            def _(g, ich=ich, pp=pp, wgx=wgx, wux=wux):
                if ich < NCH - 1:
                    wait_all(g, wgbuf_free[pp])
                else:
                    g.wait_ge(sDV.h, PH4_DVE_END)
                    g.wait_ge(sx1w.h, X1_SPILL_END)
                if ich < 2:
                    g.wait_ge(sPE.h, WO_PE_END)
                sgq.chain(g)
                isl = slice(ich * 512, (ich + 1) * 512)
                g.dma_start(out=wgx[:].rearrange("p (ct i) -> p ct i", ct=CT),
                            in_=wg8[:, isl].rearrange("(ct p) i -> p ct i", p=128)
                            ).then_inc(sgq.h, 16)
                g.dma_start(out=wux[:].rearrange("p (ct i) -> p ct i", ct=CT),
                            in_=wu8[:, isl].rearrange("(ct p) i -> p ct i", p=128)
                            ).then_inc(sgq.h, 16)
            wg_ld[ich] = sgq.tick(32)
            fb = fch[pp]
            for tc in range(2):
                for im in range(4):
                    bg = ffn_banks[bank_i[0] % 6]; bank_i[0] += 1
                    bu = ffn_banks[bank_i[0] % 6]; bank_i[0] += 1
                    it_g = ich * 4 + im
                    pe_cnt = []
                    @block.tensor
                    def _(t, ich=ich, tc=tc, im=im, bg=bg, bu=bu, pp=pp, pe_cnt=pe_cnt):
                        if ich == 0 and tc == 0 and im == 0:
                            t.wait_ge(sAC.h, H2T_DONE)
                        if tc == 0 and im == 0:
                            t.wait_ge(sgq.h, wg_ld[ich])
                        wgx = wgb[pp] if ich < NCH - 1 else wgb_alt
                        wux = wub[pp] if ich < NCH - 1 else wub_alt
                        bg.acquire(t)
                        for ct in range(CT):
                            mm = t.matmul(bg.t[:],
                                          wgx[:, ct * 512 + im * 128: ct * 512 + (im + 1) * 128],
                                          h2T[:, ct * T + tc * 512: ct * T + tc * 512 + 512],
                                          start=(ct == 0), stop=(ct == CT - 1))
                            if ct == CT - 1:
                                mm.then_inc(sPE.h, 1)
                        pe_cnt.append(sPE.tick(1))
                        bu.acquire(t)
                        for ct in range(CT):
                            mm = t.matmul(bu.t[:],
                                          wux[:, ct * 512 + im * 128: ct * 512 + (im + 1) * 128],
                                          h2T[:, ct * T + tc * 512: ct * T + tc * 512 + 512],
                                          start=(ct == 0), stop=(ct == CT - 1))
                            if ct == CT - 1:
                                mm.then_inc(sPE.h, 1)
                        pe_cnt.append(sPE.tick(1))
                    gpe, upe = pe_cnt
                    gi = gscr_i[0] % 2; gscr_i[0] += 1
                    gsc = g_scr[gi]
                    @block.scalar
                    def _(a_, it_g=it_g, gpe=gpe, gsc=gsc, bg=bg, gi=gi):
                        wait_all(a_, gscr_free[gi])
                        a_.wait_ge(sPE.h, gpe)
                        a_.activation(out=gsc[:], in_=bg.t[:], func=AF.Silu,
                                      scale=sgate_v[:, it_g:it_g + 1]).then_inc(sAC.h, 1)
                    bg.release(sAC, sAC.tick(1))
                    ac_g = sAC.n
                    @block.vector
                    def _(v, tc=tc, im=im, it_g=it_g, upe=upe, ac_g=ac_g,
                          gsc=gsc, bu=bu, fb=fb, pp=pp):
                        if tc == 0 and im == 0:
                            wait_all(v, fch_free[pp])
                        v.wait_ge(sPE.h, upe)
                        v.wait_ge(sAC.h, ac_g)
                        slot = im * 2 + tc
                        v.scalar_tensor_tensor(out=fb[:, slot * 512:(slot + 1) * 512],
                                               in0=bu.t[:], scalar=sup_v[:, it_g:it_g + 1],
                                               in1=gsc[:], op0=ALU.mult,
                                               op1=ALU.mult).then_inc(sDV.h, 1)
                    bu.release(sDV, sDV.tick(1))
                    gscr_free[gi] = [(sDV, sDV.n)]
            dv_f = sDV.n
            @block.scalar
            def _(a_, ich=ich, pp=pp, dv_f=dv_f, fb=fb):
                a_.wait_ge(sDV.h, dv_f)
                dst = fuse_hbm[ich * 512:(ich + 1) * 512, :].rearrange(
                    "(im p) (tc t) -> p im tc t", p=128, t=512)
                sfw.chain(a_)
                a_.dma_start(out=dst, in_=fb[:].rearrange("p (im tc t) -> p im tc t",
                                                          im=4, tc=2)).then_inc(sfw.h, 16)
            fuse_wr[ich] = sfw.tick(16)
            fch_free[pp] = [(sfw, fuse_wr[ich])]
            wgbuf_free[pp] = [(sPE, sPE.n)]
            if ich == NCH - 3:
                GU13_PE = sPE.n
            if ich == NCH - 2:
                GU14_PE = sPE.n
        GU_PE_END = sPE.n
        FUSE_ALL = sfw.n
        FUSE_DVE_END = sDV.n

        dn_banks = [pb[0], pb[1], pb[2], pb[3]]
        wdb_free = [[], []]
        ftt_free = [[(sPE, GU_PE_END)], [(sPE, GU_PE_END)]]
        x1rr_free = [[(sDV, PH4_DVE_END)], [(sDV, PH4_DVE_END)]]
        ost_free = [[(sDV, PH4_DVE_END)], [(sDV, PH4_DVE_END)]]
        wd_ld = [0] * 4
        for cs in range(4):
            @block.gpsimd
            def _(g, cs=cs):
                pp = cs % 2
                wait_all(g, wdb_free[pp])
                if cs == 1:
                    g.wait_ge(sPE.h, GU_PE_END)
                    g.wait_ge(sfw.h, FUSE_ALL)
                    g.wait_ge(sDV.h, FUSE_DVE_END)
                sgq.chain(g)
                order = (1, 3, 0, 2) if cs == 0 else range(4)
                for part in order:
                    if cs == 0 and part == 1:
                        g.wait_ge(sPE.h, GU13_PE)
                    if cs == 0 and part == 0:
                        g.wait_ge(sPE.h, GU14_PE)
                    rsl = slice(part * 2048, (part + 1) * 2048)
                    g.dma_start(
                        out=wdb[pp][:].rearrange("p (it c) -> p it c", it=ITT)[:, part * 16:(part + 1) * 16, :],
                        in_=wd8[rsl, cs * 512:(cs + 1) * 512].rearrange("(it p) c -> p it c", p=128)
                    ).then_inc(sgq.h, 16)
            wd_ld[cs] = sgq.tick(64)
            for tt in range(8):
                k_i = cs * 8 + tt
                pp = k_i % 2
                fsem = sftp[pp]
                ftt_base = fsem.n
                @block.sync
                def _(s, cs=cs, tt=tt, pp=pp, k_i=k_i, fsem=fsem, ftt_base=ftt_base):
                    wait_all(s, ftt_free[pp])
                    if k_i == 0:
                        s.wait_ge(sPE.h, GU_PE_END)
                    for part in range(4):
                        if k_i == 0:
                            s.wait_ge(sfw.h, fuse_wr[4 * part + 3])
                            if part:
                                s.wait_ge(fsem.h, ftt_base + 16 * part)
                        rsl = slice(part * 2048, (part + 1) * 2048)
                        s.dma_start(
                            out=fuse_tt[pp][:].rearrange("p (it t) -> p it t", it=ITT)[:, part * 16:(part + 1) * 16, :],
                            in_=fuse_hbm[rsl, tt * 128:(tt + 1) * 128].rearrange("(it p) t -> p it t", p=128)
                        ).then_inc(fsem.h, 16)
                ftt_ld = fsem.tick(64)
                @block.sync
                def _(s, cs=cs, tt=tt, pp=pp, k_i=k_i):
                    wait_all(s, x1rr_free[pp])
                    if k_i == 0:
                        s.wait_ge(sx1w.h, X1_SPILL_END)
                        s.wait_ge(sDV.h, PH4_DVE_END)
                        s.wait_ge(sPE.h, GU_PE_END)
                    sx.chain(s)
                    s.dma_start(out=x1rr[pp][:],
                                in_=x1_hbm[tt * 128:(tt + 1) * 128, cs * 512:(cs + 1) * 512]
                                ).then_inc(sx.h, 16)
                x1_ld = sx.tick(16)
                bk = dn_banks[tt % 4]
                @block.tensor
                def _(t, cs=cs, pp=pp, bk=bk, k_i=k_i, ftt_base=ftt_base, fsem=fsem):
                    t.wait_ge(sgq.h, wd_ld[cs])
                    if k_i != 0:
                        t.wait_ge(fsem.h, ftt_base + 64)
                    bk.acquire(t)
                    for it in range(ITT):
                        if k_i == 0 and it % 16 == 0:
                            t.wait_ge(fsem.h, ftt_base + 16 * (it // 16 + 1))
                        mm = t.matmul(bk.t[:], fuse_tt[pp][:, it * 128:(it + 1) * 128],
                                      wdb[cs % 2][:, it * 512:(it + 1) * 512],
                                      start=(it == 0), stop=(it == ITT - 1))
                        if it == ITT - 1:
                            mm.then_inc(sPE.h, 1)
                pev = sPE.tick(1)
                ftt_free[pp] = [(sPE, pev)]
                ob = o_st[pp]
                @block.vector
                def _(v, cs=cs, pp=pp, bk=bk, pev=pev, ob=ob, x1_ld=x1_ld, k_i=k_i):
                    wait_all(v, ost_free[pp])
                    if k_i == 0:
                        v.wait_ge(sx1w.h, X1_SPILL_END)
                    v.wait_ge(sx.h, x1_ld)
                    v.wait_ge(sPE.h, pev)
                    v.tensor_tensor(out=ob[:], in0=bk.t[:],
                                    in1=sdown_v[:, cs * 512:(cs + 1) * 512],
                                    op=ALU.mult).then_inc(sDV.h, 1)
                    t1 = sDV.tick(1)
                    v.wait_ge(sDV.h, t1)
                    v.tensor_tensor(out=ob[:], in0=ob[:], in1=x1rr[pp][:], op=ALU.add).then_inc(sDV.h, 1)
                bk.release(sDV, sDV.tick(1))
                x1rr_free[pp] = [(sDV, sDV.n)]
                dv_o = sDV.n
                if tt == 7:
                    wdb_free[cs % 2] = [(sPE, pev), (sDV, dv_o)]
                @block.scalar
                def _(a_, cs=cs, tt=tt, dv_o=dv_o, ob=ob):
                    a_.wait_ge(sDV.h, dv_o)
                    sout.chain(a_)
                    a_.dma_start(out=out[tt * 128:(tt + 1) * 128, cs * 512:(cs + 1) * 512],
                                 in_=ob[:]).then_inc(sout.h, 16)
                ost_free[pp] = [(sout, sout.tick(16))]

        @block.scalar
        def _(a_):
            a_.wait_ge(sout.h, sout.n)

    return nc


def make_in_maps(L0_attn_x, L0_attn_mu_r, L0_attn_mu_k, L0_attn_mu_v, L0_attn_wkv_w,
                 L0_attn_wkv_u, L0_attn_Wr, L0_attn_Wk, L0_attn_Wv, L0_attn_Wo,
                 L0_ffn_w_gate, L0_ffn_w_up, L0_ffn_w_down, L0_ffn_s_gate, L0_ffn_s_up,
                 L0_ffn_s_down):
    x = np.ascontiguousarray(np.asarray(L0_attn_x, dtype=np.float32).reshape(NCORES * T, C))

    def pt(v, nt):
        return np.ascontiguousarray(np.asarray(v, dtype=np.float32).reshape(nt, 128).T)

    common = dict(
        Wr=np.ascontiguousarray(L0_attn_Wr, dtype=np.float32),
        Wk=np.ascontiguousarray(L0_attn_Wk, dtype=np.float32),
        Wv=np.ascontiguousarray(L0_attn_Wv, dtype=np.float32),
        Wo=np.ascontiguousarray(L0_attn_Wo, dtype=np.float32),
        wg8=np.ascontiguousarray(L0_ffn_w_gate, dtype=np.int8),
        wu8=np.ascontiguousarray(L0_ffn_w_up, dtype=np.int8),
        wd8=np.ascontiguousarray(L0_ffn_w_down, dtype=np.int8),
        mu_r_pt=pt(L0_attn_mu_r, CT),
        mu_k_pt=pt(L0_attn_mu_k, CT),
        mu_v_pt=pt(L0_attn_mu_v, CT),
        w_pt=pt(L0_attn_wkv_w, HT),
        u_pt=pt(L0_attn_wkv_u, HT),
        sgate_pt=pt(L0_ffn_s_gate, ITT),
        sup_pt=pt(L0_ffn_s_up, ITT),
        sdown_b=np.ascontiguousarray(
            np.broadcast_to(np.asarray(L0_ffn_s_down, dtype=np.float32), (128, C))),
        ident_h=np.eye(128, dtype=np.float32),
        iota_rev_h=np.ascontiguousarray(
            np.broadcast_to(np.arange(T - 1, -1, -1, dtype=np.float32), (128, T))),
        ones_h=np.ones((128, 512), dtype=np.float32),
    )

    in_maps = []
    for c in range(NCORES):
        if c == 0:
            seg = np.concatenate([np.zeros((1, C), np.float32), x[0:T]], axis=0)
        else:
            seg = x[c * T - 1:(c + 1) * T]
        segc = np.empty(NCORES, np.float32)
        for r in range(NCORES):
            segc[r] = 1024.0 * (c - 1 - r) if r < c else 4.0e7
        m = dict(common)
        m["x_seg"] = np.ascontiguousarray(seg)
        m["segc_h"] = np.ascontiguousarray(np.broadcast_to(segc, (128, NCORES)))
        in_maps.append(m)
    return in_maps


def kernel(**inputs):
    in_maps = make_in_maps(**inputs)
    nc = build_program()
    res = run_bass_kernel_spmd(nc, in_maps, core_ids=list(range(NCORES)))
    full = np.concatenate([np.asarray(res.results[c]["out"]) for c in range(NCORES)], axis=0)
    return full.reshape(1, NCORES * T, C).astype(np.float32)
